# version 01
# baseline (speedup 1.0000x reference)
import numpy as np
import ml_dtypes
from contextlib import ExitStack
import concourse.bass as bass
import concourse.mybir as mybir
from concourse.bass_core_rust import IndirectOffsetOnAxis
from concourse.bass_utils import run_bass_kernel_spmd

F32 = mybir.dt.float32
BF16 = mybir.dt.bfloat16
I32 = mybir.dt.int32
AF = mybir.ActivationFunctionType
ALU = mybir.AluOpType
AX = mybir.AxisListType

BAT, S, D, H, DH, FF = 4, 8192, 1024, 8, 128, 2048
NEXP = 8
L = 4
TOK = 4096
NC = 8
G = 512
NG = TOK // G
CH = 128
NCH = TOK // CH
QB = 1024
NQB = TOK // QB
EPS = 1e-5
GELU_C = float(np.sqrt(2.0 / np.pi))
GELU_A = 0.044715

ALL_PHASES = ("P0", "FFN1", "RET", "MERGE", "FFN2", "P4")
DEBUG_WPE = False
RET_PARTS = ("scan", "relay", "corr", "post", "ffn")


def _gammas64():
    return 1.0 - np.exp(np.linspace(np.log(1 / 32), np.log(1 / 512), H))


def host_tables(core_id):
    gam = _gammas64()
    t = {}
    half = core_id & 1
    pos = np.arange(half * TOK, (half + 1) * TOK, dtype=np.float64)
    inv_freq = 1.0 / 10000.0 ** (np.arange(0, DH, 2, dtype=np.float64) / DH)
    ang = pos[:, None] * inv_freq[None, :]
    cos, sin = np.cos(ang), np.sin(ang)
    crot = np.empty((DH, TOK), np.float64)
    srot = np.empty((DH, TOK), np.float64)
    crot[0::2, :] = cos.T
    crot[1::2, :] = cos.T
    srot[0::2, :] = -sin.T
    srot[1::2, :] = sin.T
    t["crot"] = crot.astype(np.float32)
    t["srot"] = srot.astype(np.float32)
    jj, ii = np.meshgrid(np.arange(CH), np.arange(CH), indexing="ij")
    dd = np.zeros((CH, H * CH), np.float64)
    gq = np.zeros((CH, H * CH), np.float64)
    for h in range(H):
        m = (ii - jj).astype(np.float64)
        dd[:, h * CH:(h + 1) * CH] = np.where(m >= 0, gam[h] ** m, 0.0)
        gq[:, h * CH:(h + 1) * CH] = gam[h] ** (ii[0][None, :] + 1.0)
    t["ddec"] = dd.astype(np.float32)
    t["gq"] = gq.astype(np.float32)
    ks = np.zeros((CH, H), np.float64)
    for h in range(H):
        ks[:, h] = gam[h] ** (CH - np.arange(CH, dtype=np.float64))
    t["kscale"] = ks.astype(np.float32)
    eye = np.eye(128, dtype=np.float32)
    t["id_f"] = eye
    t["id_b"] = eye.astype(ml_dtypes.bfloat16)
    pswap = np.zeros((128, 128), np.float32)
    for d in range(128):
        pswap[d ^ 1, d] = 1.0
    t["pswap_b"] = pswap.astype(ml_dtypes.bfloat16)
    ones_col = np.ones((128, 1), np.float32)
    t["ones_col_f"] = ones_col
    t["ones_col_b"] = ones_col.astype(ml_dtypes.bfloat16)
    t["ones_row_f"] = np.ones((1, 128), np.float32)
    ehs = np.zeros((128, H * H), np.float32)
    for h in range(H):
        ehs[:, h * H + h] = 1.0
    t["ehs_b"] = ehs.astype(ml_dtypes.bfloat16)
    t["ehs_f"] = ehs
    ebc = np.zeros((H, H * 128), np.float32)
    for h in range(H):
        ebc[h, h * 128:(h + 1) * 128] = 1.0
    t["ebc_f"] = ebc
    if core_id & 1:
        idx = (np.arange(128, dtype=np.int32) * 8 + (core_id - 1) * 1024).reshape(128, 1)
    else:
        idx = np.full((128, 1), 2 ** 30, dtype=np.int32)
    t["relay_idx"] = idx
    return t


def _feat_tile(vec):
    return np.ascontiguousarray(vec.reshape(NC, 128).T)


class Bld:
    def __init__(self, nc):
        self.nc = nc
        self.sems = {}
        self.cnt = {}

    def sem(self, name):
        if name not in self.sems:
            self.sems[name] = self.nc.alloc_semaphore(name)
            self.cnt[name] = 0
        return self.sems[name]

    def inc(self, instr, name, amt=1):
        instr.then_inc(self.sem(name), amt)
        self.cnt[name] += amt
        return self.cnt[name]

    def val(self, name):
        self.sem(name)
        return self.cnt[name]

    def wait(self, eng, name, v=None):
        if v is None:
            v = self.cnt[name]
        eng.wait_ge(self.sem(name), v)


class Ring:
    def __init__(self, b: Bld, name, tiles):
        self.b = b
        self.name = name
        self.tiles = tiles
        self.n = 0
        b.sem(name)
        self.base = b.cnt[name]

    def acquire(self, eng):
        k = len(self.tiles)
        need = self.n - (k - 1)
        if need > 0:
            self.b.wait(eng, self.name, self.base + need)
        tile = self.tiles[self.n % k]
        self.n += 1
        return tile

    def release(self, instr, n=1):
        self.b.inc(instr, self.name, n)


def _rearr_cols(dram_ap, c0, cols):
    sl = dram_ap[:, c0:c0 + cols]
    return sl.rearrange("(c p) t -> p c t", p=128)


def build_program(phases=ALL_PHASES):
    nc = bass.Bass("TRN2", target_bir_lowering=False, debug=False)
    b = Bld(nc)

    din = {}
    def dram_in(name, shape):
        din[name] = nc.dram_tensor(name, list(shape), F32, kind="ExternalInput").ap()
        return din[name]

    x_in = dram_in("x", (TOK, D))
    w_q = dram_in("wq", (L, D, D))
    w_k = dram_in("wk", (L, D, D))
    w_v = dram_in("wv", (L, D, D))
    w_g = dram_in("wg", (L, D, D))
    w_o = dram_in("wo", (L, D, D))
    w_1 = dram_in("w1", (L, D, FF))
    w_2 = dram_in("w2", (L, FF, D))
    ln_s = dram_in("ln_s", (128, NC))
    ln_b = dram_in("ln_b", (128, NC))
    gn_s = dram_in("gn_s", (128, L * NC))
    gn_b = dram_in("gn_b", (128, L * NC))
    gate_w = dram_in("gate_w", (D, NEXP))
    we_g = dram_in("we_g", (NEXP, D, FF))
    we_u = dram_in("we_u", (NEXP, D, FF))
    we_d = dram_in("we_d", (NEXP, FF, D))
    ws_g = dram_in("ws_g", (D, FF))
    ws_u = dram_in("ws_u", (D, FF))
    ws_d = dram_in("ws_d", (FF, D))
    crot = dram_in("crot", (DH, TOK))
    srot = dram_in("srot", (DH, TOK))
    ddec = dram_in("ddec", (CH, H * CH))
    gq_t = dram_in("gq", (CH, H * CH))
    kscale = dram_in("kscale", (CH, H))
    id_f = dram_in("id_f", (128, 128))
    ones_row_f = dram_in("ones_row_f", (1, 128))
    ehs_f = dram_in("ehs_f", (128, H * H))
    ebc_f = dram_in("ebc_f", (H, H * 128))
    bf16_tabs = {}
    for nm, shape in [("id_b", (128, 128)), ("pswap_b", (128, 128)),
                      ("ones_col_b", (128, 1)), ("ehs_b", (128, H * H))]:
        bf16_tabs[nm] = nc.dram_tensor(nm, list(shape), BF16, kind="ExternalInput").ap()
    ones_col_f = dram_in("ones_col_f", (128, 1))
    relay_idx = nc.dram_tensor("relay_idx", [128, 1], I32, kind="ExternalInput").ap()

    out_t = nc.dram_tensor("out", [TOK, D], F32, kind="ExternalOutput").ap()
    dbg_t = nc.dram_tensor("dbg", [8, TOK], F32, kind="ExternalOutput").ap() if DEBUG_WPE else None
    dbg2_t = nc.dram_tensor("dbg2", [128, 64], F32, kind="ExternalOutput").ap() if DEBUG_WPE else None

    r_res = nc.dram_tensor("r_res", [D, TOK], F32).ap()
    z_buf = nc.dram_tensor("z_buf", [D, TOK], F32).ap()
    y_dram = nc.dram_tensor("y_dram", [D, TOK], BF16).ap()
    relay_in = nc.dram_tensor("relay_in", [H * DH, DH], F32)
    relay_out = nc.dram_tensor("relay_out", [NC * H * DH, DH], F32)

    with ExitStack() as ctx:
        sb = lambda name, shape, dt: ctx.enter_context(nc.sbuf_tensor(name, shape, dt))
        psb = lambda name, shape, dt=F32: ctx.enter_context(nc.psum_tensor(name, shape, dt))

        te, ve, se, ge, sy = nc.tensor, nc.vector, nc.scalar, nc.gpsimd, nc.sync

        t_id_f = sb("t_id_f", [128, 128], F32)
        t_id_b = sb("t_id_b", [128, 128], BF16)
        t_pswap = sb("t_pswap", [128, 128], BF16)
        t_ones_row = sb("t_ones_row", [1, 128], F32)
        t_ones_col_f = sb("t_ones_col_f", [128, 1], F32)
        t_ones_col_b = sb("t_ones_col_b", [128, 1], BF16)
        t_ehs_b = sb("t_ehs_b", [128, H * H], BF16)
        t_ehs_f = sb("t_ehs_f", [128, H * H], F32)
        t_ebc_f = sb("t_ebc_f", [H, H * 128], F32)
        t_ddec = sb("t_ddec", [CH, H * CH], F32)
        t_gq = sb("t_gq", [CH, H * CH], F32)
        t_kscale = sb("t_kscale", [CH, H], F32)
        t_lns = sb("t_lns", [128, NC], F32)
        t_lnb = sb("t_lnb", [128, NC], F32)
        t_gns = sb("t_gns", [128, L * NC], F32)
        t_gnb = sb("t_gnb", [128, L * NC], F32)
        t_ridx = sb("t_ridx", [128, 1], I32)
        st_A = sb("st_A", [1, TOK], F32)
        st_B = sb("st_B", [1, TOK], F32)
        st_A8 = sb("st_A8", [H, TOK], F32)
        st_B8 = sb("st_B8", [H, TOK], F32)
        st_R = sb("st_R", [128, H * DH], F32)
        st_Rb = sb("st_Rb", [128, H * DH], BF16)
        st_Rrelay = sb("st_Rrelay", [128, H * DH], BF16)
        rlb_f = sb("rlb_f", [128, H * DH], F32)

        consts = sb("consts", [128, 2], F32)
        nc.const_aps.register(F32, 0.5, consts[:, 0:1])
        nc.const_aps.register(F32, 2.0, consts[:, 1:2])
        t_one2 = sb("t_one2", [128, 2], F32)
        t_neg2 = sb("t_neg2", [128, 2], F32)

        def load_tables():
            ld = [
                (t_id_f, id_f), (t_ones_row, ones_row_f), (t_ones_col_f, ones_col_f),
                (t_ehs_f, ehs_f), (t_ebc_f, ebc_f), (t_ddec, ddec), (t_gq, gq_t),
                (t_kscale, kscale), (t_lns, ln_s), (t_lnb, ln_b), (t_gns, gn_s),
                (t_gnb, gn_b),
                (t_id_b, bf16_tabs["id_b"]), (t_pswap, bf16_tabs["pswap_b"]),
                (t_ones_col_b, bf16_tabs["ones_col_b"]), (t_ehs_b, bf16_tabs["ehs_b"]),
                (t_ridx, relay_idx),
            ]
            for dst, src in ld:
                b.inc(sy.dma_start(out=dst[:], in_=src[:, :]), "ld", 16)
            ge.memset(consts[:, 0:1], 0.5)
            ge.memset(t_one2[:], 1.0)
            ge.memset(t_neg2[:], -1.0)
            b.inc(ge.memset(consts[:, 1:2], 2.0), "pool", 1)
            b.wait(sy, "ld")
            b.wait(ge, "pool")
            nc.all_engine_barrier()

        def transpose_stream(src_load, dst_store, ph, tag):
            xin = ph.enter_context(nc.sbuf_tensor(f"{tag}_xin", [128, 2 * D], F32))
            xout = ph.enter_context(nc.sbuf_tensor(f"{tag}_xout", [128, 2 * D], F32))
            tps = [ph.enter_context(nc.psum_tensor(f"{tag}_tp{i}", [128, 512], F32)) for i in range(4)]
            ring = Ring(b, f"{tag}_r", [tp[:, 0:128] for tp in tps])
            stv = []
            for tc in range(NCH):
                par = tc % 2
                xin_s = xin[:, par * D:(par + 1) * D]
                xout_s = xout[:, par * D:(par + 1) * D]
                if tc >= 2:
                    b.wait(sy, "st", stv[tc - 2])
                v = src_load(tc, xin_s)
                b.wait(te, "ld", v)
                for fc in range(NC):
                    ps = ring.acquire(te)
                    pv = b.inc(te.transpose(ps, xin_s[:, fc * 128:(fc + 1) * 128], t_id_f[:]), "pe")
                    b.wait(ve, "pe", pv)
                    if fc == 0 and tc >= 2:
                        b.wait(ve, "st", stv[tc - 2])
                    ring.release(ve.tensor_copy(xout_s[:, fc * 128:(fc + 1) * 128], ps))
                b.wait(sy, ring.name)
                dst_store(tc, xout_s)
                stv.append(b.cnt["st"])
            b.wait(sy, "st")

        def p0_ingest():
            with ExitStack() as ph:
                def src(tc, buf):
                    return b.inc(sy.dma_start(out=buf[:, :], in_=x_in[tc * 128:(tc + 1) * 128, :]), "ld", 16)
                def dst(tc, buf):
                    b.inc(sy.dma_start(
                        out=_rearr_cols(r_res, tc * 128, 128),
                        in_=buf.rearrange("p (c t) -> p c t", c=NC)), "st", 16)
                transpose_stream(src, dst, ph, "p0")
            nc.all_engine_barrier()

        def lnstats(src_dram, t0, ntok, tag):
            ngr = ntok // G
            with ExitStack() as s:
                rg = s.enter_context(nc.sbuf_tensor(f"{tag}_rg", [128, 2 * NC * G], F32))
                sq = s.enter_context(nc.sbuf_tensor(f"{tag}_sq", [128, 2 * G], F32))
                mu = s.enter_context(nc.sbuf_tensor(f"{tag}_mu", [1, ntok], F32))
                var = s.enter_context(nc.sbuf_tensor(f"{tag}_var", [1, ntok], F32))
                tmp = s.enter_context(nc.sbuf_tensor(f"{tag}_tmp", [1, ntok], F32))
                psS = s.enter_context(nc.psum_tensor(f"{tag}_psS", [128, 512], F32))
                psQ = s.enter_context(nc.psum_tensor(f"{tag}_psQ", [128, 512], F32))
                pev = [0] * ngr
                vev = [0] * ngr
                for g in range(ngr):
                    tg = t0 + g * G
                    par = g % 2
                    rgs = rg[:, par * NC * G:(par + 1) * NC * G]
                    if g >= 2:
                        b.wait(sy, "pe", pev[g - 2])
                    v = b.inc(sy.dma_start(out=rgs.rearrange("p (c t) -> p c t", c=NC),
                                           in_=_rearr_cols(src_dram, tg, G)), "ld", 16)
                    b.wait(ve, "ld", v)
                    b.wait(te, "ld", v)
                    if g >= 1:
                        b.wait(te, "vec", vev[g - 1])
                    pv = 0
                    for c in range(NC):
                        rc = rgs[:, c * G:(c + 1) * G]
                        sqs = sq[:, (c % 2) * G:(c % 2) * G + G]
                        if c >= 2 or g > 0:
                            b.wait(ve, "pe")
                        vv = b.inc(ve.tensor_mul(sqs, rc, rc), "vec")
                        b.inc(te.matmul(psS[0:1, :], t_ones_col_f[:], rc,
                                        start=(c == 0), stop=(c == NC - 1)), "pe")
                        b.wait(te, "vec", vv)
                        pv = b.inc(te.matmul(psQ[0:1, :], t_ones_col_f[:], sqs,
                                             start=(c == 0), stop=(c == NC - 1)), "pe")
                    pev[g] = pv
                    b.wait(ve, "pe", pv)
                    o = tg - t0
                    b.inc(ve.tensor_scalar_mul(mu[0:1, o:o + G], psS[0:1, :], 1.0 / D), "vec")
                    b.inc(ve.tensor_mul(tmp[0:1, o:o + G], mu[0:1, o:o + G], mu[0:1, o:o + G]), "vec")
                    b.inc(ve.tensor_scalar_mul(var[0:1, o:o + G], psQ[0:1, :], 1.0 / D), "vec")
                    vev[g] = b.inc(ve.tensor_sub(var[0:1, o:o + G], var[0:1, o:o + G], tmp[0:1, o:o + G]), "vec")
                b.inc(ve.tensor_scalar_add(var[0:1, 0:ntok], var[0:1, 0:ntok], EPS), "vec")
                vv = b.inc(ve.reciprocal(tmp[0:1, 0:ntok], var[0:1, 0:ntok]), "vec")
                b.wait(se, "vec", vv)
                av = b.inc(se.activation(st_A[0:1, t0:t0 + ntok], tmp[0:1, 0:ntok], AF.Sqrt), "act")
                b.wait(ve, "act", av)
                bv = b.inc(ve.scalar_tensor_tensor(st_B[0:1, t0:t0 + ntok], mu[0:1, 0:ntok], -1.0,
                                                   st_A[0:1, t0:t0 + ntok], ALU.mult, ALU.mult), "vec")
                b.wait(se, "vec", bv)
            return bv

        def moe_phase(tag):
            with ExitStack() as ph:
                xn = ph.enter_context(nc.sbuf_tensor(f"{tag}_xn", [128, NC * QB], BF16))
                yac = ph.enter_context(nc.sbuf_tensor(f"{tag}_yac", [128, NC * QB], F32))
                wpeT = ph.enter_context(nc.sbuf_tensor(f"{tag}_wpeT", [8, QB], F32))
                gwt = ph.enter_context(nc.sbuf_tensor(f"{tag}_gwt", [128, NC * 8], F32))
                wrow = ph.enter_context(nc.sbuf_tensor(f"{tag}_wrow", [128, QB], BF16))
                v = b.inc(sy.dma_start(out=gwt.rearrange("p (c e) -> p c e", c=NC),
                                       in_=gate_w.rearrange("(c p) e -> p c e", p=128)), "ld", 16)
                b.wait(te, "ld", v)
                for q in range(NQB):
                    t0 = q * QB
                    lnstats(r_res, t0, QB, f"{tag}q{q}s")
                    nc.all_engine_barrier()
                    _moe_a2(tag, q, xn, wpeT, gwt)
                    nc.all_engine_barrier()
                    if DEBUG_WPE and tag == "f1":
                        b.inc(sy.dma_start(out=dbg_t[:, t0:t0 + QB], in_=wpeT[:, :]), "st", 16)
                        b.wait(sy, "st")
                    _moe_experts(tag, q, xn, yac, wpeT, wrow)
                    nc.all_engine_barrier()
                    _moe_cadd(tag, q, yac)
                    nc.all_engine_barrier()

        def _moe_a2(tag, q, xn, wpeT, gwt):
            t0 = q * QB
            with ExitStack() as s:
                rg = s.enter_context(nc.sbuf_tensor(f"{tag}a2rg_{q}", [128, NC * G], F32))
                tmpa = s.enter_context(nc.sbuf_tensor(f"{tag}a2ta_{q}", [128, 2 * G], F32))
                tmpf = s.enter_context(nc.sbuf_tensor(f"{tag}a2tf_{q}", [128, 2 * G], F32))
                scsb = s.enter_context(nc.sbuf_tensor(f"{tag}a2sc_{q}", [8, G], F32))
                tk = s.enter_context(nc.sbuf_tensor(f"{tag}a2tk_{q}", [128, 6 * 8], F32))
                sm = s.enter_context(nc.sbuf_tensor(f"{tag}a2sm_{q}", [128, 48], F32))
                psA = s.enter_context(nc.psum_tensor(f"{tag}a2pA_{q}", [128, 512], F32))
                psB = s.enter_context(nc.psum_tensor(f"{tag}a2pB_{q}", [128, 512], F32))
                psS = s.enter_context(nc.psum_tensor(f"{tag}a2pS_{q}", [128, 512], F32))
                psT = s.enter_context(nc.psum_tensor(f"{tag}a2pT_{q}", [128, 512], F32))
                psW = s.enter_context(nc.psum_tensor(f"{tag}a2pW_{q}", [128, 512], F32))
                for g2 in range(2):
                    tg = t0 + g2 * G
                    if g2 > 0:
                        b.wait(sy, "pe")
                        b.wait(sy, "vec")
                    v = b.inc(sy.dma_start(out=rg.rearrange("p (c t) -> p c t", c=NC),
                                           in_=_rearr_cols(r_res, tg, G)), "ld", 16)
                    if g2 > 0:
                        b.wait(te, "vec")
                    bca = b.inc(te.matmul(psA[:, 0:G], t_ones_row[0:1, :], st_A[0:1, tg:tg + G],
                                          start=True, stop=True), "pe")
                    bcb = b.inc(te.matmul(psB[:, 0:G], t_ones_row[0:1, :], st_B[0:1, tg:tg + G],
                                          start=True, stop=True), "pe")
                    b.wait(ve, "ld", v)
                    b.wait(ve, "pe", bcb)
                    pv_gate = 0
                    for c in range(NC):
                        sl = c % 2
                        ta = tmpa[:, sl * G:(sl + 1) * G]
                        tf = tmpf[:, sl * G:(sl + 1) * G]
                        if c >= 2:
                            b.wait(ve, "pe")
                        b.inc(ve.tensor_mul(ta, rg[:, c * G:(c + 1) * G], psA[:, 0:G]), "vec")
                        fv = b.inc(ve.tensor_add(tf, ta, psB[:, 0:G]), "vec")
                        b.inc(ve.tensor_copy(xn[:, c * QB + g2 * G: c * QB + g2 * G + G], tf), "vec")
                        b.wait(te, "vec", fv)
                        pv_gate = b.inc(te.matmul(psS[0:8, 0:G], gwt[:, c * 8:(c + 1) * 8], tf,
                                                  start=(c == 0), stop=(c == NC - 1)), "pe")
                    b.wait(ve, "pe", pv_gate)
                    sv = b.inc(ve.tensor_copy(scsb[:, :], psS[0:8, 0:G]), "vec")
                    if DEBUG_WPE and tag == "f1" and q == 0 and g2 == 0:
                        b.wait(sy, "vec", sv)
                        b.inc(sy.dma_start(out=dbg_t[:, TOK - G:TOK], in_=scsb[:, :]), "st", 16)
                    for tc in range(4):
                        b.wait(te, "vec")
                        ptv = b.inc(te.transpose(psT[:, 0:8], scsb[:, tc * 128:(tc + 1) * 128],
                                                 t_id_f[0:8, 0:8]), "pe")
                        b.wait(ve, "pe", ptv)
                        def nvw(instr):
                            vv = b.inc(instr, "vec")
                            b.wait(ve, "vec", vv)
                            return vv
                        toksc, esb = tk[:, 0:8], tk[:, 8:16]
                        scn, sc2 = tk[:, 16:24], tk[:, 24:32]
                        mk1, mk2 = tk[:, 32:40], tk[:, 40:48]
                        t4 = sm[:, 0:4]
                        t2 = sm[:, 4:6]
                        mx2 = sm[:, 8:10]
                        nm2 = sm[:, 12:14]
                        zr2 = sm[:, 16:18]
                        m1_2 = sm[:, 20:22]
                        m2_2 = sm[:, 24:26]
                        den2 = sm[:, 28:30]
                        wr2 = sm[:, 32:34]
                        w1_2 = sm[:, 36:38]
                        w2_2 = sm[:, 40:42]

                        def rowmax(src8, out2):
                            nvw(ve.tensor_max(t4, src8[:, 0:4], src8[:, 4:8]))
                            nvw(ve.tensor_max(t2, t4[:, 0:2], t4[:, 2:4]))
                            nvw(ve.tensor_single_scalar(out2, t2, t2[:, 1:2], ALU.max))

                        def rowsum(src8, out2):
                            nvw(ve.tensor_add(t4, src8[:, 0:4], src8[:, 4:8]))
                            nvw(ve.tensor_add(t2, t4[:, 0:2], t4[:, 2:4]))
                            nvw(ve.tensor_single_scalar(out2, t2, t2[:, 1:2], ALU.add))

                        nvw(ve.tensor_copy(toksc, psT[:, 0:8]))
                        rowmax(toksc, mx2)
                        nv = nvw(ve.tensor_single_scalar(nm2, t_neg2[:], mx2[:, 0:1], ALU.mult))
                        b.wait(se, "vec", nv)
                        av = b.inc(se.activation(esb, toksc, AF.Exp, bias=nm2[:, 0:1]), "act")
                        b.wait(ve, "act", av)
                        rowsum(esb, zr2)
                        nvw(ve.reciprocal(zr2, zr2))
                        nvw(ve.tensor_scalar_mul(scn, esb, zr2[:, 0:1]))
                        rowmax(scn, m1_2)
                        nvw(ve.tensor_single_scalar(mk1, scn, m1_2[:, 0:1], ALU.is_ge))
                        nvw(ve.tensor_scalar(sc2, mk1, -1.0, 1.0, ALU.mult, ALU.add))
                        nvw(ve.tensor_mul(sc2, scn, sc2))
                        rowmax(sc2, m2_2)
                        nvw(ve.tensor_single_scalar(mk2, sc2, m2_2[:, 0:1], ALU.is_ge))
                        nvw(ve.tensor_single_scalar(den2, t_one2[:], m1_2[:, 0:1], ALU.mult))
                        nvw(ve.tensor_single_scalar(den2, den2, m2_2[:, 0:1], ALU.add))
                        nvw(ve.tensor_scalar_add(den2, den2, 1e-20))
                        nvw(ve.reciprocal(wr2, den2))
                        nvw(ve.tensor_scalar_mul(wr2, wr2, 0.5))
                        nvw(ve.tensor_single_scalar(w1_2, wr2, m1_2[:, 0:1], ALU.mult))
                        nvw(ve.tensor_single_scalar(w2_2, wr2, m2_2[:, 0:1], ALU.mult))
                        nvw(ve.tensor_scalar_mul(toksc, mk1, w1_2[:, 0:1]))
                        wv = nvw(ve.scalar_tensor_tensor(toksc, mk2, w2_2[:, 0:1], toksc,
                                                         ALU.mult, ALU.add))
                        if DEBUG_WPE and tag == "f1" and q == 0 and g2 == 0 and tc == 0:
                            b.wait(sy, "vec", wv)
                            b.inc(sy.dma_start(out=dbg2_t[:, 0:48], in_=tk[:, 0:48]), "st", 16)
                            dv = b.inc(sy.dma_start(out=dbg2_t[:, 48:58], in_=sm[:, 0:10]), "st", 16)
                            b.wait(ve, "st", dv)
                        b.wait(te, "vec", wv)
                        if tc > 0 or g2 > 0:
                            b.wait(te, "vec")
                        pwv = b.inc(te.transpose(psW[0:8, 0:128], toksc, t_id_f[:, :]), "pe")
                        b.wait(ve, "pe", pwv)
                        b.inc(ve.tensor_copy(wpeT[:, g2 * G + tc * 128: g2 * G + (tc + 1) * 128],
                                             psW[0:8, 0:128]), "vec")
                b.wait(se, "vec")
                b.wait(te, "vec")

        def _w_col_block(base_ap, e, shared_ap, c0, cols):
            if e < NEXP:
                sl = base_ap[e, :, c0:c0 + cols]
            else:
                sl = shared_ap[:, c0:c0 + cols]
            return sl.rearrange("(c p) d -> p c d", p=128)

        def _moe_experts(tag, q, xn, yac, wpeT, wrow):
            with ExitStack() as s:
                wgb = s.enter_context(nc.sbuf_tensor(f"{tag}wg_{q}", [128, 2 * NC * 128], BF16))
                wub = s.enter_context(nc.sbuf_tensor(f"{tag}wu_{q}", [128, 2 * NC * 128], BF16))
                wdb = s.enter_context(nc.sbuf_tensor(f"{tag}wd_{q}", [128, 2 * 16 * 128], BF16))
                hid = s.enter_context(nc.sbuf_tensor(f"{tag}hid_{q}", [128, 16 * QB], BF16))
                sgb = s.enter_context(nc.sbuf_tensor(f"{tag}sg_{q}", [128, 2 * 512], F32))
                h1b = s.enter_context(nc.sbuf_tensor(f"{tag}h1_{q}", [128, 2 * 512], F32))
                psgu = [s.enter_context(nc.psum_tensor(f"{tag}pgu{i}_{q}", [128, 512], F32)) for i in range(4)]
                psd = [s.enter_context(nc.psum_tensor(f"{tag}pd{i}_{q}", [128, 512], F32)) for i in range(2)]
                pswr = s.enter_context(nc.psum_tensor(f"{tag}pwr_{q}", [128, 512], F32))
                ringGU = Ring(b, f"{tag}rgu_{q}", [p[:, 0:512] for p in psgu])
                ringD = Ring(b, f"{tag}rd_{q}", [p[:, 0:512] for p in psd])
                wrl = 0
                for e in range(NEXP + 1):
                    if e == NEXP:
                        wrl = b.inc(ve.memset(wrow[:, :], 0.5), "vec")
                    else:
                        for g2 in range(2):
                            if e > 0 or g2 > 0:
                                b.wait(te, "vec", wrl)
                            pv = b.inc(te.matmul(pswr[:, 0:G], t_ebc_f[:, e * 128:(e + 1) * 128],
                                                 wpeT[0:8, g2 * G:(g2 + 1) * G], start=True, stop=True), "pe")
                            b.wait(ve, "pe", pv)
                            wrl = b.inc(ve.tensor_copy(wrow[:, g2 * G:(g2 + 1) * G], pswr[:, 0:G]), "vec")
                    wuse = {}
                    hid_last = 0
                    for dc in range(16):
                        sl = dc % 2
                        wg_s = wgb[:, sl * NC * 128:(sl + 1) * NC * 128]
                        wu_s = wub[:, sl * NC * 128:(sl + 1) * NC * 128]
                        if dc >= 2:
                            b.wait(ge, "pe", wuse[dc - 2])
                        elif e > 0:
                            b.wait(ge, "pe")
                        b.inc(ge.dma_start(out=wg_s.rearrange("p (c d) -> p c d", d=128),
                                           in_=_w_col_block(we_g, e, ws_g, dc * 128, 128)), "wld", 16)
                        lv = b.inc(ge.dma_start(out=wu_s.rearrange("p (c d) -> p c d", d=128),
                                                in_=_w_col_block(we_u, e, ws_u, dc * 128, 128)), "wld", 16)
                        for ts in range(2):
                            pg = ringGU.acquire(te)
                            pu = ringGU.acquire(te)
                            b.wait(te, "wld", lv)
                            pgv = puv = 0
                            for c in range(NC):
                                pgv = b.inc(te.matmul(pg, wg_s[:, c * 128:(c + 1) * 128],
                                                      xn[:, c * QB + ts * 512: c * QB + ts * 512 + 512],
                                                      start=(c == 0), stop=(c == NC - 1)), "pe")
                            for c in range(NC):
                                puv = b.inc(te.matmul(pu, wu_s[:, c * 128:(c + 1) * 128],
                                                      xn[:, c * QB + ts * 512: c * QB + ts * 512 + 512],
                                                      start=(c == 0), stop=(c == NC - 1)), "pe")
                            wuse[dc] = puv
                            sgs = sgb[:, ts * 512:(ts + 1) * 512]
                            h1s = h1b[:, ts * 512:(ts + 1) * 512]
                            b.wait(se, "pe", pgv)
                            b.wait(se, "vec", hid_last)
                            av = b.inc(se.activation(sgs, pg, AF.Silu), "act")
                            b.wait(ve, "act", av)
                            b.wait(ve, "pe", puv)
                            h1i = ve.tensor_mul(h1s, sgs, pu)
                            ringGU.release(h1i, 2)
                            hidi = ve.tensor_mul(hid[:, dc * QB + ts * 512: dc * QB + ts * 512 + 512],
                                                 h1s, wrow[:, ts * 512:(ts + 1) * 512])
                            hv = b.inc(hidi, "vec")
                            hid_last = hv
                    b.wait(te, "vec", hid_last)
                    wduse = {}
                    for o2 in range(NC):
                        sl = o2 % 2
                        wd_s = wdb[:, sl * 16 * 128:(sl + 1) * 16 * 128]
                        if o2 >= 2:
                            b.wait(ge, "pe", wduse[o2 - 2])
                        elif e > 0:
                            b.wait(ge, "pe")
                        lv = b.inc(ge.dma_start(out=wd_s.rearrange("p (c d) -> p c d", d=128),
                                                in_=_w_col_block(we_d, e, ws_d, o2 * 128, 128)), "wld", 16)
                        for ts in range(2):
                            pd = ringD.acquire(te)
                            b.wait(te, "wld", lv)
                            pdv = 0
                            for dc in range(16):
                                pdv = b.inc(te.matmul(pd, wd_s[:, dc * 128:(dc + 1) * 128],
                                                      hid[:, dc * QB + ts * 512: dc * QB + ts * 512 + 512],
                                                      start=(dc == 0), stop=(dc == 15)), "pe")
                            wduse[o2] = pdv
                            b.wait(ve, "pe", pdv)
                            ysl = yac[:, o2 * QB + ts * 512: o2 * QB + ts * 512 + 512]
                            if e == 0:
                                yi = ve.tensor_copy(ysl, pd)
                            else:
                                yi = ve.tensor_add(ysl, ysl, pd)
                            ringD.release(yi)

        def _moe_cadd(tag, q, yac):
            t0 = q * QB
            with ExitStack() as s:
                rg = s.enter_context(nc.sbuf_tensor(f"{tag}crg_{q}", [128, NC * G], F32))
                for g2 in range(2):
                    tg = t0 + g2 * G
                    if g2 > 0:
                        b.wait(sy, "st")
                        b.wait(sy, "vec")
                    v = b.inc(sy.dma_start(out=rg.rearrange("p (c t) -> p c t", c=NC),
                                           in_=_rearr_cols(r_res, tg, G)), "ld", 16)
                    b.wait(ve, "ld", v)
                    vv = 0
                    for c in range(NC):
                        vv = b.inc(ve.tensor_add(rg[:, c * G:(c + 1) * G], rg[:, c * G:(c + 1) * G],
                                                 yac[:, c * QB + g2 * G: c * QB + g2 * G + G]), "vec")
                    b.wait(sy, "vec", vv)
                    b.inc(sy.dma_start(out=_rearr_cols(r_res, tg, G),
                                       in_=rg.rearrange("p (c t) -> p c t", c=NC)), "st", 16)
                b.wait(sy, "st")

        def znorm_init():
            lnstats(r_res, 0, TOK, "zi")
            nc.all_engine_barrier()
            with ExitStack() as s:
                zg = s.enter_context(nc.sbuf_tensor("zi_zg", [128, NC * G], F32))
                zo = s.enter_context(nc.sbuf_tensor("zi_zo", [128, NC * G], F32))
                psA = s.enter_context(nc.psum_tensor("zi_pA", [128, 512], F32))
                psB = s.enter_context(nc.psum_tensor("zi_pB", [128, 512], F32))
                stv = 0
                for g in range(NG):
                    tg = g * G
                    if g > 0:
                        b.wait(sy, "st", stv)
                        b.wait(sy, "vec")
                    v = b.inc(sy.dma_start(out=zg.rearrange("p (c t) -> p c t", c=NC),
                                           in_=_rearr_cols(r_res, tg, G)), "ld", 16)
                    if g > 0:
                        b.wait(te, "vec")
                    ba = b.inc(te.matmul(psA[:, 0:G], t_ones_row[0:1, :], st_A[0:1, tg:tg + G],
                                         start=True, stop=True), "pe")
                    bb2 = b.inc(te.matmul(psB[:, 0:G], t_ones_row[0:1, :], st_B[0:1, tg:tg + G],
                                          start=True, stop=True), "pe")
                    b.wait(ve, "ld", v)
                    b.wait(ve, "pe", bb2)
                    vv = 0
                    for c in range(NC):
                        b.inc(ve.tensor_mul(zo[:, c * G:(c + 1) * G], zg[:, c * G:(c + 1) * G], psA[:, 0:G]), "vec")
                        vv = b.inc(ve.tensor_add(zo[:, c * G:(c + 1) * G], zo[:, c * G:(c + 1) * G], psB[:, 0:G]), "vec")
                    b.wait(sy, "vec", vv)
                    stv = b.inc(sy.dma_start(out=_rearr_cols(z_buf, tg, G),
                                             in_=zo.rearrange("p (c t) -> p c t", c=NC)), "st", 16)
                b.wait(sy, "st")
            nc.all_engine_barrier()

        def _xn_build(zg, xn_g, psA, psB, tg, scr, scaled=True):
            b.inc(te.matmul(psA[:, 0:G], t_ones_row[0:1, :], st_A[0:1, tg:tg + G],
                            start=True, stop=True), "pe")
            bb2 = b.inc(te.matmul(psB[:, 0:G], t_ones_row[0:1, :], st_B[0:1, tg:tg + G],
                                  start=True, stop=True), "pe")
            b.wait(ve, "pe", bb2)
            vv = 0
            for c in range(NC):
                t1 = xn_g[:, c * G:(c + 1) * G]
                b.inc(ve.tensor_mul(scr[:, 0:G], zg[:, c * G:(c + 1) * G], psA[:, 0:G]), "vec")
                b.inc(ve.tensor_add(scr[:, 0:G], scr[:, 0:G], psB[:, 0:G]), "vec")
                if scaled:
                    vv = b.inc(ve.tensor_scalar(t1, scr[:, 0:G], t_lns[:, c:c + 1],
                                                t_lnb[:, c:c + 1], ALU.mult, ALU.add), "vec")
                else:
                    vv = b.inc(ve.tensor_copy(t1, scr[:, 0:G]), "vec")
            return vv

        def ret_layer(li):
            lnstats(z_buf, 0, TOK, f"r{li}s")
            nc.all_engine_barrier()
            b.inc(ve.memset(st_R[:], 0.0), "vec")
            b.inc(ve.memset(st_Rb[:], 0.0), "vec")
            gamC = [float(_gammas64()[h] ** CH) for h in range(H)]

            with ExitStack() as s:
                wq_sb = s.enter_context(nc.sbuf_tensor(f"r{li}_wq", [128, NC * D], BF16))
                wk_sb = s.enter_context(nc.sbuf_tensor(f"r{li}_wk", [128, NC * D], BF16))
                wv_sb = s.enter_context(nc.sbuf_tensor(f"r{li}_wv", [128, NC * D], BF16))
                zg = s.enter_context(nc.sbuf_tensor(f"r{li}_zg", [128, NC * G], F32))
                xn_g = s.enter_context(nc.sbuf_tensor(f"r{li}_xn", [128, NC * G], BF16))
                rotq = s.enter_context(nc.sbuf_tensor(f"r{li}_rq", [128, H * G], BF16))
                rotk = s.enter_context(nc.sbuf_tensor(f"r{li}_rk", [128, H * G], BF16))
                v_tok = s.enter_context(nc.sbuf_tensor(f"r{li}_vt", [128, 2 * D], BF16))
                y_grp = s.enter_context(nc.sbuf_tensor(f"r{li}_yg", [128, H * G], BF16))
                crg = s.enter_context(nc.sbuf_tensor(f"r{li}_cr", [128, G], F32))
                srg = s.enter_context(nc.sbuf_tensor(f"r{li}_sr", [128, G], F32))
                evt = s.enter_context(nc.sbuf_tensor(f"r{li}_ev", [128, 2 * G], F32))
                qgam = s.enter_context(nc.sbuf_tensor(f"r{li}_qg", [128, CH], BF16))
                attd = s.enter_context(nc.sbuf_tensor(f"r{li}_ad", [128, CH], BF16))
                kts = s.enter_context(nc.sbuf_tensor(f"r{li}_kt", [128, CH], BF16))
                psV = s.enter_context(nc.psum_tensor(f"r{li}_pV", [128, 512], F32))
                psS1 = s.enter_context(nc.psum_tensor(f"r{li}_pS1", [128, 512], F32))
                psS2 = s.enter_context(nc.psum_tensor(f"r{li}_pS2", [128, 512], F32))
                psS3 = s.enter_context(nc.psum_tensor(f"r{li}_pS3", [128, 512], F32))
                psS4 = s.enter_context(nc.psum_tensor(f"r{li}_pS4", [128, 512], F32))
                psKT_b = s.enter_context(nc.psum_tensor(f"r{li}_pKT", [128, 1024], BF16))
                psA, psB = psS3, psS4
                psP = psV
                for w_dram, wsb in [(w_q, wq_sb), (w_k, wk_sb), (w_v, wv_sb)]:
                    b.inc(ge.dma_start(out=wsb.rearrange("p (c d) -> p c d", d=D),
                                       in_=w_dram[li].rearrange("(c p) d -> p c d", p=128)), "wld", 16)
                wldv = b.val("wld")
                b.wait(te, "wld", wldv)
                stv_y = 0
                for g in (range(NG) if "scan" in RET_PARTS else []):
                    tg = g * G
                    if g > 0:
                        b.wait(sy, "pe")
                        b.wait(sy, "vec")
                    v = b.inc(sy.dma_start(out=zg.rearrange("p (c t) -> p c t", c=NC),
                                           in_=_rearr_cols(z_buf, tg, G)), "ld", 16)
                    b.inc(sy.dma_start(out=crg[:], in_=crot[:, tg:tg + G]), "ld", 16)
                    cv = b.inc(sy.dma_start(out=srg[:], in_=srot[:, tg:tg + G]), "ld", 16)
                    b.wait(ve, "ld", cv)
                    _xn_build(zg, xn_g, psA, psB, tg, evt)
                    xnv = b.val("vec")
                    b.wait(te, "vec", xnv)
                    for (wsb, rot, sign) in [(wq_sb, rotq, 1.0), (wk_sb, rotk, -1.0)]:
                        for h in range(H):
                            pv = 0
                            for c in range(NC):
                                pv = b.inc(te.matmul(psS1[:, 0:G], wsb[:, c * D + h * 128: c * D + (h + 1) * 128],
                                                     xn_g[:, c * G:(c + 1) * G],
                                                     start=(c == 0), stop=(c == NC - 1)), "pe")
                            b.wait(se, "pe", pv)
                            b.wait(se, "vec")
                            ev = b.inc(se.copy(evt[:, 0:G], psS1[:, 0:G]), "act")
                            b.wait(ve, "act", ev)
                            qb_v = b.inc(ve.tensor_copy(rot[:, h * G:(h + 1) * G], evt[:, 0:G]), "vec")
                            b.wait(te, "vec", qb_v)
                            ppv = b.inc(te.matmul(psP[:, 0:G], t_pswap[:], rot[:, h * G:(h + 1) * G],
                                                  start=True, stop=True), "pe")
                            b.wait(ve, "pe", ppv)
                            b.inc(ve.tensor_mul(evt[:, G:2 * G], evt[:, 0:G], crg[:]), "vec")
                            if sign > 0:
                                b.inc(ve.tensor_mul(evt[:, 0:G], psP[:, 0:G], srg[:]), "vec")
                                rv = b.inc(ve.tensor_add(rot[:, h * G:(h + 1) * G], evt[:, G:2 * G], evt[:, 0:G]), "vec")
                            else:
                                b.inc(ve.tensor_mul(evt[:, 0:G], psP[:, 0:G], srg[:]), "vec")
                                rv = b.inc(ve.tensor_sub(rot[:, h * G:(h + 1) * G], evt[:, G:2 * G], evt[:, 0:G]), "vec")
                    b.wait(te, "vec", rv)
                    for tc in range(4):
                        tsl = tc % 2
                        vts = v_tok[:, tsl * D:(tsl + 1) * D]
                        pvv = 0
                        for dv in range(2):
                            if dv == 1:
                                b.wait(te, "act", vev)
                            for c in range(NC):
                                pvv = b.inc(te.matmul(psV[:, 0:G], xn_g[:, c * G + tc * 128: c * G + tc * 128 + 128],
                                                      wv_sb[:, c * D + dv * 512: c * D + (dv + 1) * 512],
                                                      start=(c == 0), stop=(c == NC - 1)), "pe")
                            b.wait(se, "pe", pvv)
                            vev = b.inc(se.copy(vts[:, dv * 512:(dv + 1) * 512], psV[:, 0:G]), "act")
                        b.wait(te, "act", vev)
                        for h in range(H):
                            rq_c = rotq[:, h * G + tc * 128: h * G + (tc + 1) * 128]
                            rk_c = rotk[:, h * G + tc * 128: h * G + (tc + 1) * 128]
                            vt_h = vts[:, h * 128:(h + 1) * 128]
                            at_v = b.inc(te.matmul(psS2[:, 0:128], rk_c, rq_c, start=True, stop=True), "pe")
                            b.wait(ve, "pe", at_v)
                            adv = b.inc(ve.tensor_mul(attd[:], psS2[:, 0:128], t_ddec[:, h * CH:(h + 1) * CH]), "vec")
                            qgv = b.inc(ve.tensor_mul(qgam[:], rq_c, t_gq[:, h * CH:(h + 1) * CH]), "vec")
                            b.wait(te, "vec", qgv)
                            b.inc(te.matmul(psS3[:, 0:128], vt_h, attd[:], start=True, stop=False), "pe")
                            yv = b.inc(te.matmul(psS3[:, 0:128], st_Rb[:, h * 128:(h + 1) * 128], qgam[:],
                                                 start=False, stop=True), "pe")
                            ktv = b.inc(te.transpose(psKT_b[:, 0:128], rk_c, t_id_b[:]), "pe")
                            b.wait(ve, "pe", ktv)
                            ksv = b.inc(ve.tensor_scalar_mul(kts[:], psKT_b[:, 0:128], t_kscale[:, h:h + 1]), "vec")
                            b.wait(se, "pe", yv)
                            yev = b.inc(se.copy(y_grp[:, h * G + tc * 128: h * G + (tc + 1) * 128],
                                                psS3[:, 0:128]), "act")
                            b.wait(te, "vec", ksv)
                            drv = b.inc(te.matmul(psS4[:, 0:128], kts[:], vt_h, start=True, stop=True), "pe")
                            b.wait(ve, "pe", drv)
                            ruv = b.inc(ve.scalar_tensor_tensor(st_R[:, h * 128:(h + 1) * 128],
                                                                st_R[:, h * 128:(h + 1) * 128], gamC[h],
                                                                psS4[:, 0:128], ALU.mult, ALU.add), "vec")
                            rbv = b.inc(ve.tensor_copy(st_Rb[:, h * 128:(h + 1) * 128],
                                                       st_R[:, h * 128:(h + 1) * 128]), "vec")
                            b.wait(te, "vec", rbv)
                            b.wait(te, "act", yev)
                    b.wait(sy, "act")
                    stv_y = b.inc(sy.dma_start(out=_rearr_cols(y_dram, tg, G),
                                               in_=y_grp.rearrange("p (h t) -> p h t", h=H)), "st", 16)
                    b.wait(se, "st", stv_y)
                b.wait(sy, "st")
                nc.all_engine_barrier()

                if "relay" in RET_PARTS:
                    b.inc(ge.dma_start(out=relay_in.ap().rearrange("(q h) v -> q h v", h=H),
                                       in_=st_R.rearrange("p (h v) -> p h v", v=128)), "wld", 16)
                    b.wait(ge, "wld")
                    ccv = b.inc(ge.collective_compute(
                        "AllGather", ALU.bypass, replica_groups=[list(range(8))],
                        ins=[relay_in.ap().opt()], outs=[relay_out.ap().opt()]), "cc", 1)
                    b.inc(ge.memset(rlb_f[:], 0.0), "pool")
                    b.wait(ge, "cc", ccv)
                    gv = b.inc(ge.indirect_dma_start(
                        out=rlb_f[:, :], out_offset=None,
                        in_=relay_out.ap()[:, :],
                        in_offset=IndirectOffsetOnAxis(t_ridx[:, 0:1], 0),
                        bounds_check=NC * H * DH - 1, oob_is_err=False), "wld", 16)
                    b.wait(ve, "wld", gv)
                    b.wait(ve, "pool")
                    b.inc(ve.tensor_copy(st_Rrelay[:], rlb_f[:]), "vec")
                nc.all_engine_barrier()

                for g in (range(NG) if "corr" in RET_PARTS else []):
                    tg = g * G
                    if g > 0:
                        b.wait(sy, "pe")
                        b.wait(sy, "vec")
                        b.wait(sy, "st")
                    v = b.inc(sy.dma_start(out=zg.rearrange("p (c t) -> p c t", c=NC),
                                           in_=_rearr_cols(z_buf, tg, G)), "ld", 16)
                    b.inc(sy.dma_start(out=crg[:], in_=crot[:, tg:tg + G]), "ld", 16)
                    cv = b.inc(sy.dma_start(out=srg[:], in_=srot[:, tg:tg + G]), "ld", 16)
                    yv_ld = b.inc(sy.dma_start(out=y_grp.rearrange("p (h t) -> p h t", h=H),
                                               in_=_rearr_cols(y_dram, tg, G)), "ld", 16)
                    b.wait(ve, "ld", cv)
                    _xn_build(zg, xn_g, psA, psB, tg, evt)
                    xnv = b.val("vec")
                    b.wait(te, "vec", xnv)
                    for h in range(H):
                        pv = 0
                        for c in range(NC):
                            pv = b.inc(te.matmul(psS1[:, 0:G], wq_sb[:, c * D + h * 128: c * D + (h + 1) * 128],
                                                 xn_g[:, c * G:(c + 1) * G],
                                                 start=(c == 0), stop=(c == NC - 1)), "pe")
                        b.wait(se, "pe", pv)
                        b.wait(se, "vec")
                        ev = b.inc(se.copy(evt[:, 0:G], psS1[:, 0:G]), "act")
                        b.wait(ve, "act", ev)
                        qb_v = b.inc(ve.tensor_copy(rotq[:, h * G:(h + 1) * G], evt[:, 0:G]), "vec")
                        b.wait(te, "vec", qb_v)
                        ppv = b.inc(te.matmul(psP[:, 0:G], t_pswap[:], rotq[:, h * G:(h + 1) * G],
                                              start=True, stop=True), "pe")
                        b.wait(ve, "pe", ppv)
                        b.inc(ve.tensor_mul(evt[:, G:2 * G], evt[:, 0:G], crg[:]), "vec")
                        b.inc(ve.tensor_mul(evt[:, 0:G], psP[:, 0:G], srg[:]), "vec")
                        rv = b.inc(ve.tensor_add(rotq[:, h * G:(h + 1) * G], evt[:, G:2 * G], evt[:, 0:G]), "vec")
                        b.wait(te, "vec", rv)
                        b.wait(ve, "ld", yv_ld)
                        for tc in range(4):
                            rq_c = rotq[:, h * G + tc * 128: h * G + (tc + 1) * 128]
                            qgv = b.inc(ve.tensor_mul(qgam[:], rq_c, t_gq[:, h * CH:(h + 1) * CH]), "vec")
                            b.wait(te, "vec", qgv)
                            cpv = b.inc(te.matmul(psS2[:, 0:128], st_Rrelay[:, h * 128:(h + 1) * 128],
                                                  qgam[:], start=True, stop=True), "pe")
                            b.wait(ve, "pe", cpv)
                            gscale = float(_gammas64()[h] ** (128 * (g * 4 + tc)))
                            ysl = y_grp[:, h * G + tc * 128: h * G + (tc + 1) * 128]
                            b.inc(ve.scalar_tensor_tensor(ysl, psS2[:, 0:128], gscale, ysl,
                                                          ALU.mult, ALU.add), "vec")
                    b.wait(sy, "vec")
                    b.inc(sy.dma_start(out=_rearr_cols(y_dram, tg, G),
                                       in_=y_grp.rearrange("p (h t) -> p h t", h=H)), "st", 16)
                b.wait(sy, "st")
            nc.all_engine_barrier()
            if "post" in RET_PARTS:
                _ret_post(li)
                nc.all_engine_barrier()
            if "ffn" in RET_PARTS:
                _ret_ffn(li)
                nc.all_engine_barrier()

        def _ret_post(li):
            with ExitStack() as s:
                y_g = s.enter_context(nc.sbuf_tensor(f"po{li}_yg", [128, H * G], BF16))
                sqb = s.enter_context(nc.sbuf_tensor(f"po{li}_sq", [128, 2 * G], BF16))
                mu8 = s.enter_context(nc.sbuf_tensor(f"po{li}_mu", [8, TOK], F32))
                var8 = s.enter_context(nc.sbuf_tensor(f"po{li}_va", [8, TOK], F32))
                t8 = s.enter_context(nc.sbuf_tensor(f"po{li}_t8", [8, G], F32))
                psS = s.enter_context(nc.psum_tensor(f"po{li}_pS", [128, 512], F32))
                psQ = s.enter_context(nc.psum_tensor(f"po{li}_pQ", [128, 512], F32))
                for g in range(NG):
                    tg = g * G
                    if g > 0:
                        b.wait(sy, "pe")
                    v = b.inc(sy.dma_start(out=y_g.rearrange("p (h t) -> p h t", h=H),
                                           in_=_rearr_cols(y_dram, tg, G)), "ld", 16)
                    b.wait(ve, "ld", v)
                    b.wait(te, "ld", v)
                    if g > 0:
                        b.wait(te, "vec")
                    pv = 0
                    for h in range(H):
                        ysl = y_g[:, h * G:(h + 1) * G]
                        sqs = sqb[:, (h % 2) * G:(h % 2) * G + G]
                        if h >= 2 or g > 0:
                            b.wait(ve, "pe")
                        sv = b.inc(ve.tensor_mul(sqs, ysl, ysl), "vec")
                        b.inc(te.matmul(psS[0:8, 0:G], t_ehs_b[:, h * H:(h + 1) * H], ysl,
                                        start=(h == 0), stop=(h == H - 1)), "pe")
                        b.wait(te, "vec", sv)
                        pv = b.inc(te.matmul(psQ[0:8, 0:G], t_ehs_b[:, h * H:(h + 1) * H], sqs,
                                             start=(h == 0), stop=(h == H - 1)), "pe")
                    b.wait(ve, "pe", pv)
                    b.inc(ve.tensor_scalar_mul(mu8[0:8, tg:tg + G], psS[0:8, 0:G], 1.0 / DH), "vec")
                    b.inc(ve.tensor_mul(t8[0:8, :], mu8[0:8, tg:tg + G], mu8[0:8, tg:tg + G]), "vec")
                    b.inc(ve.tensor_scalar_mul(var8[0:8, tg:tg + G], psQ[0:8, 0:G], 1.0 / DH), "vec")
                    b.inc(ve.tensor_sub(var8[0:8, tg:tg + G], var8[0:8, tg:tg + G], t8[0:8, :]), "vec")
                b.inc(ve.tensor_scalar_add(var8[0:8, :], var8[0:8, :], EPS), "vec")
                rv = b.inc(ve.reciprocal(var8[0:8, :], var8[0:8, :]), "vec")
                b.wait(se, "vec", rv)
                av = b.inc(se.activation(st_A8[0:8, :], var8[0:8, :], AF.Sqrt), "act")
                b.wait(ve, "act", av)
                bv = b.inc(ve.scalar_tensor_tensor(st_B8[0:8, :], mu8[0:8, :], -1.0,
                                                   st_A8[0:8, :], ALU.mult, ALU.mult), "vec")
                b.wait(se, "vec", bv)
                b.wait(te, "vec", bv)
            nc.all_engine_barrier()
            with ExitStack() as s:
                wg_sb = s.enter_context(nc.sbuf_tensor(f"po{li}_wg", [128, NC * D], BF16))
                wo_sb = s.enter_context(nc.sbuf_tensor(f"po{li}_wo", [128, NC * D], BF16))
                zg = s.enter_context(nc.sbuf_tensor(f"po{li}_zg", [128, NC * G], F32))
                xn_g = s.enter_context(nc.sbuf_tensor(f"po{li}_xn", [128, NC * G], BF16))
                y_g = s.enter_context(nc.sbuf_tensor(f"po{li}_yg2", [128, H * G], BF16))
                g_sb = s.enter_context(nc.sbuf_tensor(f"po{li}_gs", [128, NC * G], BF16))
                m_sb = s.enter_context(nc.sbuf_tensor(f"po{li}_ms", [128, NC * G], BF16))
                scr = s.enter_context(nc.sbuf_tensor(f"po{li}_scr", [128, 2 * G], F32))
                psA = s.enter_context(nc.psum_tensor(f"po{li}_pA", [128, 512], F32))
                psB = s.enter_context(nc.psum_tensor(f"po{li}_pB", [128, 512], F32))
                psG = s.enter_context(nc.psum_tensor(f"po{li}_pG", [128, 512], F32))
                psO = s.enter_context(nc.psum_tensor(f"po{li}_pO", [128, 512], F32))
                psbc = s.enter_context(nc.psum_tensor(f"po{li}_pbc", [128, 512], F32))
                psbd = s.enter_context(nc.psum_tensor(f"po{li}_pbd", [128, 512], F32))
                b.inc(ge.dma_start(out=wg_sb.rearrange("p (c d) -> p c d", d=D),
                                   in_=w_g[li].rearrange("(c p) d -> p c d", p=128)), "wld", 16)
                wldv = b.inc(ge.dma_start(out=wo_sb.rearrange("p (c d) -> p c d", d=D),
                                          in_=w_o[li].rearrange("(c p) d -> p c d", p=128)), "wld", 16)
                b.wait(te, "wld", wldv)
                for g in range(NG):
                    tg = g * G
                    if g > 0:
                        b.wait(sy, "pe")
                        b.wait(sy, "vec")
                        b.wait(sy, "st")
                    v = b.inc(sy.dma_start(out=zg.rearrange("p (c t) -> p c t", c=NC),
                                           in_=_rearr_cols(z_buf, tg, G)), "ld", 16)
                    yv = b.inc(sy.dma_start(out=y_g.rearrange("p (h t) -> p h t", h=H),
                                            in_=_rearr_cols(y_dram, tg, G)), "ld", 16)
                    b.wait(ve, "ld", v)
                    _xn_build(zg, xn_g, psA, psB, tg, scr)
                    xnv = b.val("vec")
                    b.wait(te, "vec", xnv)
                    gv = 0
                    for co in range(NC):
                        if gv > 0:
                            b.wait(te, "act", gv)
                        pv = 0
                        for c in range(NC):
                            pv = b.inc(te.matmul(psG[:, 0:G], wg_sb[:, c * D + co * 128: c * D + (co + 1) * 128],
                                                 xn_g[:, c * G:(c + 1) * G],
                                                 start=(c == 0), stop=(c == NC - 1)), "pe")
                        b.wait(se, "pe", pv)
                        b.wait(se, "vec") if co == 0 and g > 0 else None
                        gv = b.inc(se.activation(g_sb[:, co * G:(co + 1) * G], psG[:, 0:G], AF.Silu), "act")
                    b.wait(ve, "ld", yv)
                    for h in range(H):
                        if h > 0 or g > 0:
                            b.wait(te, "vec")
                        bca = b.inc(te.matmul(psbc[:, 0:G], t_ebc_f[:, h * 128:(h + 1) * 128],
                                              st_A8[0:8, tg:tg + G], start=True, stop=True), "pe")
                        bcb = b.inc(te.matmul(psbd[:, 0:G], t_ebc_f[:, h * 128:(h + 1) * 128],
                                              st_B8[0:8, tg:tg + G], start=True, stop=True), "pe")
                        b.wait(ve, "pe", bcb)
                        b.wait(ve, "act", gv)
                        ysl = y_g[:, h * G:(h + 1) * G]
                        b.inc(ve.tensor_mul(scr[:, 0:G], ysl, psbc[:, 0:G]), "vec")
                        b.inc(ve.tensor_add(scr[:, 0:G], scr[:, 0:G], psbd[:, 0:G]), "vec")
                        b.inc(ve.tensor_scalar(scr[:, G:2 * G], scr[:, 0:G],
                                               t_gns[:, li * NC + h: li * NC + h + 1],
                                               t_gnb[:, li * NC + h: li * NC + h + 1],
                                               ALU.mult, ALU.add), "vec")
                        mv = b.inc(ve.tensor_mul(m_sb[:, h * G:(h + 1) * G], scr[:, G:2 * G],
                                                 g_sb[:, h * G:(h + 1) * G]), "vec")
                    b.wait(te, "vec", mv)
                    zv = 0
                    for co in range(NC):
                        if zv > 0:
                            b.wait(te, "vec", zv)
                        pv = 0
                        for c in range(NC):
                            pv = b.inc(te.matmul(psO[:, 0:G], wo_sb[:, c * D + co * 128: c * D + (co + 1) * 128],
                                                 m_sb[:, c * G:(c + 1) * G],
                                                 start=(c == 0), stop=(c == NC - 1)), "pe")
                        b.wait(ve, "pe", pv)
                        zv = b.inc(ve.tensor_add(zg[:, co * G:(co + 1) * G], zg[:, co * G:(co + 1) * G],
                                                 psO[:, 0:G]), "vec")
                    b.wait(sy, "vec", zv)
                    b.inc(sy.dma_start(out=_rearr_cols(z_buf, tg, G),
                                       in_=zg.rearrange("p (c t) -> p c t", c=NC)), "st", 16)
                b.wait(sy, "st")

        def _ret_ffn(li):
            lnstats(z_buf, 0, TOK, f"rf{li}s")
            nc.all_engine_barrier()
            with ExitStack() as s:
                w1_sb = s.enter_context(nc.sbuf_tensor(f"rf{li}_w1", [128, NC * FF], BF16))
                w2_sb = s.enter_context(nc.sbuf_tensor(f"rf{li}_w2", [128, 16 * D], BF16))
                zg = s.enter_context(nc.sbuf_tensor(f"rf{li}_zg", [128, NC * G], F32))
                xn_g = s.enter_context(nc.sbuf_tensor(f"rf{li}_xn", [128, NC * G], BF16))
                hid = s.enter_context(nc.sbuf_tensor(f"rf{li}_hid", [128, 16 * G], BF16))
                scr = s.enter_context(nc.sbuf_tensor(f"rf{li}_scr", [128, 4 * G], F32))
                psA = s.enter_context(nc.psum_tensor(f"rf{li}_pA", [128, 512], F32))
                psB = s.enter_context(nc.psum_tensor(f"rf{li}_pB", [128, 512], F32))
                psF = [s.enter_context(nc.psum_tensor(f"rf{li}_pF{i}", [128, 512], F32)) for i in range(2)]
                psW = [s.enter_context(nc.psum_tensor(f"rf{li}_pW{i}", [128, 512], F32)) for i in range(2)]
                ringF = Ring(b, f"rf{li}_rF", [p[:, 0:512] for p in psF])
                ringW = Ring(b, f"rf{li}_rW", [p[:, 0:512] for p in psW])
                b.inc(ge.dma_start(out=w1_sb.rearrange("p (c d) -> p c d", d=FF),
                                   in_=w_1[li].rearrange("(c p) d -> p c d", p=128)), "wld", 16)
                wldv = b.inc(ge.dma_start(out=w2_sb.rearrange("p (c d) -> p c d", d=D),
                                          in_=w_2[li].rearrange("(c p) d -> p c d", p=128)), "wld", 16)
                b.wait(te, "wld", wldv)
                c1 = GELU_C
                c1a = GELU_C * GELU_A
                for g in range(NG):
                    tg = g * G
                    if g > 0:
                        b.wait(sy, "pe")
                        b.wait(sy, "vec")
                        b.wait(sy, "st")
                    v = b.inc(sy.dma_start(out=zg.rearrange("p (c t) -> p c t", c=NC),
                                           in_=_rearr_cols(z_buf, tg, G)), "ld", 16)
                    b.wait(ve, "ld", v)
                    _xn_build(zg, xn_g, psA, psB, tg, scr)
                    xnv = b.val("vec")
                    b.wait(te, "vec", xnv)
                    hv = 0
                    for fo in range(16):
                        pf = ringF.acquire(te)
                        pv = 0
                        for c in range(NC):
                            pv = b.inc(te.matmul(pf, w1_sb[:, c * FF + fo * 128: c * FF + (fo + 1) * 128],
                                                 xn_g[:, c * G:(c + 1) * G],
                                                 start=(c == 0), stop=(c == NC - 1)), "pe")
                        b.wait(se, "pe", pv)
                        if hv > 0:
                            b.wait(se, ringF.name, hv)
                        sqv = b.inc(se.activation(scr[:, 0:G], pf, AF.Square), "act")
                        b.wait(ve, "act", sqv)
                        b.inc(ve.tensor_scalar(scr[:, G:2 * G], scr[:, 0:G], c1a, c1, ALU.mult, ALU.add), "vec")
                        uv = b.inc(ve.tensor_mul(scr[:, 2 * G:3 * G], pf, scr[:, G:2 * G]), "vec")
                        b.wait(se, "vec", uv)
                        tv = b.inc(se.activation(scr[:, 3 * G:4 * G], scr[:, 2 * G:3 * G], AF.Tanh), "act")
                        b.wait(ve, "act", tv)
                        b.inc(ve.tensor_scalar(scr[:, G:2 * G], scr[:, 3 * G:4 * G], 0.5, 0.5,
                                               ALU.mult, ALU.add), "vec")
                        hvi = ve.tensor_mul(hid[:, fo * G:(fo + 1) * G], pf, scr[:, G:2 * G])
                        ringF.release(hvi)
                        hv = b.cnt[ringF.name]
                    b.wait(te, ringF.name, hv)
                    for co in range(NC):
                        pw = ringW.acquire(te)
                        pv = 0
                        for fc in range(16):
                            pv = b.inc(te.matmul(pw, w2_sb[:, fc * D + co * 128: fc * D + (co + 1) * 128],
                                                 hid[:, fc * G:(fc + 1) * G],
                                                 start=(fc == 0), stop=(fc == 15)), "pe")
                        b.wait(ve, "pe", pv)
                        zvi = ve.tensor_add(zg[:, co * G:(co + 1) * G], zg[:, co * G:(co + 1) * G], pw)
                        ringW.release(zvi)
                    b.wait(sy, ringW.name)
                    b.inc(sy.dma_start(out=_rearr_cols(z_buf, tg, G),
                                       in_=zg.rearrange("p (c t) -> p c t", c=NC)), "st", 16)
                b.wait(sy, "st")

        def merge_rz():
            with ExitStack() as s:
                rg = s.enter_context(nc.sbuf_tensor("mg_rg", [128, NC * G], F32))
                zg2 = s.enter_context(nc.sbuf_tensor("mg_zg", [128, NC * G], F32))
                for g in range(NG):
                    tg = g * G
                    if g > 0:
                        b.wait(sy, "st")
                        b.wait(sy, "vec")
                    b.inc(sy.dma_start(out=rg.rearrange("p (c t) -> p c t", c=NC),
                                       in_=_rearr_cols(r_res, tg, G)), "ld", 16)
                    v = b.inc(sy.dma_start(out=zg2.rearrange("p (c t) -> p c t", c=NC),
                                           in_=_rearr_cols(z_buf, tg, G)), "ld", 16)
                    b.wait(ve, "ld", v)
                    vv = 0
                    for c in range(NC):
                        vv = b.inc(ve.tensor_add(rg[:, c * G:(c + 1) * G], rg[:, c * G:(c + 1) * G],
                                                 zg2[:, c * G:(c + 1) * G]), "vec")
                    b.wait(sy, "vec", vv)
                    b.inc(sy.dma_start(out=_rearr_cols(r_res, tg, G),
                                       in_=rg.rearrange("p (c t) -> p c t", c=NC)), "st", 16)
                b.wait(sy, "st")
            nc.all_engine_barrier()

        def p4_output():
            with ExitStack() as ph:
                def src(tc, buf):
                    return b.inc(sy.dma_start(
                        out=buf.rearrange("p (c t) -> p c t", c=NC),
                        in_=_rearr_cols(r_res, tc * 128, 128)), "ld", 16)
                def dst(tc, buf):
                    b.inc(sy.dma_start(out=out_t[tc * 128:(tc + 1) * 128, :], in_=buf[:, :]), "st", 16)
                transpose_stream(src, dst, ph, "p4")

        load_tables()
        if "P0" in phases:
            p0_ingest()
        if "FFN1" in phases:
            moe_phase("f1")
        if "RET" in phases:
            znorm_init()
            for li in range(L):
                ret_layer(li)
        if "MERGE" in phases:
            merge_rz()
        if "FFN2" in phases:
            moe_phase("f2")
        if "P4" in phases:
            p4_output()

    return nc, b


def _shard_in_maps(inputs):
    X = np.ascontiguousarray(inputs["L0_attn_X"])
    maps = []
    for core in range(8):
        bat, half = core >> 1, core & 1
        t = host_tables(core)
        m = {
            "x": np.ascontiguousarray(X[bat, half * TOK:(half + 1) * TOK, :]),
            "wq": inputs["L0_attn_Wq"], "wk": inputs["L0_attn_Wk"],
            "wv": inputs["L0_attn_Wv"], "wg": inputs["L0_attn_Wg"],
            "wo": inputs["L0_attn_Wo"], "w1": inputs["L0_attn_W1"],
            "w2": inputs["L0_attn_W2"],
            "ln_s": _feat_tile(inputs["L0_attn_ln_scale"]),
            "ln_b": _feat_tile(inputs["L0_attn_ln_bias"]),
            "gn_s": np.concatenate([_feat_tile(inputs["L0_attn_gn_scale"][i]) for i in range(L)], axis=1),
            "gn_b": np.concatenate([_feat_tile(inputs["L0_attn_gn_bias"][i]) for i in range(L)], axis=1),
            "gate_w": inputs["L0_ffn_gate_w"],
            "we_g": inputs["L0_ffn_w_gate"], "we_u": inputs["L0_ffn_w_up"],
            "we_d": inputs["L0_ffn_w_down"],
            "ws_g": inputs["L0_ffn_ws_gate"], "ws_u": inputs["L0_ffn_ws_up"],
            "ws_d": inputs["L0_ffn_ws_down"],
            "crot": t["crot"], "srot": t["srot"], "ddec": t["ddec"], "gq": t["gq"],
            "kscale": t["kscale"], "id_f": t["id_f"],
            "ones_row_f": t["ones_row_f"], "ones_col_f": t["ones_col_f"],
            "ehs_f": t["ehs_f"], "ebc_f": t["ebc_f"],
            "id_b": t["id_b"], "pswap_b": t["pswap_b"],
            "ones_col_b": t["ones_col_b"], "ehs_b": t["ehs_b"],
            "relay_idx": t["relay_idx"],
        }
        m = {k: (np.ascontiguousarray(v) if isinstance(v, np.ndarray) else np.ascontiguousarray(np.asarray(v))) for k, v in m.items()}
        maps.append(m)
    return maps


def kernel(**inputs):
    inputs = {k: np.asarray(v) for k, v in inputs.items()}
    maps = _shard_in_maps(inputs)
    nc, _ = build_program(ALL_PHASES)
    res = run_bass_kernel_spmd(nc, maps, core_ids=list(range(8)))
    out = np.empty((BAT, S, D), np.float32)
    for core in range(8):
        bat, half = core >> 1, core & 1
        out[bat, half * TOK:(half + 1) * TOK, :] = res.results[core]["out"]
    return out


# version 02
# speedup vs baseline: 1.1129x; 1.1129x over previous
import numpy as np
import ml_dtypes
from contextlib import ExitStack
import concourse.bass as bass
import concourse.mybir as mybir
from concourse.bass_core_rust import IndirectOffsetOnAxis
from concourse.bass_utils import run_bass_kernel_spmd

F32 = mybir.dt.float32
BF16 = mybir.dt.bfloat16
I32 = mybir.dt.int32
AF = mybir.ActivationFunctionType
ALU = mybir.AluOpType
AX = mybir.AxisListType

BAT, S, D, H, DH, FF = 4, 8192, 1024, 8, 128, 2048
NEXP = 8
L = 4
TOK = 4096
NC = 8
G = 512
NG = TOK // G
CH = 128
NCH = TOK // CH
QB = 1024
NQB = TOK // QB
EPS = 1e-5
GELU_C = float(np.sqrt(2.0 / np.pi))
GELU_A = 0.044715

ALL_PHASES = ("P0", "FFN1", "RET", "MERGE", "FFN2", "P4")
DEBUG_WPE = False
RET_PARTS = ("scan", "relay", "corr", "post", "ffn")
DEBUG_SCRATCH = False


def _gammas64():
    return 1.0 - np.exp(np.linspace(np.log(1 / 32), np.log(1 / 512), H))


def host_tables(core_id):
    gam = _gammas64()
    t = {}
    half = core_id & 1
    pos = np.arange(half * TOK, (half + 1) * TOK, dtype=np.float64)
    inv_freq = 1.0 / 10000.0 ** (np.arange(0, DH, 2, dtype=np.float64) / DH)
    ang = pos[:, None] * inv_freq[None, :]
    cos, sin = np.cos(ang), np.sin(ang)
    crot = np.empty((DH, TOK), np.float64)
    srot = np.empty((DH, TOK), np.float64)
    crot[0::2, :] = cos.T
    crot[1::2, :] = cos.T
    srot[0::2, :] = -sin.T
    srot[1::2, :] = sin.T
    t["crot"] = crot.astype(np.float32)
    t["srot"] = srot.astype(np.float32)
    jj, ii = np.meshgrid(np.arange(CH), np.arange(CH), indexing="ij")
    dd = np.zeros((CH, H * CH), np.float64)
    gq = np.zeros((CH, H * CH), np.float64)
    for h in range(H):
        m = (ii - jj).astype(np.float64)
        dd[:, h * CH:(h + 1) * CH] = np.where(m >= 0, gam[h] ** m, 0.0)
        gq[:, h * CH:(h + 1) * CH] = gam[h] ** (ii[0][None, :] + 1.0)
    t["ddec"] = dd.astype(np.float32)
    t["gq"] = gq.astype(np.float32)
    ks = np.zeros((CH, H), np.float64)
    for h in range(H):
        ks[:, h] = gam[h] ** (CH - np.arange(CH, dtype=np.float64))
    t["kscale"] = ks.astype(np.float32)
    eye = np.eye(128, dtype=np.float32)
    t["id_f"] = eye
    t["id_b"] = eye.astype(ml_dtypes.bfloat16)
    pswap = np.zeros((128, 128), np.float32)
    for d in range(128):
        pswap[d ^ 1, d] = 1.0
    t["pswap_b"] = pswap.astype(ml_dtypes.bfloat16)
    ones_col = np.ones((128, 1), np.float32)
    t["ones_col_f"] = ones_col
    t["ones_col_b"] = ones_col.astype(ml_dtypes.bfloat16)
    t["ones_row_f"] = np.ones((1, 128), np.float32)
    ehs = np.zeros((128, H * H), np.float32)
    for h in range(H):
        ehs[:, h * H + h] = 1.0
    t["ehs_b"] = ehs.astype(ml_dtypes.bfloat16)
    t["ehs_f"] = ehs
    ebc = np.zeros((H, H * 128), np.float32)
    for h in range(H):
        ebc[h, h * 128:(h + 1) * 128] = 1.0
    t["ebc_f"] = ebc
    if core_id & 1:
        idx = (np.arange(128, dtype=np.int32) * 8 + (core_id - 1) * 1024).reshape(128, 1)
    else:
        idx = np.full((128, 1), 2 ** 30, dtype=np.int32)
    t["relay_idx"] = idx
    return t


def _feat_tile(vec):
    return np.ascontiguousarray(vec.reshape(NC, 128).T)


class Bld:
    def __init__(self, nc):
        self.nc = nc
        self.sems = {}
        self.cnt = {}

    def sem(self, name):
        if name not in self.sems:
            self.sems[name] = self.nc.alloc_semaphore(name)
            self.cnt[name] = 0
        return self.sems[name]

    def inc(self, instr, name, amt=1):
        instr.then_inc(self.sem(name), amt)
        self.cnt[name] += amt
        return self.cnt[name]

    def val(self, name):
        self.sem(name)
        return self.cnt[name]

    def wait(self, eng, name, v=None):
        if v is None:
            v = self.cnt[name]
        eng.wait_ge(self.sem(name), v)


class Ring:
    def __init__(self, b: Bld, name, tiles):
        self.b = b
        self.name = name
        self.tiles = tiles
        self.n = 0
        b.sem(name)
        self.base = b.cnt[name]

    def acquire(self, eng):
        k = len(self.tiles)
        need = self.n - (k - 1)
        if need > 0:
            self.b.wait(eng, self.name, self.base + need)
        tile = self.tiles[self.n % k]
        self.n += 1
        return tile

    def release(self, instr, n=1):
        self.b.inc(instr, self.name, n)


def _rearr_cols(dram_ap, c0, cols):
    sl = dram_ap[:, c0:c0 + cols]
    return sl.rearrange("(c p) t -> p c t", p=128)


def build_program(phases=ALL_PHASES):
    nc = bass.Bass("TRN2", target_bir_lowering=False, debug=False)
    b = Bld(nc)

    din = {}
    def dram_in(name, shape):
        din[name] = nc.dram_tensor(name, list(shape), F32, kind="ExternalInput").ap()
        return din[name]

    x_in = dram_in("x", (TOK, D))
    w_q = dram_in("wq", (L, D, D))
    w_k = dram_in("wk", (L, D, D))
    w_v = dram_in("wv", (L, D, D))
    w_g = dram_in("wg", (L, D, D))
    w_o = dram_in("wo", (L, D, D))
    w_1 = dram_in("w1", (L, D, FF))
    w_2 = dram_in("w2", (L, FF, D))
    ln_s = dram_in("ln_s", (128, NC))
    ln_b = dram_in("ln_b", (128, NC))
    gn_s = dram_in("gn_s", (128, L * NC))
    gn_b = dram_in("gn_b", (128, L * NC))
    gate_w = dram_in("gate_w", (D, NEXP))
    we_g = dram_in("we_g", (NEXP, D, FF))
    we_u = dram_in("we_u", (NEXP, D, FF))
    we_d = dram_in("we_d", (NEXP, FF, D))
    ws_g = dram_in("ws_g", (D, FF))
    ws_u = dram_in("ws_u", (D, FF))
    ws_d = dram_in("ws_d", (FF, D))
    crot = dram_in("crot", (DH, TOK))
    srot = dram_in("srot", (DH, TOK))
    ddec = dram_in("ddec", (CH, H * CH))
    gq_t = dram_in("gq", (CH, H * CH))
    kscale = dram_in("kscale", (CH, H))
    id_f = dram_in("id_f", (128, 128))
    ones_row_f = dram_in("ones_row_f", (1, 128))
    ehs_f = dram_in("ehs_f", (128, H * H))
    ebc_f = dram_in("ebc_f", (H, H * 128))
    bf16_tabs = {}
    for nm, shape in [("id_b", (128, 128)), ("pswap_b", (128, 128)),
                      ("ones_col_b", (128, 1)), ("ehs_b", (128, H * H))]:
        bf16_tabs[nm] = nc.dram_tensor(nm, list(shape), BF16, kind="ExternalInput").ap()
    ones_col_f = dram_in("ones_col_f", (128, 1))
    relay_idx = nc.dram_tensor("relay_idx", [128, 1], I32, kind="ExternalInput").ap()

    out_t = nc.dram_tensor("out", [TOK, D], F32, kind="ExternalOutput").ap()
    dbg_t = nc.dram_tensor("dbg", [8, TOK], F32, kind="ExternalOutput").ap() if DEBUG_WPE else None
    dbg2_t = nc.dram_tensor("dbg2", [128, 64], F32, kind="ExternalOutput").ap() if DEBUG_WPE else None

    _scr_kind = "ExternalOutput" if DEBUG_SCRATCH else "Internal"
    r_res = nc.dram_tensor("r_res", [D, TOK], F32, kind=_scr_kind).ap()
    z_buf = nc.dram_tensor("z_buf", [D, TOK], F32, kind=_scr_kind).ap()
    y_dram = nc.dram_tensor("y_dram", [D, TOK], BF16, kind=_scr_kind).ap()
    relay_in = nc.dram_tensor("relay_in", [H * DH, DH], F32)
    relay_out = nc.dram_tensor("relay_out", [NC * H * DH, DH], F32)

    with ExitStack() as ctx:
        sb = lambda name, shape, dt: ctx.enter_context(nc.sbuf_tensor(name, shape, dt))
        psb = lambda name, shape, dt=F32: ctx.enter_context(nc.psum_tensor(name, shape, dt))

        te, ve, se, ge, sy = nc.tensor, nc.vector, nc.scalar, nc.gpsimd, nc.sync

        t_id_f = sb("t_id_f", [128, 128], F32)
        t_id_b = sb("t_id_b", [128, 128], BF16)
        t_pswap = sb("t_pswap", [128, 128], BF16)
        t_ones_row = sb("t_ones_row", [1, 128], F32)
        t_ones_col_f = sb("t_ones_col_f", [128, 1], F32)
        t_ones_col_b = sb("t_ones_col_b", [128, 1], BF16)
        t_ehs_b = sb("t_ehs_b", [128, H * H], BF16)
        t_ehs_f = sb("t_ehs_f", [128, H * H], F32)
        t_ebc_f = sb("t_ebc_f", [H, H * 128], F32)
        t_ddec = sb("t_ddec", [CH, H * CH], F32)
        t_gq = sb("t_gq", [CH, H * CH], F32)
        t_kscale = sb("t_kscale", [CH, H], F32)
        t_lns = sb("t_lns", [128, NC], F32)
        t_lnb = sb("t_lnb", [128, NC], F32)
        t_gns = sb("t_gns", [128, L * NC], F32)
        t_gnb = sb("t_gnb", [128, L * NC], F32)
        t_ridx = sb("t_ridx", [128, 1], I32)
        st_A = sb("st_A", [1, TOK], F32)
        st_B = sb("st_B", [1, TOK], F32)
        st_R = sb("st_R", [128, H * DH], F32)
        st_Rb = sb("st_Rb", [128, H * DH], BF16)
        st_Rrelay = sb("st_Rrelay", [128, H * DH], BF16)
        rlb_f = sb("rlb_f", [128, H * DH], F32)

        consts = sb("consts", [128, 2], F32)
        nc.const_aps.register(F32, 0.5, consts[:, 0:1])
        nc.const_aps.register(F32, 2.0, consts[:, 1:2])
        t_one2 = sb("t_one2", [128, 2], F32)
        t_neg2 = sb("t_neg2", [128, 2], F32)

        def load_tables():
            ld = [
                (t_id_f, id_f), (t_ones_row, ones_row_f), (t_ones_col_f, ones_col_f),
                (t_ehs_f, ehs_f), (t_ebc_f, ebc_f), (t_ddec, ddec), (t_gq, gq_t),
                (t_kscale, kscale), (t_lns, ln_s), (t_lnb, ln_b), (t_gns, gn_s),
                (t_gnb, gn_b),
                (t_id_b, bf16_tabs["id_b"]), (t_pswap, bf16_tabs["pswap_b"]),
                (t_ones_col_b, bf16_tabs["ones_col_b"]), (t_ehs_b, bf16_tabs["ehs_b"]),
                (t_ridx, relay_idx),
            ]
            for dst, src in ld:
                b.inc(sy.dma_start(out=dst[:], in_=src[:, :]), "ld", 16)
            ge.memset(consts[:, 0:1], 0.5)
            ge.memset(t_one2[:], 1.0)
            ge.memset(t_neg2[:], -1.0)
            b.inc(ge.memset(consts[:, 1:2], 2.0), "pool", 1)
            b.wait(sy, "ld")
            b.wait(ge, "pool")
            nc.all_engine_barrier()

        def transpose_stream(src_load, dst_store, ph, tag):
            xin = ph.enter_context(nc.sbuf_tensor(f"{tag}_xin", [128, 2 * D], F32))
            xout = ph.enter_context(nc.sbuf_tensor(f"{tag}_xout", [128, 2 * D], F32))
            tps = [ph.enter_context(nc.psum_tensor(f"{tag}_tp{i}", [128, 512], F32)) for i in range(4)]
            ring = Ring(b, f"{tag}_r", [tp[:, 0:128] for tp in tps])
            stv = []
            for tc in range(NCH):
                par = tc % 2
                xin_s = xin[:, par * D:(par + 1) * D]
                xout_s = xout[:, par * D:(par + 1) * D]
                if tc >= 2:
                    b.wait(sy, "st", stv[tc - 2])
                v = src_load(tc, xin_s)
                b.wait(te, "ld", v)
                for fc in range(NC):
                    ps = ring.acquire(te)
                    pv = b.inc(te.transpose(ps, xin_s[:, fc * 128:(fc + 1) * 128], t_id_f[:]), "pe")
                    b.wait(ve, "pe", pv)
                    if fc == 0 and tc >= 2:
                        b.wait(ve, "st", stv[tc - 2])
                    ring.release(ve.tensor_copy(xout_s[:, fc * 128:(fc + 1) * 128], ps))
                b.wait(sy, ring.name)
                dst_store(tc, xout_s)
                stv.append(b.cnt["st"])
            b.wait(sy, "st")

        def p0_ingest():
            with ExitStack() as ph:
                def src(tc, buf):
                    return b.inc(sy.dma_start(out=buf[:, :], in_=x_in[tc * 128:(tc + 1) * 128, :]), "ld", 16)
                def dst(tc, buf):
                    b.inc(sy.dma_start(
                        out=_rearr_cols(r_res, tc * 128, 128),
                        in_=buf.rearrange("p (c t) -> p c t", c=NC)), "st", 16)
                transpose_stream(src, dst, ph, "p0")
            nc.all_engine_barrier()

        def lnstats(src_dram, t0, ntok, tag):
            ngr = ntok // G
            with ExitStack() as s:
                rg = s.enter_context(nc.sbuf_tensor(f"{tag}_rg", [128, 2 * NC * G], F32))
                sq = s.enter_context(nc.sbuf_tensor(f"{tag}_sq", [128, 2 * G], F32))
                mu = s.enter_context(nc.sbuf_tensor(f"{tag}_mu", [1, ntok], F32))
                var = s.enter_context(nc.sbuf_tensor(f"{tag}_var", [1, ntok], F32))
                tmp = s.enter_context(nc.sbuf_tensor(f"{tag}_tmp", [1, ntok], F32))
                psS = s.enter_context(nc.psum_tensor(f"{tag}_psS", [128, 512], F32))
                psQ = s.enter_context(nc.psum_tensor(f"{tag}_psQ", [128, 512], F32))
                pev = [0] * ngr
                vev = [0] * ngr
                for g in range(ngr):
                    tg = t0 + g * G
                    par = g % 2
                    rgs = rg[:, par * NC * G:(par + 1) * NC * G]
                    if g >= 2:
                        b.wait(sy, "pe", pev[g - 2])
                    v = b.inc(sy.dma_start(out=rgs.rearrange("p (c t) -> p c t", c=NC),
                                           in_=_rearr_cols(src_dram, tg, G)), "ld", 16)
                    b.wait(ve, "ld", v)
                    b.wait(te, "ld", v)
                    if g >= 1:
                        b.wait(te, "vec", vev[g - 1])
                    pv = 0
                    for c in range(NC):
                        rc = rgs[:, c * G:(c + 1) * G]
                        sqs = sq[:, (c % 2) * G:(c % 2) * G + G]
                        if c >= 2 or g > 0:
                            b.wait(ve, "pe")
                        vv = b.inc(ve.tensor_mul(sqs, rc, rc), "vec")
                        b.inc(te.matmul(psS[0:1, :], t_ones_col_f[:], rc,
                                        start=(c == 0), stop=(c == NC - 1)), "pe")
                        b.wait(te, "vec", vv)
                        pv = b.inc(te.matmul(psQ[0:1, :], t_ones_col_f[:], sqs,
                                             start=(c == 0), stop=(c == NC - 1)), "pe")
                    pev[g] = pv
                    b.wait(ve, "pe", pv)
                    o = tg - t0
                    b.inc(ve.tensor_scalar_mul(mu[0:1, o:o + G], psS[0:1, :], 1.0 / D), "vec")
                    b.inc(ve.tensor_mul(tmp[0:1, o:o + G], mu[0:1, o:o + G], mu[0:1, o:o + G]), "vec")
                    b.inc(ve.tensor_scalar_mul(var[0:1, o:o + G], psQ[0:1, :], 1.0 / D), "vec")
                    vev[g] = b.inc(ve.tensor_sub(var[0:1, o:o + G], var[0:1, o:o + G], tmp[0:1, o:o + G]), "vec")
                b.inc(ve.tensor_scalar_add(var[0:1, 0:ntok], var[0:1, 0:ntok], EPS), "vec")
                vv = b.inc(ve.reciprocal(tmp[0:1, 0:ntok], var[0:1, 0:ntok]), "vec")
                b.wait(se, "vec", vv)
                av = b.inc(se.activation(st_A[0:1, t0:t0 + ntok], tmp[0:1, 0:ntok], AF.Sqrt), "act")
                b.wait(ve, "act", av)
                bv = b.inc(ve.scalar_tensor_tensor(st_B[0:1, t0:t0 + ntok], mu[0:1, 0:ntok], -1.0,
                                                   st_A[0:1, t0:t0 + ntok], ALU.mult, ALU.mult), "vec")
                b.wait(se, "vec", bv)
            return bv

        def moe_phase(tag):
            with ExitStack() as ph:
                xn = ph.enter_context(nc.sbuf_tensor(f"{tag}_xn", [128, NC * QB], BF16))
                yac = ph.enter_context(nc.sbuf_tensor(f"{tag}_yac", [128, NC * QB], F32))
                wpeT = ph.enter_context(nc.sbuf_tensor(f"{tag}_wpeT", [8, QB], F32))
                gwt = ph.enter_context(nc.sbuf_tensor(f"{tag}_gwt", [128, NC * 8], F32))
                wrow = ph.enter_context(nc.sbuf_tensor(f"{tag}_wrow", [128, QB], BF16))
                v = b.inc(sy.dma_start(out=gwt.rearrange("p (c e) -> p c e", c=NC),
                                       in_=gate_w.rearrange("(c p) e -> p c e", p=128)), "ld", 16)
                b.wait(te, "ld", v)
                for q in range(NQB):
                    t0 = q * QB
                    lnstats(r_res, t0, QB, f"{tag}q{q}s")
                    nc.all_engine_barrier()
                    _moe_a2(tag, q, xn, wpeT, gwt)
                    nc.all_engine_barrier()
                    if DEBUG_WPE and tag == "f1":
                        b.inc(sy.dma_start(out=dbg_t[:, t0:t0 + QB], in_=wpeT[:, :]), "st", 16)
                        b.wait(sy, "st")
                    _moe_experts(tag, q, xn, yac, wpeT, wrow)
                    nc.all_engine_barrier()
                    _moe_cadd(tag, q, yac)
                    nc.all_engine_barrier()

        def _moe_a2(tag, q, xn, wpeT, gwt):
            t0 = q * QB
            with ExitStack() as s:
                rg = s.enter_context(nc.sbuf_tensor(f"{tag}a2rg_{q}", [128, NC * G], F32))
                tmpa = s.enter_context(nc.sbuf_tensor(f"{tag}a2ta_{q}", [128, 2 * G], F32))
                tmpf = s.enter_context(nc.sbuf_tensor(f"{tag}a2tf_{q}", [128, 2 * G], F32))
                scsb = s.enter_context(nc.sbuf_tensor(f"{tag}a2sc_{q}", [8, G], F32))
                tk = s.enter_context(nc.sbuf_tensor(f"{tag}a2tk_{q}", [128, 6 * 8], F32))
                sm = s.enter_context(nc.sbuf_tensor(f"{tag}a2sm_{q}", [128, 48], F32))
                psA = s.enter_context(nc.psum_tensor(f"{tag}a2pA_{q}", [128, 512], F32))
                psB = s.enter_context(nc.psum_tensor(f"{tag}a2pB_{q}", [128, 512], F32))
                psS = s.enter_context(nc.psum_tensor(f"{tag}a2pS_{q}", [128, 512], F32))
                psT = s.enter_context(nc.psum_tensor(f"{tag}a2pT_{q}", [128, 512], F32))
                psW = s.enter_context(nc.psum_tensor(f"{tag}a2pW_{q}", [128, 512], F32))
                for g2 in range(2):
                    tg = t0 + g2 * G
                    if g2 > 0:
                        b.wait(sy, "pe")
                        b.wait(sy, "vec")
                    v = b.inc(sy.dma_start(out=rg.rearrange("p (c t) -> p c t", c=NC),
                                           in_=_rearr_cols(r_res, tg, G)), "ld", 16)
                    if g2 > 0:
                        b.wait(te, "vec")
                    bca = b.inc(te.matmul(psA[:, 0:G], t_ones_row[0:1, :], st_A[0:1, tg:tg + G],
                                          start=True, stop=True), "pe")
                    bcb = b.inc(te.matmul(psB[:, 0:G], t_ones_row[0:1, :], st_B[0:1, tg:tg + G],
                                          start=True, stop=True), "pe")
                    b.wait(ve, "ld", v)
                    b.wait(ve, "pe", bcb)
                    pv_gate = 0
                    for c in range(NC):
                        sl = c % 2
                        ta = tmpa[:, sl * G:(sl + 1) * G]
                        tf = tmpf[:, sl * G:(sl + 1) * G]
                        if c >= 2:
                            b.wait(ve, "pe")
                        b.inc(ve.tensor_mul(ta, rg[:, c * G:(c + 1) * G], psA[:, 0:G]), "vec")
                        fv = b.inc(ve.tensor_add(tf, ta, psB[:, 0:G]), "vec")
                        b.inc(ve.tensor_copy(xn[:, c * QB + g2 * G: c * QB + g2 * G + G], tf), "vec")
                        b.wait(te, "vec", fv)
                        pv_gate = b.inc(te.matmul(psS[0:8, 0:G], gwt[:, c * 8:(c + 1) * 8], tf,
                                                  start=(c == 0), stop=(c == NC - 1)), "pe")
                    b.wait(ve, "pe", pv_gate)
                    sv = b.inc(ve.tensor_copy(scsb[:, :], psS[0:8, 0:G]), "vec")
                    for tc in range(4):
                        b.wait(te, "vec")
                        ptv = b.inc(te.transpose(psT[:, 0:8], scsb[:, tc * 128:(tc + 1) * 128],
                                                 t_id_f[0:8, 0:8]), "pe")
                        b.wait(ve, "pe", ptv)
                        def nvw(instr):
                            vv = b.inc(instr, "vec")
                            b.wait(ve, "vec", vv)
                            return vv
                        toksc, esb = tk[:, 0:8], tk[:, 8:16]
                        scn, sc2 = tk[:, 16:24], tk[:, 24:32]
                        mk1, mk2 = tk[:, 32:40], tk[:, 40:48]
                        t4 = sm[:, 0:4]
                        t2 = sm[:, 4:6]
                        mx2 = sm[:, 8:10]
                        nm2 = sm[:, 12:14]
                        zr2 = sm[:, 16:18]
                        m1_2 = sm[:, 20:22]
                        m2_2 = sm[:, 24:26]
                        den2 = sm[:, 28:30]
                        wr2 = sm[:, 32:34]
                        w1_2 = sm[:, 36:38]
                        w2_2 = sm[:, 40:42]

                        def rowmax(src8, out2):
                            nvw(ve.tensor_max(t4, src8[:, 0:4], src8[:, 4:8]))
                            nvw(ve.tensor_max(t2, t4[:, 0:2], t4[:, 2:4]))
                            nvw(ve.tensor_single_scalar(out2, t2, t2[:, 1:2], ALU.max))

                        def rowsum(src8, out2):
                            nvw(ve.tensor_add(t4, src8[:, 0:4], src8[:, 4:8]))
                            nvw(ve.tensor_add(t2, t4[:, 0:2], t4[:, 2:4]))
                            nvw(ve.tensor_single_scalar(out2, t2, t2[:, 1:2], ALU.add))

                        nvw(ve.tensor_copy(toksc, psT[:, 0:8]))
                        rowmax(toksc, mx2)
                        nv = nvw(ve.tensor_single_scalar(nm2, t_neg2[:], mx2[:, 0:1], ALU.mult))
                        b.wait(se, "vec", nv)
                        av = b.inc(se.activation(esb, toksc, AF.Exp, bias=nm2[:, 0:1]), "act")
                        b.wait(ve, "act", av)
                        rowsum(esb, zr2)
                        nvw(ve.reciprocal(zr2, zr2))
                        nvw(ve.tensor_scalar_mul(scn, esb, zr2[:, 0:1]))
                        rowmax(scn, m1_2)
                        nvw(ve.tensor_single_scalar(mk1, scn, m1_2[:, 0:1], ALU.is_ge))
                        nvw(ve.tensor_scalar(sc2, mk1, -1.0, 1.0, ALU.mult, ALU.add))
                        nvw(ve.tensor_mul(sc2, scn, sc2))
                        rowmax(sc2, m2_2)
                        nvw(ve.tensor_single_scalar(mk2, sc2, m2_2[:, 0:1], ALU.is_ge))
                        nvw(ve.tensor_single_scalar(den2, t_one2[:], m1_2[:, 0:1], ALU.mult))
                        nvw(ve.tensor_single_scalar(den2, den2, m2_2[:, 0:1], ALU.add))
                        nvw(ve.tensor_scalar_add(den2, den2, 1e-20))
                        nvw(ve.reciprocal(wr2, den2))
                        nvw(ve.tensor_scalar_mul(wr2, wr2, 0.5))
                        nvw(ve.tensor_single_scalar(w1_2, wr2, m1_2[:, 0:1], ALU.mult))
                        nvw(ve.tensor_single_scalar(w2_2, wr2, m2_2[:, 0:1], ALU.mult))
                        nvw(ve.tensor_scalar_mul(toksc, mk1, w1_2[:, 0:1]))
                        wv = nvw(ve.scalar_tensor_tensor(toksc, mk2, w2_2[:, 0:1], toksc,
                                                         ALU.mult, ALU.add))
                        b.wait(te, "vec", wv)
                        pwv = b.inc(te.transpose(psW[0:8, 0:128], toksc, t_id_f[:, :]), "pe")
                        b.wait(ve, "pe", pwv)
                        b.inc(ve.tensor_copy(wpeT[:, g2 * G + tc * 128: g2 * G + (tc + 1) * 128],
                                             psW[0:8, 0:128]), "vec")
                b.wait(se, "vec")
                b.wait(te, "vec")

        def _w_col_block(base_ap, e, shared_ap, c0, cols):
            if e < NEXP:
                sl = base_ap[e, :, c0:c0 + cols]
            else:
                sl = shared_ap[:, c0:c0 + cols]
            return sl.rearrange("(c p) d -> p c d", p=128)

        def _moe_experts(tag, q, xn, yac, wpeT, wrow):
            with ExitStack() as s:
                wgb = s.enter_context(nc.sbuf_tensor(f"{tag}wg_{q}", [128, 3 * NC * 128], BF16))
                wub = s.enter_context(nc.sbuf_tensor(f"{tag}wu_{q}", [128, 3 * NC * 128], BF16))
                wdb = s.enter_context(nc.sbuf_tensor(f"{tag}wd_{q}", [128, 3 * 16 * 128], BF16))
                hid = s.enter_context(nc.sbuf_tensor(f"{tag}hid_{q}", [128, 16 * QB], BF16))
                sgb = s.enter_context(nc.sbuf_tensor(f"{tag}sg_{q}", [128, 2 * 512], F32))
                h1b = s.enter_context(nc.sbuf_tensor(f"{tag}h1_{q}", [128, 2 * 512], F32))
                psgu = [s.enter_context(nc.psum_tensor(f"{tag}pgu{i}_{q}", [128, 512], F32)) for i in range(4)]
                psd = [s.enter_context(nc.psum_tensor(f"{tag}pd{i}_{q}", [128, 512], F32)) for i in range(2)]
                pswr = s.enter_context(nc.psum_tensor(f"{tag}pwr_{q}", [128, 512], F32))
                ringGU = Ring(b, f"{tag}rgu_{q}", [p[:, 0:512] for p in psgu])
                ringD = Ring(b, f"{tag}rd_{q}", [p[:, 0:512] for p in psd])
                wrl = 0
                slot_gu = [0, 0, 0]
                slot_d = [0, 0, 0]
                for e in range(NEXP + 1):
                    if e == NEXP:
                        wrl = b.inc(ve.memset(wrow[:, :], 0.5), "vec")
                    else:
                        for g2 in range(2):
                            if e > 0 or g2 > 0:
                                b.wait(te, "vec", wrl)
                            pv = b.inc(te.matmul(pswr[:, 0:G], t_ebc_f[:, e * 128:(e + 1) * 128],
                                                 wpeT[0:8, g2 * G:(g2 + 1) * G], start=True, stop=True), "pe")
                            b.wait(ve, "pe", pv)
                            wrl = b.inc(ve.tensor_copy(wrow[:, g2 * G:(g2 + 1) * G], pswr[:, 0:G]), "vec")
                    wuse = {}
                    hid_last = 0
                    for dc in range(16):
                        sl = dc % 3
                        wg_s = wgb[:, sl * NC * 128:(sl + 1) * NC * 128]
                        wu_s = wub[:, sl * NC * 128:(sl + 1) * NC * 128]
                        if slot_gu[sl]:
                            b.wait(ge, "pe", slot_gu[sl])
                        b.inc(ge.dma_start(out=wg_s.rearrange("p (c d) -> p c d", d=128),
                                           in_=_w_col_block(we_g, e, ws_g, dc * 128, 128)), "wld", 16)
                        lv = b.inc(ge.dma_start(out=wu_s.rearrange("p (c d) -> p c d", d=128),
                                                in_=_w_col_block(we_u, e, ws_u, dc * 128, 128)), "wld", 16)
                        for ts in range(2):
                            pg = ringGU.acquire(te)
                            pu = ringGU.acquire(te)
                            b.wait(te, "wld", lv)
                            pgv = puv = 0
                            for c in range(NC):
                                pgv = b.inc(te.matmul(pg, wg_s[:, c * 128:(c + 1) * 128],
                                                      xn[:, c * QB + ts * 512: c * QB + ts * 512 + 512],
                                                      start=(c == 0), stop=(c == NC - 1)), "pe")
                            for c in range(NC):
                                puv = b.inc(te.matmul(pu, wu_s[:, c * 128:(c + 1) * 128],
                                                      xn[:, c * QB + ts * 512: c * QB + ts * 512 + 512],
                                                      start=(c == 0), stop=(c == NC - 1)), "pe")
                            wuse[dc] = puv
                            slot_gu[sl] = puv
                            sgs = sgb[:, ts * 512:(ts + 1) * 512]
                            h1s = h1b[:, ts * 512:(ts + 1) * 512]
                            b.wait(se, "pe", pgv)
                            b.wait(se, "vec", hid_last)
                            av = b.inc(se.activation(sgs, pg, AF.Silu), "act")
                            b.wait(ve, "act", av)
                            b.wait(ve, "pe", puv)
                            h1i = ve.tensor_mul(h1s, sgs, pu)
                            ringGU.release(h1i, 2)
                            hidi = ve.tensor_mul(hid[:, dc * QB + ts * 512: dc * QB + ts * 512 + 512],
                                                 h1s, wrow[:, ts * 512:(ts + 1) * 512])
                            hv = b.inc(hidi, "vec")
                            hid_last = hv
                    b.wait(te, "vec", hid_last)
                    wduse = {}
                    for o2 in range(NC):
                        sl = o2 % 3
                        wd_s = wdb[:, sl * 16 * 128:(sl + 1) * 16 * 128]
                        if slot_d[sl]:
                            b.wait(ge, "pe", slot_d[sl])
                        lv = b.inc(ge.dma_start(out=wd_s.rearrange("p (c d) -> p c d", d=128),
                                                in_=_w_col_block(we_d, e, ws_d, o2 * 128, 128)), "wld", 16)
                        for ts in range(2):
                            pd = ringD.acquire(te)
                            b.wait(te, "wld", lv)
                            pdv = 0
                            for dc in range(16):
                                pdv = b.inc(te.matmul(pd, wd_s[:, dc * 128:(dc + 1) * 128],
                                                      hid[:, dc * QB + ts * 512: dc * QB + ts * 512 + 512],
                                                      start=(dc == 0), stop=(dc == 15)), "pe")
                            wduse[o2] = pdv
                            slot_d[sl] = pdv
                            b.wait(ve, "pe", pdv)
                            ysl = yac[:, o2 * QB + ts * 512: o2 * QB + ts * 512 + 512]
                            if e == 0:
                                yi = ve.tensor_copy(ysl, pd)
                            else:
                                yi = ve.tensor_add(ysl, ysl, pd)
                            ringD.release(yi)

        def _moe_cadd(tag, q, yac):
            t0 = q * QB
            with ExitStack() as s:
                rg = s.enter_context(nc.sbuf_tensor(f"{tag}crg_{q}", [128, NC * G], F32))
                for g2 in range(2):
                    tg = t0 + g2 * G
                    if g2 > 0:
                        b.wait(sy, "st")
                        b.wait(sy, "vec")
                    v = b.inc(sy.dma_start(out=rg.rearrange("p (c t) -> p c t", c=NC),
                                           in_=_rearr_cols(r_res, tg, G)), "ld", 16)
                    b.wait(ve, "ld", v)
                    vv = 0
                    for c in range(NC):
                        vv = b.inc(ve.tensor_add(rg[:, c * G:(c + 1) * G], rg[:, c * G:(c + 1) * G],
                                                 yac[:, c * QB + g2 * G: c * QB + g2 * G + G]), "vec")
                    b.wait(sy, "vec", vv)
                    b.inc(sy.dma_start(out=_rearr_cols(r_res, tg, G),
                                       in_=rg.rearrange("p (c t) -> p c t", c=NC)), "st", 16)
                b.wait(sy, "st")

        def znorm_init():
            lnstats(r_res, 0, TOK, "zi")
            nc.all_engine_barrier()
            with ExitStack() as s:
                zg = s.enter_context(nc.sbuf_tensor("zi_zg", [128, NC * G], F32))
                zo = s.enter_context(nc.sbuf_tensor("zi_zo", [128, NC * G], F32))
                psA = s.enter_context(nc.psum_tensor("zi_pA", [128, 512], F32))
                psB = s.enter_context(nc.psum_tensor("zi_pB", [128, 512], F32))
                stv = 0
                for g in range(NG):
                    tg = g * G
                    if g > 0:
                        b.wait(sy, "st", stv)
                        b.wait(sy, "vec")
                    v = b.inc(sy.dma_start(out=zg.rearrange("p (c t) -> p c t", c=NC),
                                           in_=_rearr_cols(r_res, tg, G)), "ld", 16)
                    if g > 0:
                        b.wait(te, "vec")
                    ba = b.inc(te.matmul(psA[:, 0:G], t_ones_row[0:1, :], st_A[0:1, tg:tg + G],
                                         start=True, stop=True), "pe")
                    bb2 = b.inc(te.matmul(psB[:, 0:G], t_ones_row[0:1, :], st_B[0:1, tg:tg + G],
                                          start=True, stop=True), "pe")
                    b.wait(ve, "ld", v)
                    b.wait(ve, "pe", bb2)
                    vv = 0
                    for c in range(NC):
                        b.inc(ve.tensor_mul(zo[:, c * G:(c + 1) * G], zg[:, c * G:(c + 1) * G], psA[:, 0:G]), "vec")
                        vv = b.inc(ve.tensor_add(zo[:, c * G:(c + 1) * G], zo[:, c * G:(c + 1) * G], psB[:, 0:G]), "vec")
                    b.wait(sy, "vec", vv)
                    stv = b.inc(sy.dma_start(out=_rearr_cols(z_buf, tg, G),
                                             in_=zo.rearrange("p (c t) -> p c t", c=NC)), "st", 16)
                b.wait(sy, "st")
            nc.all_engine_barrier()

        def _xn_build(zg, xn_g, psA, psB, tg, scr, scaled=True):
            b.inc(te.matmul(psA[:, 0:G], t_ones_row[0:1, :], st_A[0:1, tg:tg + G],
                            start=True, stop=True), "pe")
            bb2 = b.inc(te.matmul(psB[:, 0:G], t_ones_row[0:1, :], st_B[0:1, tg:tg + G],
                                  start=True, stop=True), "pe")
            b.wait(ve, "pe", bb2)
            vv = 0
            for c in range(NC):
                t1 = xn_g[:, c * G:(c + 1) * G]
                b.inc(ve.tensor_mul(scr[:, 0:G], zg[:, c * G:(c + 1) * G], psA[:, 0:G]), "vec")
                b.inc(ve.tensor_add(scr[:, 0:G], scr[:, 0:G], psB[:, 0:G]), "vec")
                if scaled:
                    vv = b.inc(ve.tensor_scalar(t1, scr[:, 0:G], t_lns[:, c:c + 1],
                                                t_lnb[:, c:c + 1], ALU.mult, ALU.add), "vec")
                else:
                    vv = b.inc(ve.tensor_copy(t1, scr[:, 0:G]), "vec")
            return vv

        def ret_layer(li):
            lnstats(z_buf, 0, TOK, f"r{li}s")
            nc.all_engine_barrier()
            b.inc(ve.memset(st_R[:], 0.0), "vec")
            b.inc(ve.memset(st_Rb[:], 0.0), "vec")
            gamC = [float(_gammas64()[h] ** CH) for h in range(H)]

            with ExitStack() as s:
                wq_sb = s.enter_context(nc.sbuf_tensor(f"r{li}_wq", [128, NC * D], BF16))
                wk_sb = s.enter_context(nc.sbuf_tensor(f"r{li}_wk", [128, NC * D], BF16))
                wv_sb = s.enter_context(nc.sbuf_tensor(f"r{li}_wv", [128, NC * D], BF16))
                zg2 = s.enter_context(nc.sbuf_tensor(f"r{li}_zg", [128, 2 * NC * G], F32))
                xn2 = s.enter_context(nc.sbuf_tensor(f"r{li}_xn", [128, NC * G], BF16))
                crg2 = s.enter_context(nc.sbuf_tensor(f"r{li}_cr", [128, G], F32))
                srg2 = s.enter_context(nc.sbuf_tensor(f"r{li}_sr", [128, G], F32))
                rotq = s.enter_context(nc.sbuf_tensor(f"r{li}_rq", [128, H * G], BF16))
                rotk = s.enter_context(nc.sbuf_tensor(f"r{li}_rk", [128, H * G], BF16))
                v_tok = s.enter_context(nc.sbuf_tensor(f"r{li}_vt", [128, 2 * D], BF16))
                y_grp = s.enter_context(nc.sbuf_tensor(f"r{li}_yg", [128, H * G], BF16))
                evt = s.enter_context(nc.sbuf_tensor(f"r{li}_ev", [128, 2 * G], F32))
                evc = s.enter_context(nc.sbuf_tensor(f"r{li}_ec", [128, 2 * G], F32))
                attd_all = s.enter_context(nc.sbuf_tensor(f"r{li}_ad", [128, H * CH], BF16))
                qg_all = s.enter_context(nc.sbuf_tensor(f"r{li}_qg", [128, H * G], BF16))
                kts2 = s.enter_context(nc.sbuf_tensor(f"r{li}_kt", [128, 2 * CH], BF16))
                ba = [s.enter_context(nc.psum_tensor(f"r{li}_ba{i}", [128, 512], F32)) for i in range(2)]
                by = [s.enter_context(nc.psum_tensor(f"r{li}_by{i}", [128, 512], F32)) for i in range(2)]
                bd = [s.enter_context(nc.psum_tensor(f"r{li}_bd{i}", [128, 512], F32)) for i in range(2)]
                psV = s.enter_context(nc.psum_tensor(f"r{li}_pV", [128, 512], F32))
                psKT_b = s.enter_context(nc.psum_tensor(f"r{li}_pKT", [128, 1024], BF16))
                for w_dram, wsb in [(w_q, wq_sb), (w_k, wk_sb), (w_v, wv_sb)]:
                    b.inc(ge.dma_start(out=wsb.rearrange("p (c d) -> p c d", d=D),
                                       in_=w_dram[li].rearrange("(c p) d -> p c d", p=128)), "wld", 16)
                wldv = b.val("wld")
                b.wait(te, "wld", wldv)

                ldv = [0] * NG
                gv_pe = [0] * NG
                gv_vec = [0] * NG
                rbv_prev = [0] * H

                def emit_loads(g):
                    sl = g % 2
                    if g >= 2:
                        b.wait(sy, "pe", gv_pe[g - 2])
                        b.wait(sy, "vec", gv_vec[g - 2])
                    tg = g * G
                    ldv[g] = b.inc(sy.dma_start(out=zg2[:, sl * NC * G:(sl + 1) * NC * G].rearrange("p (c t) -> p c t", c=NC),
                                                in_=_rearr_cols(z_buf, tg, G)), "ld", 16)

                def emit_cs_loads(g):
                    tg = g * G
                    if g > 0:
                        b.wait(sy, "vec")
                    b.inc(sy.dma_start(out=crg2[:, 0:G], in_=crot[:, tg:tg + G]), "ld", 16)
                    csld[0] = b.inc(sy.dma_start(out=srg2[:, 0:G], in_=srot[:, tg:tg + G]), "ld", 16)

                csld = [0]
                emit_loads(0)
                stv_y = 0
                ybank = [0, 0]
                for g in (range(NG) if "scan" in RET_PARTS else []):
                    sl = g % 2
                    tg = g * G
                    zg = zg2[:, sl * NC * G:(sl + 1) * NC * G]
                    xn_g = xn2[:, 0:NC * G]
                    crg = crg2[:, 0:G]
                    srg = srg2[:, 0:G]
                    emit_cs_loads(g)
                    b.wait(ve, "ld", ldv[g])
                    b.wait(ve, "ld", csld[0])
                    b.wait(te, "act")
                    b.wait(te, "vec")
                    _xn_build(zg, xn_g, by[0], by[1], tg, evt)
                    xnv = b.val("vec")
                    b.wait(te, "vec", xnv)
                    for (wsb, rot, sign) in [(wq_sb, rotq, 1.0), (wk_sb, rotk, -1.0)]:
                        evh = [0, 0]
                        rvh = [0, 0]
                        swv = 0
                        for h in range(H):
                            p = h % 2
                            if evh[p]:
                                b.wait(te, "act", evh[p])
                            pv = 0
                            for c in range(NC):
                                pv = b.inc(te.matmul(ba[p][:, 0:G], wsb[:, c * D + h * 128: c * D + (h + 1) * 128],
                                                     xn_g[:, c * G:(c + 1) * G],
                                                     start=(c == 0), stop=(c == NC - 1)), "pe")
                            b.wait(se, "pe", pv)
                            if rvh[p]:
                                b.wait(se, "vec", rvh[p])
                            evh[p] = b.inc(se.copy(evt[:, p * G:(p + 1) * G], ba[p][:, 0:G]), "act")
                            b.wait(ve, "act", evh[p])
                            qb_v = b.inc(ve.tensor_copy(rot[:, h * G:(h + 1) * G], evt[:, p * G:(p + 1) * G]), "vec")
                            b.wait(te, "vec", qb_v)
                            if swv:
                                b.wait(te, "vec", swv)
                            ppv = b.inc(te.matmul(psV[:, 0:G], t_pswap[:], rot[:, h * G:(h + 1) * G],
                                                  start=True, stop=True), "pe")
                            b.wait(ve, "pe", ppv)
                            b.inc(ve.tensor_mul(evc[:, p * G:(p + 1) * G], evt[:, p * G:(p + 1) * G], crg), "vec")
                            swv = b.inc(ve.tensor_mul(evt[:, p * G:(p + 1) * G], psV[:, 0:G], srg), "vec")
                            if sign > 0:
                                rvh[p] = b.inc(ve.tensor_add(rot[:, h * G:(h + 1) * G],
                                                             evc[:, p * G:(p + 1) * G],
                                                             evt[:, p * G:(p + 1) * G]), "vec")
                            else:
                                rvh[p] = b.inc(ve.tensor_sub(rot[:, h * G:(h + 1) * G],
                                                             evc[:, p * G:(p + 1) * G],
                                                             evt[:, p * G:(p + 1) * G]), "vec")
                        b.wait(te, "vec", rvh[0])
                        b.wait(te, "vec", rvh[1])
                    qgv_h = [0] * H
                    for h in range(H):
                        a = rotq[:, h * G:(h + 1) * G].rearrange("p (c t) -> p c t", t=CH)
                        gqb = t_gq[:, h * CH:(h + 1) * CH].rearrange("p (c t) -> p c t", t=CH)
                        a2, b2 = bass.broadcast_tensor_aps(a, gqb)
                        qgv_h[h] = b.inc(ve.tensor_tensor(
                            qg_all[:, h * G:(h + 1) * G].rearrange("p (c t) -> p c t", t=CH),
                            a2, b2, ALU.mult), "vec")
                    ksv_last = 0
                    for tc in range(4):
                        tsl = tc % 2
                        vts = v_tok[:, tsl * D:(tsl + 1) * D]
                        vev = 0
                        for dv in range(2):
                            if dv == 1:
                                b.wait(te, "act", vev)
                            pvv = 0
                            for c in range(NC):
                                pvv = b.inc(te.matmul(psV[:, 0:G], xn_g[:, c * G + tc * 128: c * G + tc * 128 + 128],
                                                      wv_sb[:, c * D + dv * 512: c * D + (dv + 1) * 512],
                                                      start=(c == 0), stop=(c == NC - 1)), "pe")
                            b.wait(se, "pe", pvv)
                            vev = b.inc(se.copy(vts[:, dv * 512:(dv + 1) * 512], psV[:, 0:G]), "act")
                        b.wait(te, "act", vev)
                        adv = [0] * H
                        for h in range(H):
                            p = h % 2
                            rq_c = rotq[:, h * G + tc * 128: h * G + (tc + 1) * 128]
                            rk_c = rotk[:, h * G + tc * 128: h * G + (tc + 1) * 128]
                            if h >= 2:
                                b.wait(te, "vec", adv[h - 2])
                            atv = b.inc(te.matmul(ba[p][:, 0:128], rk_c, rq_c, start=True, stop=True), "pe")
                            b.wait(ve, "pe", atv)
                            adv[h] = b.inc(ve.tensor_mul(attd_all[:, h * CH:(h + 1) * CH], ba[p][:, 0:128],
                                                         t_ddec[:, h * CH:(h + 1) * CH]), "vec")
                        for h in range(H):
                            p = h % 2
                            vt_h = vts[:, h * 128:(h + 1) * 128]
                            if ybank[p]:
                                b.wait(te, "act", ybank[p])
                            b.wait(te, "vec", max(adv[h], qgv_h[h]))
                            b.inc(te.matmul(by[p][:, 0:128], vt_h, attd_all[:, h * CH:(h + 1) * CH],
                                            start=True, stop=False), "pe")
                            yv = b.inc(te.matmul(by[p][:, 0:128], st_Rb[:, h * 128:(h + 1) * 128],
                                                 qg_all[:, h * G + tc * 128: h * G + (tc + 1) * 128],
                                                 start=False, stop=True), "pe")
                            b.wait(se, "pe", yv)
                            ybank[p] = b.inc(se.copy(y_grp[:, h * G + tc * 128: h * G + (tc + 1) * 128],
                                                     by[p][:, 0:128]), "act")
                        ksv = [0, 0]
                        ruv = [0] * H
                        for h in range(H):
                            p = h % 2
                            rk_c = rotk[:, h * G + tc * 128: h * G + (tc + 1) * 128]
                            vt_h = vts[:, h * 128:(h + 1) * 128]
                            if ksv_last:
                                b.wait(te, "vec", ksv_last)
                            ktv = b.inc(te.transpose(psKT_b[:, 0:128], rk_c, t_id_b[:]), "pe")
                            b.wait(ve, "pe", ktv)
                            ksv[p] = b.inc(ve.tensor_scalar_mul(kts2[:, p * CH:(p + 1) * CH],
                                                                psKT_b[:, 0:128], t_kscale[:, h:h + 1]), "vec")
                            ksv_last = ksv[p]
                            if h >= 2:
                                b.wait(te, "vec", ruv[h - 2])
                            b.wait(te, "vec", ksv[p])
                            drv = b.inc(te.matmul(bd[p][:, 0:128], kts2[:, p * CH:(p + 1) * CH], vt_h,
                                                  start=True, stop=True), "pe")
                            b.wait(ve, "pe", drv)
                            b.inc(ve.scalar_tensor_tensor(st_R[:, h * 128:(h + 1) * 128],
                                                          st_R[:, h * 128:(h + 1) * 128], gamC[h],
                                                          bd[p][:, 0:128], ALU.mult, ALU.add), "vec")
                            ruv[h] = b.inc(ve.tensor_copy(st_Rb[:, h * 128:(h + 1) * 128],
                                                          st_R[:, h * 128:(h + 1) * 128]), "vec")
                    gv_pe[g] = b.val("pe")
                    gv_vec[g] = b.val("vec")
                    if g + 1 < NG:
                        emit_loads(g + 1)
                    b.wait(sy, "act")
                    stv_y = b.inc(sy.dma_start(out=_rearr_cols(y_dram, tg, G),
                                               in_=y_grp.rearrange("p (h t) -> p h t", h=H)), "st", 16)
                    b.wait(se, "st", stv_y)
                b.wait(sy, "st")
                nc.all_engine_barrier()

                if "relay" in RET_PARTS:
                    b.inc(ge.dma_start(out=relay_in.ap().rearrange("(q h) v -> q h v", h=H),
                                       in_=st_R.rearrange("p (h v) -> p h v", v=128)), "wld", 16)
                    b.wait(ge, "wld")
                    ccv = b.inc(ge.collective_compute(
                        "AllGather", ALU.bypass, replica_groups=[list(range(8))],
                        ins=[relay_in.ap().opt()], outs=[relay_out.ap().opt()]), "cc", 1)
                    b.inc(ge.memset(rlb_f[:], 0.0), "pool")
                    b.wait(ge, "cc", ccv)
                    gvx = b.inc(ge.indirect_dma_start(
                        out=rlb_f[:, :], out_offset=None,
                        in_=relay_out.ap()[:, :],
                        in_offset=IndirectOffsetOnAxis(t_ridx[:, 0:1], 0),
                        bounds_check=NC * H * DH - 1, oob_is_err=False), "wld", 16)
                    b.wait(ve, "wld", gvx)
                    b.wait(ve, "pool")
                    b.inc(ve.tensor_copy(st_Rrelay[:], rlb_f[:]), "vec")
                    nc.all_engine_barrier()
                else:
                    b.inc(ve.memset(st_Rrelay[:], 0.0), "vec")

                ldv = [0] * NG
                gv_pe = [0] * NG
                gv_vec = [0] * NG
                yldv = [0] * NG
                ystv = [0] * NG

                def emit_loads_c(g):
                    sl = g % 2
                    if g >= 2:
                        b.wait(sy, "pe", gv_pe[g - 2])
                        b.wait(sy, "vec", gv_vec[g - 2])
                    tg = g * G
                    ldv[g] = b.inc(sy.dma_start(out=zg2[:, sl * NC * G:(sl + 1) * NC * G].rearrange("p (c t) -> p c t", c=NC),
                                                in_=_rearr_cols(z_buf, tg, G)), "ld", 16)

                def emit_yload_c(g):
                    yldv[g] = b.inc(sy.dma_start(out=y_grp.rearrange("p (h t) -> p h t", h=H),
                                                 in_=_rearr_cols(y_dram, g * G, G)), "ld", 16)

                emit_loads_c(0)
                emit_yload_c(0)
                for g in (range(NG) if "corr" in RET_PARTS else []):
                    sl = g % 2
                    tg = g * G
                    zg = zg2[:, sl * NC * G:(sl + 1) * NC * G]
                    xn_g = xn2[:, 0:NC * G]
                    crg = crg2[:, 0:G]
                    srg = srg2[:, 0:G]
                    emit_cs_loads(g)
                    b.wait(ve, "ld", ldv[g])
                    b.wait(ve, "ld", csld[0])
                    b.wait(te, "act")
                    b.wait(te, "vec")
                    _xn_build(zg, xn_g, by[0], by[1], tg, evt)
                    xnv = b.val("vec")
                    b.wait(te, "vec", xnv)
                    evh = [0, 0]
                    rvh = [0, 0]
                    swv = 0
                    stt_v = [0] * H
                    b.wait(ve, "ld", yldv[g])
                    for h in range(H):
                        p = h % 2
                        if evh[p]:
                            b.wait(te, "act", evh[p])
                        pv = 0
                        for c in range(NC):
                            pv = b.inc(te.matmul(ba[p][:, 0:G], wq_sb[:, c * D + h * 128: c * D + (h + 1) * 128],
                                                 xn_g[:, c * G:(c + 1) * G],
                                                 start=(c == 0), stop=(c == NC - 1)), "pe")
                        b.wait(se, "pe", pv)
                        if rvh[p]:
                            b.wait(se, "vec", rvh[p])
                        evh[p] = b.inc(se.copy(evt[:, p * G:(p + 1) * G], ba[p][:, 0:G]), "act")
                        b.wait(ve, "act", evh[p])
                        qb_v = b.inc(ve.tensor_copy(rotq[:, h * G:(h + 1) * G], evt[:, p * G:(p + 1) * G]), "vec")
                        b.wait(te, "vec", qb_v)
                        if swv:
                            b.wait(te, "vec", swv)
                        ppv = b.inc(te.matmul(psV[:, 0:G], t_pswap[:], rotq[:, h * G:(h + 1) * G],
                                              start=True, stop=True), "pe")
                        b.wait(ve, "pe", ppv)
                        b.inc(ve.tensor_mul(evc[:, p * G:(p + 1) * G], evt[:, p * G:(p + 1) * G], crg), "vec")
                        swv = b.inc(ve.tensor_mul(evt[:, p * G:(p + 1) * G], psV[:, 0:G], srg), "vec")
                        rvh[p] = b.inc(ve.tensor_add(rotq[:, h * G:(h + 1) * G],
                                                     evc[:, p * G:(p + 1) * G],
                                                     evt[:, p * G:(p + 1) * G]), "vec")
                        a = rotq[:, h * G:(h + 1) * G].rearrange("p (c t) -> p c t", t=CH)
                        gqb = t_gq[:, h * CH:(h + 1) * CH].rearrange("p (c t) -> p c t", t=CH)
                        a2, b2 = bass.broadcast_tensor_aps(a, gqb)
                        qgv = b.inc(ve.tensor_tensor(
                            qg_all[:, h * G:(h + 1) * G].rearrange("p (c t) -> p c t", t=CH),
                            a2, b2, ALU.mult), "vec")
                        b.wait(te, "vec", qgv)
                        sttl = [0, 0]
                        for tc in range(4):
                            pq = tc % 2
                            if sttl[pq]:
                                b.wait(te, "vec", sttl[pq])
                            cpv = b.inc(te.matmul(bd[pq][:, 0:128], st_Rrelay[:, h * 128:(h + 1) * 128],
                                                  qg_all[:, h * G + tc * 128: h * G + (tc + 1) * 128],
                                                  start=True, stop=True), "pe")
                            b.wait(ve, "pe", cpv)
                            gscale = float(_gammas64()[h] ** (128 * (g * 4 + tc)))
                            ysl = y_grp[:, h * G + tc * 128: h * G + (tc + 1) * 128]
                            sttl[pq] = b.inc(ve.scalar_tensor_tensor(ysl, bd[pq][:, 0:128], gscale, ysl,
                                                                     ALU.mult, ALU.add), "vec")
                    gv_pe[g] = b.val("pe")
                    gv_vec[g] = b.val("vec")
                    if g + 1 < NG:
                        emit_loads_c(g + 1)
                    b.wait(sy, "vec")
                    ystv[g] = b.inc(sy.dma_start(out=_rearr_cols(y_dram, tg, G),
                                                 in_=y_grp.rearrange("p (h t) -> p h t", h=H)), "st", 16)
                    if g + 1 < NG:
                        emit_yload_c(g + 1)
                b.wait(sy, "st")
            nc.all_engine_barrier()
            if "post" in RET_PARTS:
                _ret_post(li)
                nc.all_engine_barrier()
            if "ffn" in RET_PARTS:
                _ret_ffn(li)
                nc.all_engine_barrier()

        def _ret_post(li):
            outer = ExitStack()
            st_A8 = outer.enter_context(nc.sbuf_tensor(f"po{li}_A8", [H, TOK], F32))
            st_B8 = outer.enter_context(nc.sbuf_tensor(f"po{li}_B8", [H, TOK], F32))
            with ExitStack() as s:
                y_g = s.enter_context(nc.sbuf_tensor(f"po{li}_yg", [128, H * G], BF16))
                sqb = s.enter_context(nc.sbuf_tensor(f"po{li}_sq", [128, 2 * G], BF16))
                mu8 = s.enter_context(nc.sbuf_tensor(f"po{li}_mu", [8, TOK], F32))
                var8 = s.enter_context(nc.sbuf_tensor(f"po{li}_va", [8, TOK], F32))
                t8 = s.enter_context(nc.sbuf_tensor(f"po{li}_t8", [8, G], F32))
                psS = s.enter_context(nc.psum_tensor(f"po{li}_pS", [128, 512], F32))
                psQ = s.enter_context(nc.psum_tensor(f"po{li}_pQ", [128, 512], F32))
                for g in range(NG):
                    tg = g * G
                    if g > 0:
                        b.wait(sy, "pe")
                    v = b.inc(sy.dma_start(out=y_g.rearrange("p (h t) -> p h t", h=H),
                                           in_=_rearr_cols(y_dram, tg, G)), "ld", 16)
                    b.wait(ve, "ld", v)
                    b.wait(te, "ld", v)
                    if g > 0:
                        b.wait(te, "vec")
                    pv = 0
                    for h in range(H):
                        ysl = y_g[:, h * G:(h + 1) * G]
                        sqs = sqb[:, (h % 2) * G:(h % 2) * G + G]
                        if h >= 2 or g > 0:
                            b.wait(ve, "pe")
                        sv = b.inc(ve.tensor_mul(sqs, ysl, ysl), "vec")
                        b.inc(te.matmul(psS[0:8, 0:G], t_ehs_b[:, h * H:(h + 1) * H], ysl,
                                        start=(h == 0), stop=(h == H - 1)), "pe")
                        b.wait(te, "vec", sv)
                        pv = b.inc(te.matmul(psQ[0:8, 0:G], t_ehs_b[:, h * H:(h + 1) * H], sqs,
                                             start=(h == 0), stop=(h == H - 1)), "pe")
                    b.wait(ve, "pe", pv)
                    b.inc(ve.tensor_scalar_mul(mu8[0:8, tg:tg + G], psS[0:8, 0:G], 1.0 / DH), "vec")
                    b.inc(ve.tensor_mul(t8[0:8, :], mu8[0:8, tg:tg + G], mu8[0:8, tg:tg + G]), "vec")
                    b.inc(ve.tensor_scalar_mul(var8[0:8, tg:tg + G], psQ[0:8, 0:G], 1.0 / DH), "vec")
                    b.inc(ve.tensor_sub(var8[0:8, tg:tg + G], var8[0:8, tg:tg + G], t8[0:8, :]), "vec")
                b.inc(ve.tensor_scalar_add(var8[0:8, :], var8[0:8, :], EPS), "vec")
                rv = b.inc(ve.reciprocal(var8[0:8, :], var8[0:8, :]), "vec")
                b.wait(se, "vec", rv)
                av = b.inc(se.activation(st_A8[0:8, :], var8[0:8, :], AF.Sqrt), "act")
                b.wait(ve, "act", av)
                bv = b.inc(ve.scalar_tensor_tensor(st_B8[0:8, :], mu8[0:8, :], -1.0,
                                                   st_A8[0:8, :], ALU.mult, ALU.mult), "vec")
                b.wait(se, "vec", bv)
                b.wait(te, "vec", bv)
            nc.all_engine_barrier()
            with ExitStack() as s:
                wg_sb = s.enter_context(nc.sbuf_tensor(f"po{li}_wg", [128, NC * D], BF16))
                wo_sb = s.enter_context(nc.sbuf_tensor(f"po{li}_wo", [128, NC * D], BF16))
                zg = s.enter_context(nc.sbuf_tensor(f"po{li}_zg", [128, NC * G], F32))
                xn_g = s.enter_context(nc.sbuf_tensor(f"po{li}_xn", [128, NC * G], BF16))
                y_g = s.enter_context(nc.sbuf_tensor(f"po{li}_yg2", [128, H * G], BF16))
                g_sb = s.enter_context(nc.sbuf_tensor(f"po{li}_gs", [128, NC * G], BF16))
                m_sb = s.enter_context(nc.sbuf_tensor(f"po{li}_ms", [128, NC * G], BF16))
                scr = s.enter_context(nc.sbuf_tensor(f"po{li}_scr", [128, 2 * G], F32))
                psA = s.enter_context(nc.psum_tensor(f"po{li}_pA", [128, 512], F32))
                psB = s.enter_context(nc.psum_tensor(f"po{li}_pB", [128, 512], F32))
                psG = s.enter_context(nc.psum_tensor(f"po{li}_pG", [128, 512], F32))
                psO = s.enter_context(nc.psum_tensor(f"po{li}_pO", [128, 512], F32))
                psbc = s.enter_context(nc.psum_tensor(f"po{li}_pbc", [128, 512], F32))
                psbd = s.enter_context(nc.psum_tensor(f"po{li}_pbd", [128, 512], F32))
                b.inc(ge.dma_start(out=wg_sb.rearrange("p (c d) -> p c d", d=D),
                                   in_=w_g[li].rearrange("(c p) d -> p c d", p=128)), "wld", 16)
                wldv = b.inc(ge.dma_start(out=wo_sb.rearrange("p (c d) -> p c d", d=D),
                                          in_=w_o[li].rearrange("(c p) d -> p c d", p=128)), "wld", 16)
                b.wait(te, "wld", wldv)
                for g in range(NG):
                    tg = g * G
                    if g > 0:
                        b.wait(sy, "pe")
                        b.wait(sy, "vec")
                        b.wait(sy, "st")
                    v = b.inc(sy.dma_start(out=zg.rearrange("p (c t) -> p c t", c=NC),
                                           in_=_rearr_cols(z_buf, tg, G)), "ld", 16)
                    yv = b.inc(sy.dma_start(out=y_g.rearrange("p (h t) -> p h t", h=H),
                                            in_=_rearr_cols(y_dram, tg, G)), "ld", 16)
                    b.wait(ve, "ld", v)
                    _xn_build(zg, xn_g, psA, psB, tg, scr)
                    xnv = b.val("vec")
                    b.wait(te, "vec", xnv)
                    gv = 0
                    for co in range(NC):
                        if gv > 0:
                            b.wait(te, "act", gv)
                        pv = 0
                        for c in range(NC):
                            pv = b.inc(te.matmul(psG[:, 0:G], wg_sb[:, c * D + co * 128: c * D + (co + 1) * 128],
                                                 xn_g[:, c * G:(c + 1) * G],
                                                 start=(c == 0), stop=(c == NC - 1)), "pe")
                        b.wait(se, "pe", pv)
                        b.wait(se, "vec") if co == 0 and g > 0 else None
                        gv = b.inc(se.activation(g_sb[:, co * G:(co + 1) * G], psG[:, 0:G], AF.Silu), "act")
                    b.wait(ve, "ld", yv)
                    for h in range(H):
                        if h > 0 or g > 0:
                            b.wait(te, "vec")
                        bca = b.inc(te.matmul(psbc[:, 0:G], t_ebc_f[:, h * 128:(h + 1) * 128],
                                              st_A8[0:8, tg:tg + G], start=True, stop=True), "pe")
                        bcb = b.inc(te.matmul(psbd[:, 0:G], t_ebc_f[:, h * 128:(h + 1) * 128],
                                              st_B8[0:8, tg:tg + G], start=True, stop=True), "pe")
                        b.wait(ve, "pe", bcb)
                        b.wait(ve, "act", gv)
                        ysl = y_g[:, h * G:(h + 1) * G]
                        b.inc(ve.tensor_mul(scr[:, 0:G], ysl, psbc[:, 0:G]), "vec")
                        b.inc(ve.tensor_add(scr[:, 0:G], scr[:, 0:G], psbd[:, 0:G]), "vec")
                        b.inc(ve.tensor_scalar(scr[:, G:2 * G], scr[:, 0:G],
                                               t_gns[:, li * NC + h: li * NC + h + 1],
                                               t_gnb[:, li * NC + h: li * NC + h + 1],
                                               ALU.mult, ALU.add), "vec")
                        mv = b.inc(ve.tensor_mul(m_sb[:, h * G:(h + 1) * G], scr[:, G:2 * G],
                                                 g_sb[:, h * G:(h + 1) * G]), "vec")
                    b.wait(te, "vec", mv)
                    zv = 0
                    for co in range(NC):
                        if zv > 0:
                            b.wait(te, "vec", zv)
                        pv = 0
                        for c in range(NC):
                            pv = b.inc(te.matmul(psO[:, 0:G], wo_sb[:, c * D + co * 128: c * D + (co + 1) * 128],
                                                 m_sb[:, c * G:(c + 1) * G],
                                                 start=(c == 0), stop=(c == NC - 1)), "pe")
                        b.wait(ve, "pe", pv)
                        zv = b.inc(ve.tensor_add(zg[:, co * G:(co + 1) * G], zg[:, co * G:(co + 1) * G],
                                                 psO[:, 0:G]), "vec")
                    b.wait(sy, "vec", zv)
                    b.inc(sy.dma_start(out=_rearr_cols(z_buf, tg, G),
                                       in_=zg.rearrange("p (c t) -> p c t", c=NC)), "st", 16)
                b.wait(sy, "st")
            outer.close()

        def _ret_ffn(li):
            lnstats(z_buf, 0, TOK, f"rf{li}s")
            nc.all_engine_barrier()
            with ExitStack() as s:
                w1_sb = s.enter_context(nc.sbuf_tensor(f"rf{li}_w1", [128, NC * FF], BF16))
                w2_sb = s.enter_context(nc.sbuf_tensor(f"rf{li}_w2", [128, 16 * D], BF16))
                zg = s.enter_context(nc.sbuf_tensor(f"rf{li}_zg", [128, NC * G], F32))
                xn_g = s.enter_context(nc.sbuf_tensor(f"rf{li}_xn", [128, NC * G], BF16))
                hid = s.enter_context(nc.sbuf_tensor(f"rf{li}_hid", [128, 16 * G], BF16))
                scr = s.enter_context(nc.sbuf_tensor(f"rf{li}_scr", [128, 4 * G], F32))
                psA = s.enter_context(nc.psum_tensor(f"rf{li}_pA", [128, 512], F32))
                psB = s.enter_context(nc.psum_tensor(f"rf{li}_pB", [128, 512], F32))
                psF = [s.enter_context(nc.psum_tensor(f"rf{li}_pF{i}", [128, 512], F32)) for i in range(2)]
                psW = [s.enter_context(nc.psum_tensor(f"rf{li}_pW{i}", [128, 512], F32)) for i in range(2)]
                ringF = Ring(b, f"rf{li}_rF", [p[:, 0:512] for p in psF])
                ringW = Ring(b, f"rf{li}_rW", [p[:, 0:512] for p in psW])
                b.inc(ge.dma_start(out=w1_sb.rearrange("p (c d) -> p c d", d=FF),
                                   in_=w_1[li].rearrange("(c p) d -> p c d", p=128)), "wld", 16)
                wldv = b.inc(ge.dma_start(out=w2_sb.rearrange("p (c d) -> p c d", d=D),
                                          in_=w_2[li].rearrange("(c p) d -> p c d", p=128)), "wld", 16)
                b.wait(te, "wld", wldv)
                c1 = GELU_C
                c1a = GELU_C * GELU_A
                for g in range(NG):
                    tg = g * G
                    if g > 0:
                        b.wait(sy, "pe")
                        b.wait(sy, "vec")
                        b.wait(sy, "st")
                    v = b.inc(sy.dma_start(out=zg.rearrange("p (c t) -> p c t", c=NC),
                                           in_=_rearr_cols(z_buf, tg, G)), "ld", 16)
                    b.wait(ve, "ld", v)
                    _xn_build(zg, xn_g, psA, psB, tg, scr)
                    xnv = b.val("vec")
                    b.wait(te, "vec", xnv)
                    hv = 0
                    for fo in range(16):
                        pf = ringF.acquire(te)
                        pv = 0
                        for c in range(NC):
                            pv = b.inc(te.matmul(pf, w1_sb[:, c * FF + fo * 128: c * FF + (fo + 1) * 128],
                                                 xn_g[:, c * G:(c + 1) * G],
                                                 start=(c == 0), stop=(c == NC - 1)), "pe")
                        b.wait(se, "pe", pv)
                        if hv > 0:
                            b.wait(se, ringF.name, hv)
                        sqv = b.inc(se.activation(scr[:, 0:G], pf, AF.Square), "act")
                        b.wait(ve, "act", sqv)
                        b.inc(ve.tensor_scalar(scr[:, G:2 * G], scr[:, 0:G], c1a, c1, ALU.mult, ALU.add), "vec")
                        uv = b.inc(ve.tensor_mul(scr[:, 2 * G:3 * G], pf, scr[:, G:2 * G]), "vec")
                        b.wait(se, "vec", uv)
                        tv = b.inc(se.activation(scr[:, 3 * G:4 * G], scr[:, 2 * G:3 * G], AF.Tanh), "act")
                        b.wait(ve, "act", tv)
                        b.inc(ve.tensor_scalar(scr[:, G:2 * G], scr[:, 3 * G:4 * G], 0.5, 0.5,
                                               ALU.mult, ALU.add), "vec")
                        hvi = ve.tensor_mul(hid[:, fo * G:(fo + 1) * G], pf, scr[:, G:2 * G])
                        ringF.release(hvi)
                        hv = b.cnt[ringF.name]
                    b.wait(te, ringF.name, hv)
                    for co in range(NC):
                        pw = ringW.acquire(te)
                        pv = 0
                        for fc in range(16):
                            pv = b.inc(te.matmul(pw, w2_sb[:, fc * D + co * 128: fc * D + (co + 1) * 128],
                                                 hid[:, fc * G:(fc + 1) * G],
                                                 start=(fc == 0), stop=(fc == 15)), "pe")
                        b.wait(ve, "pe", pv)
                        zvi = ve.tensor_add(zg[:, co * G:(co + 1) * G], zg[:, co * G:(co + 1) * G], pw)
                        ringW.release(zvi)
                    b.wait(sy, ringW.name)
                    b.inc(sy.dma_start(out=_rearr_cols(z_buf, tg, G),
                                       in_=zg.rearrange("p (c t) -> p c t", c=NC)), "st", 16)
                b.wait(sy, "st")

        def merge_rz():
            with ExitStack() as s:
                rg = s.enter_context(nc.sbuf_tensor("mg_rg", [128, NC * G], F32))
                zg2 = s.enter_context(nc.sbuf_tensor("mg_zg", [128, NC * G], F32))
                for g in range(NG):
                    tg = g * G
                    if g > 0:
                        b.wait(sy, "st")
                        b.wait(sy, "vec")
                    b.inc(sy.dma_start(out=rg.rearrange("p (c t) -> p c t", c=NC),
                                       in_=_rearr_cols(r_res, tg, G)), "ld", 16)
                    v = b.inc(sy.dma_start(out=zg2.rearrange("p (c t) -> p c t", c=NC),
                                           in_=_rearr_cols(z_buf, tg, G)), "ld", 16)
                    b.wait(ve, "ld", v)
                    vv = 0
                    for c in range(NC):
                        vv = b.inc(ve.tensor_add(rg[:, c * G:(c + 1) * G], rg[:, c * G:(c + 1) * G],
                                                 zg2[:, c * G:(c + 1) * G]), "vec")
                    b.wait(sy, "vec", vv)
                    b.inc(sy.dma_start(out=_rearr_cols(r_res, tg, G),
                                       in_=rg.rearrange("p (c t) -> p c t", c=NC)), "st", 16)
                b.wait(sy, "st")
            nc.all_engine_barrier()

        def p4_output():
            with ExitStack() as ph:
                def src(tc, buf):
                    return b.inc(sy.dma_start(
                        out=buf.rearrange("p (c t) -> p c t", c=NC),
                        in_=_rearr_cols(r_res, tc * 128, 128)), "ld", 16)
                def dst(tc, buf):
                    b.inc(sy.dma_start(out=out_t[tc * 128:(tc + 1) * 128, :], in_=buf[:, :]), "st", 16)
                transpose_stream(src, dst, ph, "p4")

        load_tables()
        if "P0" in phases:
            p0_ingest()
        if "FFN1" in phases:
            moe_phase("f1")
        if "RET" in phases:
            znorm_init()
            for li in range(L):
                ret_layer(li)
        if "MERGE" in phases:
            merge_rz()
        if "FFN2" in phases:
            moe_phase("f2")
        if "P4" in phases:
            p4_output()

    return nc, b


def _shard_in_maps(inputs):
    X = np.ascontiguousarray(inputs["L0_attn_X"])
    maps = []
    for core in range(8):
        bat, half = core >> 1, core & 1
        t = host_tables(core)
        m = {
            "x": np.ascontiguousarray(X[bat, half * TOK:(half + 1) * TOK, :]),
            "wq": inputs["L0_attn_Wq"], "wk": inputs["L0_attn_Wk"],
            "wv": inputs["L0_attn_Wv"], "wg": inputs["L0_attn_Wg"],
            "wo": inputs["L0_attn_Wo"], "w1": inputs["L0_attn_W1"],
            "w2": inputs["L0_attn_W2"],
            "ln_s": _feat_tile(inputs["L0_attn_ln_scale"]),
            "ln_b": _feat_tile(inputs["L0_attn_ln_bias"]),
            "gn_s": np.concatenate([_feat_tile(inputs["L0_attn_gn_scale"][i]) for i in range(L)], axis=1),
            "gn_b": np.concatenate([_feat_tile(inputs["L0_attn_gn_bias"][i]) for i in range(L)], axis=1),
            "gate_w": inputs["L0_ffn_gate_w"],
            "we_g": inputs["L0_ffn_w_gate"], "we_u": inputs["L0_ffn_w_up"],
            "we_d": inputs["L0_ffn_w_down"],
            "ws_g": inputs["L0_ffn_ws_gate"], "ws_u": inputs["L0_ffn_ws_up"],
            "ws_d": inputs["L0_ffn_ws_down"],
            "crot": t["crot"], "srot": t["srot"], "ddec": t["ddec"], "gq": t["gq"],
            "kscale": t["kscale"], "id_f": t["id_f"],
            "ones_row_f": t["ones_row_f"], "ones_col_f": t["ones_col_f"],
            "ehs_f": t["ehs_f"], "ebc_f": t["ebc_f"],
            "id_b": t["id_b"], "pswap_b": t["pswap_b"],
            "ones_col_b": t["ones_col_b"], "ehs_b": t["ehs_b"],
            "relay_idx": t["relay_idx"],
        }
        m = {k: (np.ascontiguousarray(v) if isinstance(v, np.ndarray) else np.ascontiguousarray(np.asarray(v))) for k, v in m.items()}
        maps.append(m)
    return maps


def kernel(**inputs):
    inputs = {k: np.asarray(v) for k, v in inputs.items()}
    maps = _shard_in_maps(inputs)
    nc, _ = build_program(ALL_PHASES)
    res = run_bass_kernel_spmd(nc, maps, core_ids=list(range(8)))
    out = np.empty((BAT, S, D), np.float32)
    for core in range(8):
        bat, half = core >> 1, core & 1
        out[bat, half * TOK:(half + 1) * TOK, :] = res.results[core]["out"]
    return out


# version 03
# speedup vs baseline: 1.1316x; 1.0168x over previous
import numpy as np
import ml_dtypes
from contextlib import ExitStack
import concourse.bass as bass
import concourse.mybir as mybir
from concourse.bass_core_rust import IndirectOffsetOnAxis
from concourse.bass_utils import run_bass_kernel_spmd

F32 = mybir.dt.float32
BF16 = mybir.dt.bfloat16
I32 = mybir.dt.int32
AF = mybir.ActivationFunctionType
ALU = mybir.AluOpType
AX = mybir.AxisListType

BAT, S, D, H, DH, FF = 4, 8192, 1024, 8, 128, 2048
NEXP = 8
L = 4
TOK = 4096
NC = 8
G = 512
NG = TOK // G
CH = 128
NCH = TOK // CH
QB = 1024
NQB = TOK // QB
EPS = 1e-5
GELU_C = float(np.sqrt(2.0 / np.pi))
GELU_A = 0.044715

ALL_PHASES = ("P0", "FFN1", "RET", "MERGE", "FFN2", "P4")
DEBUG_WPE = False
RET_PARTS = ("scan", "relay", "corr", "post", "ffn")
DEBUG_SCRATCH = False


def _gammas64():
    return 1.0 - np.exp(np.linspace(np.log(1 / 32), np.log(1 / 512), H))


def host_tables(core_id):
    gam = _gammas64()
    t = {}
    half = core_id & 1
    pos = np.arange(half * TOK, (half + 1) * TOK, dtype=np.float64)
    inv_freq = 1.0 / 10000.0 ** (np.arange(0, DH, 2, dtype=np.float64) / DH)
    ang = pos[:, None] * inv_freq[None, :]
    cos, sin = np.cos(ang), np.sin(ang)
    crot = np.empty((DH, TOK), np.float64)
    srot = np.empty((DH, TOK), np.float64)
    crot[0::2, :] = cos.T
    crot[1::2, :] = cos.T
    srot[0::2, :] = -sin.T
    srot[1::2, :] = sin.T
    t["crot"] = crot.astype(np.float32)
    t["srot"] = srot.astype(np.float32)
    jj, ii = np.meshgrid(np.arange(CH), np.arange(CH), indexing="ij")
    dd = np.zeros((CH, H * CH), np.float64)
    gq = np.zeros((CH, H * CH), np.float64)
    for h in range(H):
        m = (ii - jj).astype(np.float64)
        dd[:, h * CH:(h + 1) * CH] = np.where(m >= 0, gam[h] ** m, 0.0)
        gq[:, h * CH:(h + 1) * CH] = gam[h] ** (ii[0][None, :] + 1.0)
    t["ddec"] = dd.astype(np.float32)
    t["gq"] = gq.astype(np.float32)
    ks = np.zeros((CH, H), np.float64)
    for h in range(H):
        ks[:, h] = gam[h] ** (CH - np.arange(CH, dtype=np.float64))
    t["kscale"] = ks.astype(np.float32)
    eye = np.eye(128, dtype=np.float32)
    t["id_f"] = eye
    t["id_b"] = eye.astype(ml_dtypes.bfloat16)
    pswap = np.zeros((128, 128), np.float32)
    for d in range(128):
        pswap[d ^ 1, d] = 1.0
    t["pswap_b"] = pswap.astype(ml_dtypes.bfloat16)
    ones_col = np.ones((128, 1), np.float32)
    t["ones_col_f"] = ones_col
    t["ones_col_b"] = ones_col.astype(ml_dtypes.bfloat16)
    t["ones_row_f"] = np.ones((1, 128), np.float32)
    ehs = np.zeros((128, H * H), np.float32)
    for h in range(H):
        ehs[:, h * H + h] = 1.0
    t["ehs_b"] = ehs.astype(ml_dtypes.bfloat16)
    t["ehs_f"] = ehs
    ebc = np.zeros((H, H * 128), np.float32)
    for h in range(H):
        ebc[h, h * 128:(h + 1) * 128] = 1.0
    t["ebc_f"] = ebc
    if core_id & 1:
        idx = (np.arange(128, dtype=np.int32) * 8 + (core_id - 1) * 1024).reshape(128, 1)
    else:
        idx = np.full((128, 1), 2 ** 30, dtype=np.int32)
    t["relay_idx"] = idx
    return t


def _feat_tile(vec):
    return np.ascontiguousarray(vec.reshape(NC, 128).T)


class Bld:
    def __init__(self, nc):
        self.nc = nc
        self.sems = {}
        self.cnt = {}

    def sem(self, name):
        if name not in self.sems:
            self.sems[name] = self.nc.alloc_semaphore(name)
            self.cnt[name] = 0
        return self.sems[name]

    def inc(self, instr, name, amt=1):
        instr.then_inc(self.sem(name), amt)
        self.cnt[name] += amt
        return self.cnt[name]

    def val(self, name):
        self.sem(name)
        return self.cnt[name]

    def wait(self, eng, name, v=None):
        if v is None:
            v = self.cnt[name]
        eng.wait_ge(self.sem(name), v)


class Ring:
    def __init__(self, b: Bld, name, tiles):
        self.b = b
        self.name = name
        self.tiles = tiles
        self.n = 0
        b.sem(name)
        self.base = b.cnt[name]

    def acquire(self, eng):
        k = len(self.tiles)
        need = self.n - (k - 1)
        if need > 0:
            self.b.wait(eng, self.name, self.base + need)
        tile = self.tiles[self.n % k]
        self.n += 1
        return tile

    def release(self, instr, n=1):
        self.b.inc(instr, self.name, n)


def _rearr_cols(dram_ap, c0, cols):
    sl = dram_ap[:, c0:c0 + cols]
    return sl.rearrange("(c p) t -> p c t", p=128)


def build_program(phases=ALL_PHASES):
    nc = bass.Bass("TRN2", target_bir_lowering=False, debug=False)
    b = Bld(nc)

    din = {}
    def dram_in(name, shape):
        din[name] = nc.dram_tensor(name, list(shape), F32, kind="ExternalInput").ap()
        return din[name]

    x_in = dram_in("x", (TOK, D))
    w_q = dram_in("wq", (L, D, D))
    w_k = dram_in("wk", (L, D, D))
    w_v = dram_in("wv", (L, D, D))
    w_g = dram_in("wg", (L, D, D))
    w_o = dram_in("wo", (L, D, D))
    w_1 = dram_in("w1", (L, D, FF))
    w_2 = dram_in("w2", (L, FF, D))
    ln_s = dram_in("ln_s", (128, NC))
    ln_b = dram_in("ln_b", (128, NC))
    gn_s = dram_in("gn_s", (128, L * NC))
    gn_b = dram_in("gn_b", (128, L * NC))
    gate_w = dram_in("gate_w", (D, NEXP))
    we_g = dram_in("we_g", (NEXP, D, FF))
    we_u = dram_in("we_u", (NEXP, D, FF))
    we_d = dram_in("we_d", (NEXP, FF, D))
    ws_g = dram_in("ws_g", (D, FF))
    ws_u = dram_in("ws_u", (D, FF))
    ws_d = dram_in("ws_d", (FF, D))
    crot = dram_in("crot", (DH, TOK))
    srot = dram_in("srot", (DH, TOK))
    ddec = dram_in("ddec", (CH, H * CH))
    gq_t = dram_in("gq", (CH, H * CH))
    kscale = dram_in("kscale", (CH, H))
    id_f = dram_in("id_f", (128, 128))
    ones_row_f = dram_in("ones_row_f", (1, 128))
    ehs_f = dram_in("ehs_f", (128, H * H))
    ebc_f = dram_in("ebc_f", (H, H * 128))
    bf16_tabs = {}
    for nm, shape in [("id_b", (128, 128)), ("pswap_b", (128, 128)),
                      ("ones_col_b", (128, 1)), ("ehs_b", (128, H * H))]:
        bf16_tabs[nm] = nc.dram_tensor(nm, list(shape), BF16, kind="ExternalInput").ap()
    ones_col_f = dram_in("ones_col_f", (128, 1))
    relay_idx = nc.dram_tensor("relay_idx", [128, 1], I32, kind="ExternalInput").ap()

    out_t = nc.dram_tensor("out", [TOK, D], F32, kind="ExternalOutput").ap()
    dbg_t = nc.dram_tensor("dbg", [8, TOK], F32, kind="ExternalOutput").ap() if DEBUG_WPE else None
    dbg2_t = nc.dram_tensor("dbg2", [128, 64], F32, kind="ExternalOutput").ap() if DEBUG_WPE else None

    _scr_kind = "ExternalOutput" if DEBUG_SCRATCH else "Internal"
    r_res = nc.dram_tensor("r_res", [D, TOK], F32, kind=_scr_kind).ap()
    z_buf = nc.dram_tensor("z_buf", [D, TOK], F32, kind=_scr_kind).ap()
    y_dram = nc.dram_tensor("y_dram", [D, TOK], BF16, kind=_scr_kind).ap()
    qg_dram = nc.dram_tensor("qg_dram", [D, TOK], BF16).ap()
    relay_in = nc.dram_tensor("relay_in", [H * DH, DH], F32)
    relay_out = nc.dram_tensor("relay_out", [NC * H * DH, DH], F32)

    with ExitStack() as ctx:
        sb = lambda name, shape, dt: ctx.enter_context(nc.sbuf_tensor(name, shape, dt))
        psb = lambda name, shape, dt=F32: ctx.enter_context(nc.psum_tensor(name, shape, dt))

        te, ve, se, ge, sy = nc.tensor, nc.vector, nc.scalar, nc.gpsimd, nc.sync

        t_id_f = sb("t_id_f", [128, 128], F32)
        t_id_b = sb("t_id_b", [128, 128], BF16)
        t_pswap = sb("t_pswap", [128, 128], BF16)
        t_ones_row = sb("t_ones_row", [1, 128], F32)
        t_ones_col_f = sb("t_ones_col_f", [128, 1], F32)
        t_ones_col_b = sb("t_ones_col_b", [128, 1], BF16)
        t_ehs_b = sb("t_ehs_b", [128, H * H], BF16)
        t_ehs_f = sb("t_ehs_f", [128, H * H], F32)
        t_ebc_f = sb("t_ebc_f", [H, H * 128], F32)
        t_ddec = sb("t_ddec", [CH, H * CH], F32)
        t_gq = sb("t_gq", [CH, H * CH], F32)
        t_kscale = sb("t_kscale", [CH, H], F32)
        t_lns = sb("t_lns", [128, NC], F32)
        t_lnb = sb("t_lnb", [128, NC], F32)
        t_gns = sb("t_gns", [128, L * NC], F32)
        t_gnb = sb("t_gnb", [128, L * NC], F32)
        t_ridx = sb("t_ridx", [128, 1], I32)
        st_A = sb("st_A", [1, TOK], F32)
        st_B = sb("st_B", [1, TOK], F32)
        st_R = sb("st_R", [128, H * DH], F32)
        st_Rb = sb("st_Rb", [128, H * DH], BF16)
        st_Rrelay = sb("st_Rrelay", [128, H * DH], BF16)
        rlb_f = sb("rlb_f", [128, H * DH], F32)

        consts = sb("consts", [128, 2], F32)
        nc.const_aps.register(F32, 0.5, consts[:, 0:1])
        nc.const_aps.register(F32, 2.0, consts[:, 1:2])
        t_one2 = sb("t_one2", [128, 2], F32)
        t_neg2 = sb("t_neg2", [128, 2], F32)

        def load_tables():
            ld = [
                (t_id_f, id_f), (t_ones_row, ones_row_f), (t_ones_col_f, ones_col_f),
                (t_ehs_f, ehs_f), (t_ebc_f, ebc_f), (t_ddec, ddec), (t_gq, gq_t),
                (t_kscale, kscale), (t_lns, ln_s), (t_lnb, ln_b), (t_gns, gn_s),
                (t_gnb, gn_b),
                (t_id_b, bf16_tabs["id_b"]), (t_pswap, bf16_tabs["pswap_b"]),
                (t_ones_col_b, bf16_tabs["ones_col_b"]), (t_ehs_b, bf16_tabs["ehs_b"]),
                (t_ridx, relay_idx),
            ]
            for dst, src in ld:
                b.inc(sy.dma_start(out=dst[:], in_=src[:, :]), "ld", 16)
            ge.memset(consts[:, 0:1], 0.5)
            ge.memset(t_one2[:], 1.0)
            ge.memset(t_neg2[:], -1.0)
            b.inc(ge.memset(consts[:, 1:2], 2.0), "pool", 1)
            b.wait(sy, "ld")
            b.wait(ge, "pool")
            nc.all_engine_barrier()

        def transpose_stream(src_load, dst_store, ph, tag):
            xin = ph.enter_context(nc.sbuf_tensor(f"{tag}_xin", [128, 2 * D], F32))
            xout = ph.enter_context(nc.sbuf_tensor(f"{tag}_xout", [128, 2 * D], F32))
            tps = [ph.enter_context(nc.psum_tensor(f"{tag}_tp{i}", [128, 512], F32)) for i in range(4)]
            ring = Ring(b, f"{tag}_r", [tp[:, 0:128] for tp in tps])
            stv = []
            for tc in range(NCH):
                par = tc % 2
                xin_s = xin[:, par * D:(par + 1) * D]
                xout_s = xout[:, par * D:(par + 1) * D]
                if tc >= 2:
                    b.wait(sy, "st", stv[tc - 2])
                v = src_load(tc, xin_s)
                b.wait(te, "ld", v)
                for fc in range(NC):
                    ps = ring.acquire(te)
                    pv = b.inc(te.transpose(ps, xin_s[:, fc * 128:(fc + 1) * 128], t_id_f[:]), "pe")
                    b.wait(ve, "pe", pv)
                    if fc == 0 and tc >= 2:
                        b.wait(ve, "st", stv[tc - 2])
                    ring.release(ve.tensor_copy(xout_s[:, fc * 128:(fc + 1) * 128], ps))
                b.wait(sy, ring.name)
                dst_store(tc, xout_s)
                stv.append(b.cnt["st"])
            b.wait(sy, "st")

        def p0_ingest():
            with ExitStack() as ph:
                def src(tc, buf):
                    return b.inc(sy.dma_start(out=buf[:, :], in_=x_in[tc * 128:(tc + 1) * 128, :]), "ld", 16)
                def dst(tc, buf):
                    b.inc(sy.dma_start(
                        out=_rearr_cols(r_res, tc * 128, 128),
                        in_=buf.rearrange("p (c t) -> p c t", c=NC)), "st", 16)
                transpose_stream(src, dst, ph, "p0")
            nc.all_engine_barrier()

        def lnstats(src_dram, t0, ntok, tag):
            ngr = ntok // G
            with ExitStack() as s:
                rg = s.enter_context(nc.sbuf_tensor(f"{tag}_rg", [128, 2 * NC * G], F32))
                sq = s.enter_context(nc.sbuf_tensor(f"{tag}_sq", [128, 2 * G], F32))
                mu = s.enter_context(nc.sbuf_tensor(f"{tag}_mu", [1, ntok], F32))
                var = s.enter_context(nc.sbuf_tensor(f"{tag}_var", [1, ntok], F32))
                tmp = s.enter_context(nc.sbuf_tensor(f"{tag}_tmp", [1, ntok], F32))
                psS = s.enter_context(nc.psum_tensor(f"{tag}_psS", [128, 512], F32))
                psQ = s.enter_context(nc.psum_tensor(f"{tag}_psQ", [128, 512], F32))
                pev = [0] * ngr
                vev = [0] * ngr
                for g in range(ngr):
                    tg = t0 + g * G
                    par = g % 2
                    rgs = rg[:, par * NC * G:(par + 1) * NC * G]
                    if g >= 2:
                        b.wait(sy, "pe", pev[g - 2])
                    v = b.inc(sy.dma_start(out=rgs.rearrange("p (c t) -> p c t", c=NC),
                                           in_=_rearr_cols(src_dram, tg, G)), "ld", 16)
                    b.wait(ve, "ld", v)
                    b.wait(te, "ld", v)
                    if g >= 1:
                        b.wait(te, "vec", vev[g - 1])
                    pv = 0
                    for c in range(NC):
                        rc = rgs[:, c * G:(c + 1) * G]
                        sqs = sq[:, (c % 2) * G:(c % 2) * G + G]
                        if c >= 2 or g > 0:
                            b.wait(ve, "pe")
                        vv = b.inc(ve.tensor_mul(sqs, rc, rc), "vec")
                        b.inc(te.matmul(psS[0:1, :], t_ones_col_f[:], rc,
                                        start=(c == 0), stop=(c == NC - 1)), "pe")
                        b.wait(te, "vec", vv)
                        pv = b.inc(te.matmul(psQ[0:1, :], t_ones_col_f[:], sqs,
                                             start=(c == 0), stop=(c == NC - 1)), "pe")
                    pev[g] = pv
                    b.wait(ve, "pe", pv)
                    o = tg - t0
                    b.inc(ve.tensor_scalar_mul(mu[0:1, o:o + G], psS[0:1, :], 1.0 / D), "vec")
                    b.inc(ve.tensor_mul(tmp[0:1, o:o + G], mu[0:1, o:o + G], mu[0:1, o:o + G]), "vec")
                    b.inc(ve.tensor_scalar_mul(var[0:1, o:o + G], psQ[0:1, :], 1.0 / D), "vec")
                    vev[g] = b.inc(ve.tensor_sub(var[0:1, o:o + G], var[0:1, o:o + G], tmp[0:1, o:o + G]), "vec")
                b.inc(ve.tensor_scalar_add(var[0:1, 0:ntok], var[0:1, 0:ntok], EPS), "vec")
                vv = b.inc(ve.reciprocal(tmp[0:1, 0:ntok], var[0:1, 0:ntok]), "vec")
                b.wait(se, "vec", vv)
                av = b.inc(se.activation(st_A[0:1, t0:t0 + ntok], tmp[0:1, 0:ntok], AF.Sqrt), "act")
                b.wait(ve, "act", av)
                bv = b.inc(ve.scalar_tensor_tensor(st_B[0:1, t0:t0 + ntok], mu[0:1, 0:ntok], -1.0,
                                                   st_A[0:1, t0:t0 + ntok], ALU.mult, ALU.mult), "vec")
                b.wait(se, "vec", bv)
            return bv

        def moe_phase(tag):
            with ExitStack() as ph:
                xn = ph.enter_context(nc.sbuf_tensor(f"{tag}_xn", [128, NC * QB], BF16))
                yac = ph.enter_context(nc.sbuf_tensor(f"{tag}_yac", [128, NC * QB], F32))
                wpeT = ph.enter_context(nc.sbuf_tensor(f"{tag}_wpeT", [8, QB], F32))
                gwt = ph.enter_context(nc.sbuf_tensor(f"{tag}_gwt", [128, NC * 8], F32))
                wrow = ph.enter_context(nc.sbuf_tensor(f"{tag}_wrow", [128, QB], BF16))
                v = b.inc(sy.dma_start(out=gwt.rearrange("p (c e) -> p c e", c=NC),
                                       in_=gate_w.rearrange("(c p) e -> p c e", p=128)), "ld", 16)
                b.wait(te, "ld", v)
                for q in range(NQB):
                    t0 = q * QB
                    lnstats(r_res, t0, QB, f"{tag}q{q}s")
                    nc.all_engine_barrier()
                    _moe_a2(tag, q, xn, wpeT, gwt)
                    nc.all_engine_barrier()
                    if DEBUG_WPE and tag == "f1":
                        b.inc(sy.dma_start(out=dbg_t[:, t0:t0 + QB], in_=wpeT[:, :]), "st", 16)
                        b.wait(sy, "st")
                    _moe_experts(tag, q, xn, yac, wpeT, wrow)
                    nc.all_engine_barrier()
                    _moe_cadd(tag, q, yac)
                    nc.all_engine_barrier()

        def _moe_a2(tag, q, xn, wpeT, gwt):
            t0 = q * QB
            with ExitStack() as s:
                rg = s.enter_context(nc.sbuf_tensor(f"{tag}a2rg_{q}", [128, NC * G], F32))
                tmpa = s.enter_context(nc.sbuf_tensor(f"{tag}a2ta_{q}", [128, 2 * G], F32))
                tmpf = s.enter_context(nc.sbuf_tensor(f"{tag}a2tf_{q}", [128, 2 * G], F32))
                scsb = s.enter_context(nc.sbuf_tensor(f"{tag}a2sc_{q}", [8, G], F32))
                tk = s.enter_context(nc.sbuf_tensor(f"{tag}a2tk_{q}", [128, 6 * 8], F32))
                sm = s.enter_context(nc.sbuf_tensor(f"{tag}a2sm_{q}", [128, 48], F32))
                psA = s.enter_context(nc.psum_tensor(f"{tag}a2pA_{q}", [128, 512], F32))
                psB = s.enter_context(nc.psum_tensor(f"{tag}a2pB_{q}", [128, 512], F32))
                psS = s.enter_context(nc.psum_tensor(f"{tag}a2pS_{q}", [128, 512], F32))
                psT = s.enter_context(nc.psum_tensor(f"{tag}a2pT_{q}", [128, 512], F32))
                psW = s.enter_context(nc.psum_tensor(f"{tag}a2pW_{q}", [128, 512], F32))
                for g2 in range(2):
                    tg = t0 + g2 * G
                    if g2 > 0:
                        b.wait(sy, "pe")
                        b.wait(sy, "vec")
                    v = b.inc(sy.dma_start(out=rg.rearrange("p (c t) -> p c t", c=NC),
                                           in_=_rearr_cols(r_res, tg, G)), "ld", 16)
                    if g2 > 0:
                        b.wait(te, "vec")
                    bca = b.inc(te.matmul(psA[:, 0:G], t_ones_row[0:1, :], st_A[0:1, tg:tg + G],
                                          start=True, stop=True), "pe")
                    bcb = b.inc(te.matmul(psB[:, 0:G], t_ones_row[0:1, :], st_B[0:1, tg:tg + G],
                                          start=True, stop=True), "pe")
                    b.wait(ve, "ld", v)
                    b.wait(ve, "pe", bcb)
                    pv_gate = 0
                    for c in range(NC):
                        sl = c % 2
                        ta = tmpa[:, sl * G:(sl + 1) * G]
                        tf = tmpf[:, sl * G:(sl + 1) * G]
                        if c >= 2:
                            b.wait(ve, "pe")
                        b.inc(ve.tensor_mul(ta, rg[:, c * G:(c + 1) * G], psA[:, 0:G]), "vec")
                        fv = b.inc(ve.tensor_add(tf, ta, psB[:, 0:G]), "vec")
                        b.inc(ve.tensor_copy(xn[:, c * QB + g2 * G: c * QB + g2 * G + G], tf), "vec")
                        b.wait(te, "vec", fv)
                        pv_gate = b.inc(te.matmul(psS[0:8, 0:G], gwt[:, c * 8:(c + 1) * 8], tf,
                                                  start=(c == 0), stop=(c == NC - 1)), "pe")
                    b.wait(ve, "pe", pv_gate)
                    sv = b.inc(ve.tensor_copy(scsb[:, :], psS[0:8, 0:G]), "vec")
                    for tc in range(4):
                        b.wait(te, "vec")
                        ptv = b.inc(te.transpose(psT[:, 0:8], scsb[:, tc * 128:(tc + 1) * 128],
                                                 t_id_f[0:8, 0:8]), "pe")
                        b.wait(ve, "pe", ptv)
                        def nvw(instr):
                            vv = b.inc(instr, "vec")
                            b.wait(ve, "vec", vv)
                            return vv
                        toksc, esb = tk[:, 0:8], tk[:, 8:16]
                        scn, sc2 = tk[:, 16:24], tk[:, 24:32]
                        mk1, mk2 = tk[:, 32:40], tk[:, 40:48]
                        t4 = sm[:, 0:4]
                        t2 = sm[:, 4:6]
                        mx2 = sm[:, 8:10]
                        nm2 = sm[:, 12:14]
                        zr2 = sm[:, 16:18]
                        m1_2 = sm[:, 20:22]
                        m2_2 = sm[:, 24:26]
                        den2 = sm[:, 28:30]
                        wr2 = sm[:, 32:34]
                        w1_2 = sm[:, 36:38]
                        w2_2 = sm[:, 40:42]

                        def rowmax(src8, out2):
                            nvw(ve.tensor_max(t4, src8[:, 0:4], src8[:, 4:8]))
                            nvw(ve.tensor_max(t2, t4[:, 0:2], t4[:, 2:4]))
                            nvw(ve.tensor_single_scalar(out2, t2, t2[:, 1:2], ALU.max))

                        def rowsum(src8, out2):
                            nvw(ve.tensor_add(t4, src8[:, 0:4], src8[:, 4:8]))
                            nvw(ve.tensor_add(t2, t4[:, 0:2], t4[:, 2:4]))
                            nvw(ve.tensor_single_scalar(out2, t2, t2[:, 1:2], ALU.add))

                        nvw(ve.tensor_copy(toksc, psT[:, 0:8]))
                        rowmax(toksc, mx2)
                        nv = nvw(ve.tensor_single_scalar(nm2, t_neg2[:], mx2[:, 0:1], ALU.mult))
                        b.wait(se, "vec", nv)
                        av = b.inc(se.activation(esb, toksc, AF.Exp, bias=nm2[:, 0:1]), "act")
                        b.wait(ve, "act", av)
                        rowsum(esb, zr2)
                        nvw(ve.reciprocal(zr2, zr2))
                        nvw(ve.tensor_scalar_mul(scn, esb, zr2[:, 0:1]))
                        rowmax(scn, m1_2)
                        nvw(ve.tensor_single_scalar(mk1, scn, m1_2[:, 0:1], ALU.is_ge))
                        nvw(ve.tensor_scalar(sc2, mk1, -1.0, 1.0, ALU.mult, ALU.add))
                        nvw(ve.tensor_mul(sc2, scn, sc2))
                        rowmax(sc2, m2_2)
                        nvw(ve.tensor_single_scalar(mk2, sc2, m2_2[:, 0:1], ALU.is_ge))
                        nvw(ve.tensor_single_scalar(den2, t_one2[:], m1_2[:, 0:1], ALU.mult))
                        nvw(ve.tensor_single_scalar(den2, den2, m2_2[:, 0:1], ALU.add))
                        nvw(ve.tensor_scalar_add(den2, den2, 1e-20))
                        nvw(ve.reciprocal(wr2, den2))
                        nvw(ve.tensor_scalar_mul(wr2, wr2, 0.5))
                        nvw(ve.tensor_single_scalar(w1_2, wr2, m1_2[:, 0:1], ALU.mult))
                        nvw(ve.tensor_single_scalar(w2_2, wr2, m2_2[:, 0:1], ALU.mult))
                        nvw(ve.tensor_scalar_mul(toksc, mk1, w1_2[:, 0:1]))
                        wv = nvw(ve.scalar_tensor_tensor(toksc, mk2, w2_2[:, 0:1], toksc,
                                                         ALU.mult, ALU.add))
                        b.wait(te, "vec", wv)
                        pwv = b.inc(te.transpose(psW[0:8, 0:128], toksc, t_id_f[:, :]), "pe")
                        b.wait(ve, "pe", pwv)
                        b.inc(ve.tensor_copy(wpeT[:, g2 * G + tc * 128: g2 * G + (tc + 1) * 128],
                                             psW[0:8, 0:128]), "vec")
                b.wait(se, "vec")
                b.wait(te, "vec")

        def _w_col_block(base_ap, e, shared_ap, c0, cols):
            if e < NEXP:
                sl = base_ap[e, :, c0:c0 + cols]
            else:
                sl = shared_ap[:, c0:c0 + cols]
            return sl.rearrange("(c p) d -> p c d", p=128)

        def _moe_experts(tag, q, xn, yac, wpeT, wrow):
            with ExitStack() as s:
                wgb = s.enter_context(nc.sbuf_tensor(f"{tag}wg_{q}", [128, 3 * NC * 128], BF16))
                wub = s.enter_context(nc.sbuf_tensor(f"{tag}wu_{q}", [128, 3 * NC * 128], BF16))
                wdb = s.enter_context(nc.sbuf_tensor(f"{tag}wd_{q}", [128, 3 * 16 * 128], BF16))
                hid = s.enter_context(nc.sbuf_tensor(f"{tag}hid_{q}", [128, 16 * QB], BF16))
                sgb = s.enter_context(nc.sbuf_tensor(f"{tag}sg_{q}", [128, 2 * 512], F32))
                h1b = s.enter_context(nc.sbuf_tensor(f"{tag}h1_{q}", [128, 2 * 512], F32))
                psgu = [s.enter_context(nc.psum_tensor(f"{tag}pgu{i}_{q}", [128, 512], F32)) for i in range(4)]
                psd = [s.enter_context(nc.psum_tensor(f"{tag}pd{i}_{q}", [128, 512], F32)) for i in range(2)]
                pswr = s.enter_context(nc.psum_tensor(f"{tag}pwr_{q}", [128, 512], F32))
                ringGU = Ring(b, f"{tag}rgu_{q}", [p[:, 0:512] for p in psgu])
                ringD = Ring(b, f"{tag}rd_{q}", [p[:, 0:512] for p in psd])
                wrl = 0
                slot_gu = [0, 0, 0]
                slot_d = [0, 0, 0]
                for e in range(NEXP + 1):
                    if e == NEXP:
                        wrl = b.inc(ve.memset(wrow[:, :], 0.5), "vec")
                    else:
                        for g2 in range(2):
                            if e > 0 or g2 > 0:
                                b.wait(te, "vec", wrl)
                            pv = b.inc(te.matmul(pswr[:, 0:G], t_ebc_f[:, e * 128:(e + 1) * 128],
                                                 wpeT[0:8, g2 * G:(g2 + 1) * G], start=True, stop=True), "pe")
                            b.wait(ve, "pe", pv)
                            wrl = b.inc(ve.tensor_copy(wrow[:, g2 * G:(g2 + 1) * G], pswr[:, 0:G]), "vec")
                    wuse = {}
                    hid_last = 0
                    for dc in range(16):
                        sl = dc % 3
                        wg_s = wgb[:, sl * NC * 128:(sl + 1) * NC * 128]
                        wu_s = wub[:, sl * NC * 128:(sl + 1) * NC * 128]
                        if slot_gu[sl]:
                            b.wait(ge, "pe", slot_gu[sl])
                        b.inc(ge.dma_start(out=wg_s.rearrange("p (c d) -> p c d", d=128),
                                           in_=_w_col_block(we_g, e, ws_g, dc * 128, 128)), "wld", 16)
                        lv = b.inc(ge.dma_start(out=wu_s.rearrange("p (c d) -> p c d", d=128),
                                                in_=_w_col_block(we_u, e, ws_u, dc * 128, 128)), "wld", 16)
                        for ts in range(2):
                            pg = ringGU.acquire(te)
                            pu = ringGU.acquire(te)
                            b.wait(te, "wld", lv)
                            pgv = puv = 0
                            for c in range(NC):
                                pgv = b.inc(te.matmul(pg, wg_s[:, c * 128:(c + 1) * 128],
                                                      xn[:, c * QB + ts * 512: c * QB + ts * 512 + 512],
                                                      start=(c == 0), stop=(c == NC - 1)), "pe")
                            for c in range(NC):
                                puv = b.inc(te.matmul(pu, wu_s[:, c * 128:(c + 1) * 128],
                                                      xn[:, c * QB + ts * 512: c * QB + ts * 512 + 512],
                                                      start=(c == 0), stop=(c == NC - 1)), "pe")
                            wuse[dc] = puv
                            slot_gu[sl] = puv
                            sgs = sgb[:, ts * 512:(ts + 1) * 512]
                            h1s = h1b[:, ts * 512:(ts + 1) * 512]
                            b.wait(se, "pe", pgv)
                            b.wait(se, "vec", hid_last)
                            av = b.inc(se.activation(sgs, pg, AF.Silu), "act")
                            b.wait(ve, "act", av)
                            b.wait(ve, "pe", puv)
                            h1i = ve.tensor_mul(h1s, sgs, pu)
                            ringGU.release(h1i, 2)
                            hidi = ve.tensor_mul(hid[:, dc * QB + ts * 512: dc * QB + ts * 512 + 512],
                                                 h1s, wrow[:, ts * 512:(ts + 1) * 512])
                            hv = b.inc(hidi, "vec")
                            hid_last = hv
                    b.wait(te, "vec", hid_last)
                    wduse = {}
                    for o2 in range(NC):
                        sl = o2 % 3
                        wd_s = wdb[:, sl * 16 * 128:(sl + 1) * 16 * 128]
                        if slot_d[sl]:
                            b.wait(ge, "pe", slot_d[sl])
                        lv = b.inc(ge.dma_start(out=wd_s.rearrange("p (c d) -> p c d", d=128),
                                                in_=_w_col_block(we_d, e, ws_d, o2 * 128, 128)), "wld", 16)
                        for ts in range(2):
                            pd = ringD.acquire(te)
                            b.wait(te, "wld", lv)
                            pdv = 0
                            for dc in range(16):
                                pdv = b.inc(te.matmul(pd, wd_s[:, dc * 128:(dc + 1) * 128],
                                                      hid[:, dc * QB + ts * 512: dc * QB + ts * 512 + 512],
                                                      start=(dc == 0), stop=(dc == 15)), "pe")
                            wduse[o2] = pdv
                            slot_d[sl] = pdv
                            b.wait(ve, "pe", pdv)
                            ysl = yac[:, o2 * QB + ts * 512: o2 * QB + ts * 512 + 512]
                            if e == 0:
                                yi = ve.tensor_copy(ysl, pd)
                            else:
                                yi = ve.tensor_add(ysl, ysl, pd)
                            ringD.release(yi)

        def _moe_cadd(tag, q, yac):
            t0 = q * QB
            with ExitStack() as s:
                rg = s.enter_context(nc.sbuf_tensor(f"{tag}crg_{q}", [128, NC * G], F32))
                for g2 in range(2):
                    tg = t0 + g2 * G
                    if g2 > 0:
                        b.wait(sy, "st")
                        b.wait(sy, "vec")
                    v = b.inc(sy.dma_start(out=rg.rearrange("p (c t) -> p c t", c=NC),
                                           in_=_rearr_cols(r_res, tg, G)), "ld", 16)
                    b.wait(ve, "ld", v)
                    vv = 0
                    for c in range(NC):
                        vv = b.inc(ve.tensor_add(rg[:, c * G:(c + 1) * G], rg[:, c * G:(c + 1) * G],
                                                 yac[:, c * QB + g2 * G: c * QB + g2 * G + G]), "vec")
                    b.wait(sy, "vec", vv)
                    b.inc(sy.dma_start(out=_rearr_cols(r_res, tg, G),
                                       in_=rg.rearrange("p (c t) -> p c t", c=NC)), "st", 16)
                b.wait(sy, "st")

        def znorm_init():
            lnstats(r_res, 0, TOK, "zi")
            nc.all_engine_barrier()
            with ExitStack() as s:
                zg = s.enter_context(nc.sbuf_tensor("zi_zg", [128, NC * G], F32))
                zo = s.enter_context(nc.sbuf_tensor("zi_zo", [128, NC * G], F32))
                psA = s.enter_context(nc.psum_tensor("zi_pA", [128, 512], F32))
                psB = s.enter_context(nc.psum_tensor("zi_pB", [128, 512], F32))
                stv = 0
                for g in range(NG):
                    tg = g * G
                    if g > 0:
                        b.wait(sy, "st", stv)
                        b.wait(sy, "vec")
                    v = b.inc(sy.dma_start(out=zg.rearrange("p (c t) -> p c t", c=NC),
                                           in_=_rearr_cols(r_res, tg, G)), "ld", 16)
                    if g > 0:
                        b.wait(te, "vec")
                    ba = b.inc(te.matmul(psA[:, 0:G], t_ones_row[0:1, :], st_A[0:1, tg:tg + G],
                                         start=True, stop=True), "pe")
                    bb2 = b.inc(te.matmul(psB[:, 0:G], t_ones_row[0:1, :], st_B[0:1, tg:tg + G],
                                          start=True, stop=True), "pe")
                    b.wait(ve, "ld", v)
                    b.wait(ve, "pe", bb2)
                    vv = 0
                    for c in range(NC):
                        b.inc(ve.tensor_mul(zo[:, c * G:(c + 1) * G], zg[:, c * G:(c + 1) * G], psA[:, 0:G]), "vec")
                        vv = b.inc(ve.tensor_add(zo[:, c * G:(c + 1) * G], zo[:, c * G:(c + 1) * G], psB[:, 0:G]), "vec")
                    b.wait(sy, "vec", vv)
                    stv = b.inc(sy.dma_start(out=_rearr_cols(z_buf, tg, G),
                                             in_=zo.rearrange("p (c t) -> p c t", c=NC)), "st", 16)
                b.wait(sy, "st")
            nc.all_engine_barrier()

        def _xn_build(zg, xn_g, psA, psB, tg, scr, scaled=True):
            b.inc(te.matmul(psA[:, 0:G], t_ones_row[0:1, :], st_A[0:1, tg:tg + G],
                            start=True, stop=True), "pe")
            bb2 = b.inc(te.matmul(psB[:, 0:G], t_ones_row[0:1, :], st_B[0:1, tg:tg + G],
                                  start=True, stop=True), "pe")
            b.wait(ve, "pe", bb2)
            vv = 0
            for c in range(NC):
                t1 = xn_g[:, c * G:(c + 1) * G]
                b.inc(ve.tensor_mul(scr[:, 0:G], zg[:, c * G:(c + 1) * G], psA[:, 0:G]), "vec")
                b.inc(ve.tensor_add(scr[:, 0:G], scr[:, 0:G], psB[:, 0:G]), "vec")
                if scaled:
                    vv = b.inc(ve.tensor_scalar(t1, scr[:, 0:G], t_lns[:, c:c + 1],
                                                t_lnb[:, c:c + 1], ALU.mult, ALU.add), "vec")
                else:
                    vv = b.inc(ve.tensor_copy(t1, scr[:, 0:G]), "vec")
            return vv

        def ret_layer(li):
            lnstats(z_buf, 0, TOK, f"r{li}s")
            nc.all_engine_barrier()
            b.inc(ve.memset(st_R[:], 0.0), "vec")
            b.inc(ve.memset(st_Rb[:], 0.0), "vec")
            gamC = [float(_gammas64()[h] ** CH) for h in range(H)]

            with ExitStack() as s:
                wq_sb = s.enter_context(nc.sbuf_tensor(f"r{li}_wq", [128, NC * D], BF16))
                wk_sb = s.enter_context(nc.sbuf_tensor(f"r{li}_wk", [128, NC * D], BF16))
                wv_sb = s.enter_context(nc.sbuf_tensor(f"r{li}_wv", [128, NC * D], BF16))
                zg2 = s.enter_context(nc.sbuf_tensor(f"r{li}_zg", [128, 2 * NC * G], F32))
                xn2 = s.enter_context(nc.sbuf_tensor(f"r{li}_xn", [128, NC * G], BF16))
                crg2 = s.enter_context(nc.sbuf_tensor(f"r{li}_cr", [128, G], F32))
                srg2 = s.enter_context(nc.sbuf_tensor(f"r{li}_sr", [128, G], F32))
                rotq = s.enter_context(nc.sbuf_tensor(f"r{li}_rq", [128, H * G], BF16))
                rotk = s.enter_context(nc.sbuf_tensor(f"r{li}_rk", [128, H * G], BF16))
                v_tok = s.enter_context(nc.sbuf_tensor(f"r{li}_vt", [128, 2 * D], BF16))
                y_grp = s.enter_context(nc.sbuf_tensor(f"r{li}_yg", [128, H * G], BF16))
                evt = s.enter_context(nc.sbuf_tensor(f"r{li}_ev", [128, 2 * G], F32))
                evc = s.enter_context(nc.sbuf_tensor(f"r{li}_ec", [128, 2 * G], F32))
                attd_all = s.enter_context(nc.sbuf_tensor(f"r{li}_ad", [128, H * CH], BF16))
                qg_all = s.enter_context(nc.sbuf_tensor(f"r{li}_qg", [128, H * G], BF16))
                kts2 = s.enter_context(nc.sbuf_tensor(f"r{li}_kt", [128, 2 * CH], BF16))
                ba = [s.enter_context(nc.psum_tensor(f"r{li}_ba{i}", [128, 512], F32)) for i in range(2)]
                by = [s.enter_context(nc.psum_tensor(f"r{li}_by{i}", [128, 512], F32)) for i in range(2)]
                bd = [s.enter_context(nc.psum_tensor(f"r{li}_bd{i}", [128, 512], F32)) for i in range(2)]
                psV = s.enter_context(nc.psum_tensor(f"r{li}_pV", [128, 512], F32))
                psKT_b = s.enter_context(nc.psum_tensor(f"r{li}_pKT", [128, 1024], BF16))
                for w_dram, wsb in [(w_q, wq_sb), (w_k, wk_sb), (w_v, wv_sb)]:
                    b.inc(ge.dma_start(out=wsb.rearrange("p (c d) -> p c d", d=D),
                                       in_=w_dram[li].rearrange("(c p) d -> p c d", p=128)), "wld", 16)
                wldv = b.val("wld")
                b.wait(te, "wld", wldv)

                ldv = [0] * NG
                gv_pe = [0] * NG
                gv_vec = [0] * NG
                rbv_prev = [0] * H

                def emit_loads(g):
                    sl = g % 2
                    if g >= 2:
                        b.wait(sy, "pe", gv_pe[g - 2])
                        b.wait(sy, "vec", gv_vec[g - 2])
                    tg = g * G
                    ldv[g] = b.inc(sy.dma_start(out=zg2[:, sl * NC * G:(sl + 1) * NC * G].rearrange("p (c t) -> p c t", c=NC),
                                                in_=_rearr_cols(z_buf, tg, G)), "ld", 16)

                def emit_cs_loads(g):
                    tg = g * G
                    if g > 0:
                        b.wait(sy, "vec")
                    b.inc(sy.dma_start(out=crg2[:, 0:G], in_=crot[:, tg:tg + G]), "ld", 16)
                    csld[0] = b.inc(sy.dma_start(out=srg2[:, 0:G], in_=srot[:, tg:tg + G]), "ld", 16)

                csld = [0]
                emit_loads(0)
                stv_y = 0
                ybank = [0, 0]
                rbp = [0] * H
                qgst = [0]
                for g in (range(NG) if "scan" in RET_PARTS else []):
                    sl = g % 2
                    tg = g * G
                    zg = zg2[:, sl * NC * G:(sl + 1) * NC * G]
                    xn_g = xn2[:, 0:NC * G]
                    crg = crg2[:, 0:G]
                    srg = srg2[:, 0:G]
                    emit_cs_loads(g)
                    b.wait(ve, "ld", ldv[g])
                    b.wait(ve, "ld", csld[0])
                    b.wait(te, "act")
                    b.wait(te, "vec")
                    _xn_build(zg, xn_g, by[0], by[1], tg, evt)
                    xnv = b.val("vec")
                    b.wait(te, "vec", xnv)
                    for (wsb, rot, sign) in [(wq_sb, rotq, 1.0), (wk_sb, rotk, -1.0)]:
                        evh = [0, 0]
                        rvh = [0, 0]
                        pqb = [0, 0]
                        swv = 0
                        for h in range(H):
                            p = h % 2
                            if evh[p]:
                                b.wait(te, "act", evh[p])
                            pv = 0
                            for c in range(NC):
                                pv = b.inc(te.matmul(ba[p][:, 0:G], wsb[:, c * D + h * 128: c * D + (h + 1) * 128],
                                                     xn_g[:, c * G:(c + 1) * G],
                                                     start=(c == 0), stop=(c == NC - 1)), "pe")
                            b.wait(se, "pe", pv)
                            if rvh[p]:
                                b.wait(se, "vec", rvh[p])
                                b.wait(se, "pool", pqb[p])
                            evh[p] = b.inc(se.copy(evt[:, p * G:(p + 1) * G], ba[p][:, 0:G]), "act")
                            b.wait(ge, "act", evh[p])
                            pqb[p] = b.inc(ge.tensor_copy(rot[:, h * G:(h + 1) * G], evt[:, p * G:(p + 1) * G]), "pool")
                            b.inc(ge.tensor_mul(evc[:, p * G:(p + 1) * G], evt[:, p * G:(p + 1) * G], crg), "pool")
                            pqb[p] = b.val("pool")
                            b.wait(te, "pool", pqb[p])
                            if swv:
                                b.wait(te, "vec", swv)
                            ppv = b.inc(te.matmul(psV[:, 0:G], t_pswap[:], rot[:, h * G:(h + 1) * G],
                                                  start=True, stop=True), "pe")
                            b.wait(ve, "pe", ppv)
                            b.wait(ve, "pool", pqb[p])
                            swv = b.inc(ve.tensor_mul(evt[:, p * G:(p + 1) * G], psV[:, 0:G], srg), "vec")
                            if sign > 0:
                                rvh[p] = b.inc(ve.tensor_add(rot[:, h * G:(h + 1) * G],
                                                             evc[:, p * G:(p + 1) * G],
                                                             evt[:, p * G:(p + 1) * G]), "vec")
                            else:
                                rvh[p] = b.inc(ve.tensor_sub(rot[:, h * G:(h + 1) * G],
                                                             evc[:, p * G:(p + 1) * G],
                                                             evt[:, p * G:(p + 1) * G]), "vec")
                        b.wait(te, "vec", rvh[0])
                        b.wait(te, "vec", rvh[1])
                    qgv_h = [0] * H
                    b.wait(ge, "vec")
                    b.wait(ge, "st", qgst[0])
                    for h in range(H):
                        a = rotq[:, h * G:(h + 1) * G].rearrange("p (c t) -> p c t", t=CH)
                        gqb = t_gq[:, h * CH:(h + 1) * CH].rearrange("p (c t) -> p c t", t=CH)
                        a2, b2 = bass.broadcast_tensor_aps(a, gqb)
                        qgv_h[h] = b.inc(ge.tensor_tensor(
                            qg_all[:, h * G:(h + 1) * G].rearrange("p (c t) -> p c t", t=CH),
                            a2, b2, ALU.mult), "pool")
                    b.wait(sy, "pool", qgv_h[H - 1])
                    qgst[0] = b.inc(sy.dma_start(out=_rearr_cols(qg_dram, tg, G),
                                                 in_=qg_all.rearrange("p (h t) -> p h t", h=H)), "st", 16)
                    ksv_last = 0
                    for tc in range(4):
                        tsl = tc % 2
                        vts = v_tok[:, tsl * D:(tsl + 1) * D]
                        vev = 0
                        for dv in range(2):
                            if dv == 1:
                                b.wait(te, "act", vev)
                            pvv = 0
                            for c in range(NC):
                                pvv = b.inc(te.matmul(psV[:, 0:G], xn_g[:, c * G + tc * 128: c * G + tc * 128 + 128],
                                                      wv_sb[:, c * D + dv * 512: c * D + (dv + 1) * 512],
                                                      start=(c == 0), stop=(c == NC - 1)), "pe")
                            b.wait(se, "pe", pvv)
                            vev = b.inc(se.copy(vts[:, dv * 512:(dv + 1) * 512], psV[:, 0:G]), "act")
                        b.wait(te, "act", vev)
                        adv = [0] * H
                        for h in range(H):
                            p = h % 2
                            rq_c = rotq[:, h * G + tc * 128: h * G + (tc + 1) * 128]
                            rk_c = rotk[:, h * G + tc * 128: h * G + (tc + 1) * 128]
                            if h >= 2:
                                b.wait(te, "vec", adv[h - 2])
                            atv = b.inc(te.matmul(ba[p][:, 0:128], rk_c, rq_c, start=True, stop=True), "pe")
                            b.wait(ve, "pe", atv)
                            adv[h] = b.inc(ve.tensor_mul(attd_all[:, h * CH:(h + 1) * CH], ba[p][:, 0:128],
                                                         t_ddec[:, h * CH:(h + 1) * CH]), "vec")
                        for h in range(H):
                            p = h % 2
                            vt_h = vts[:, h * 128:(h + 1) * 128]
                            if ybank[p]:
                                b.wait(te, "act", ybank[p])
                            b.wait(te, "vec", adv[h])
                            b.wait(te, "pool", max(qgv_h[h], rbp[h]))
                            b.inc(te.matmul(by[p][:, 0:128], vt_h, attd_all[:, h * CH:(h + 1) * CH],
                                            start=True, stop=False), "pe")
                            yv = b.inc(te.matmul(by[p][:, 0:128], st_Rb[:, h * 128:(h + 1) * 128],
                                                 qg_all[:, h * G + tc * 128: h * G + (tc + 1) * 128],
                                                 start=False, stop=True), "pe")
                            b.wait(se, "pe", yv)
                            ybank[p] = b.inc(se.copy(y_grp[:, h * G + tc * 128: h * G + (tc + 1) * 128],
                                                     by[p][:, 0:128]), "act")
                        ksv = [0, 0]
                        ruv = [0] * H
                        for h in range(H):
                            p = h % 2
                            rk_c = rotk[:, h * G + tc * 128: h * G + (tc + 1) * 128]
                            vt_h = vts[:, h * 128:(h + 1) * 128]
                            if ksv_last:
                                b.wait(te, "vec", ksv_last)
                            ktv = b.inc(te.transpose(psKT_b[:, 0:128], rk_c, t_id_b[:]), "pe")
                            b.wait(ve, "pe", ktv)
                            ksv[p] = b.inc(ve.tensor_scalar_mul(kts2[:, p * CH:(p + 1) * CH],
                                                                psKT_b[:, 0:128], t_kscale[:, h:h + 1]), "vec")
                            ksv_last = ksv[p]
                            if h >= 2:
                                b.wait(te, "vec", ruv[h - 2])
                            b.wait(te, "vec", ksv[p])
                            drv = b.inc(te.matmul(bd[p][:, 0:128], kts2[:, p * CH:(p + 1) * CH], vt_h,
                                                  start=True, stop=True), "pe")
                            b.wait(ve, "pe", drv)
                            b.wait(ve, "pool", rbp[h])
                            ruv[h] = b.inc(ve.scalar_tensor_tensor(st_R[:, h * 128:(h + 1) * 128],
                                                                   st_R[:, h * 128:(h + 1) * 128], gamC[h],
                                                                   bd[p][:, 0:128], ALU.mult, ALU.add), "vec")
                            b.wait(ge, "vec", ruv[h])
                            rbp[h] = b.inc(ge.tensor_copy(st_Rb[:, h * 128:(h + 1) * 128],
                                                          st_R[:, h * 128:(h + 1) * 128]), "pool")
                    gv_pe[g] = b.val("pe")
                    gv_vec[g] = b.val("vec")
                    if g + 1 < NG:
                        emit_loads(g + 1)
                    b.wait(sy, "act")
                    stv_y = b.inc(sy.dma_start(out=_rearr_cols(y_dram, tg, G),
                                               in_=y_grp.rearrange("p (h t) -> p h t", h=H)), "st", 16)
                    b.wait(se, "st", stv_y)
                b.wait(sy, "st")
                nc.all_engine_barrier()

                if "relay" in RET_PARTS:
                    b.inc(ge.dma_start(out=relay_in.ap().rearrange("(q h) v -> q h v", h=H),
                                       in_=st_R.rearrange("p (h v) -> p h v", v=128)), "wld", 16)
                    b.wait(ge, "wld")
                    ccv = b.inc(ge.collective_compute(
                        "AllGather", ALU.bypass, replica_groups=[list(range(8))],
                        ins=[relay_in.ap().opt()], outs=[relay_out.ap().opt()]), "cc", 1)
                    b.inc(ge.memset(rlb_f[:], 0.0), "pool")
                    b.wait(ge, "cc", ccv)
                    gvx = b.inc(ge.indirect_dma_start(
                        out=rlb_f[:, :], out_offset=None,
                        in_=relay_out.ap()[:, :],
                        in_offset=IndirectOffsetOnAxis(t_ridx[:, 0:1], 0),
                        bounds_check=NC * H * DH - 1, oob_is_err=False), "wld", 16)
                    b.wait(ve, "wld", gvx)
                    b.wait(ve, "pool")
                    b.inc(ve.tensor_copy(st_Rrelay[:], rlb_f[:]), "vec")
                    nc.all_engine_barrier()
                else:
                    b.inc(ve.memset(st_Rrelay[:], 0.0), "vec")

                ldv = [0] * NG
                gv_pe = [0] * NG
                gv_vec = [0] * NG
                yldv = [0] * NG
                ystv = [0] * NG

                def emit_loads_c(g):
                    sl = g % 2
                    if g >= 2:
                        b.wait(sy, "pe", gv_pe[g - 2])
                        b.wait(sy, "vec", gv_vec[g - 2])
                    tg = g * G
                    ldv[g] = b.inc(sy.dma_start(out=zg2[:, sl * NC * G:(sl + 1) * NC * G].rearrange("p (c t) -> p c t", c=NC),
                                                in_=_rearr_cols(z_buf, tg, G)), "ld", 16)

                def emit_yload_c(g):
                    yldv[g] = b.inc(sy.dma_start(out=y_grp.rearrange("p (h t) -> p h t", h=H),
                                                 in_=_rearr_cols(y_dram, g * G, G)), "ld", 16)

                emit_loads_c(0)
                emit_yload_c(0)
                for g in (range(NG) if "corrOLD" in RET_PARTS else []):
                    sl = g % 2
                    tg = g * G
                    zg = zg2[:, sl * NC * G:(sl + 1) * NC * G]
                    xn_g = xn2[:, 0:NC * G]
                    crg = crg2[:, 0:G]
                    srg = srg2[:, 0:G]
                    emit_cs_loads(g)
                    b.wait(ve, "ld", ldv[g])
                    b.wait(ve, "ld", csld[0])
                    b.wait(te, "act")
                    b.wait(te, "vec")
                    _xn_build(zg, xn_g, by[0], by[1], tg, evt)
                    xnv = b.val("vec")
                    b.wait(te, "vec", xnv)
                    evh = [0, 0]
                    rvh = [0, 0]
                    pqb = [0, 0]
                    swv = 0
                    b.wait(ve, "ld", yldv[g])
                    for h in range(H):
                        p = h % 2
                        if evh[p]:
                            b.wait(te, "act", evh[p])
                        pv = 0
                        for c in range(NC):
                            pv = b.inc(te.matmul(ba[p][:, 0:G], wq_sb[:, c * D + h * 128: c * D + (h + 1) * 128],
                                                 xn_g[:, c * G:(c + 1) * G],
                                                 start=(c == 0), stop=(c == NC - 1)), "pe")
                        b.wait(se, "pe", pv)
                        if rvh[p]:
                            b.wait(se, "vec", rvh[p])
                            b.wait(se, "pool", pqb[p])
                        evh[p] = b.inc(se.copy(evt[:, p * G:(p + 1) * G], ba[p][:, 0:G]), "act")
                        b.wait(ge, "act", evh[p])
                        pqb[p] = b.inc(ge.tensor_copy(rotq[:, h * G:(h + 1) * G], evt[:, p * G:(p + 1) * G]), "pool")
                        b.inc(ge.tensor_mul(evc[:, p * G:(p + 1) * G], evt[:, p * G:(p + 1) * G], crg), "pool")
                        pqb[p] = b.val("pool")
                        b.wait(te, "pool", pqb[p])
                        if swv:
                            b.wait(te, "vec", swv)
                        ppv = b.inc(te.matmul(psV[:, 0:G], t_pswap[:], rotq[:, h * G:(h + 1) * G],
                                              start=True, stop=True), "pe")
                        b.wait(ve, "pe", ppv)
                        b.wait(ve, "pool", pqb[p])
                        swv = b.inc(ve.tensor_mul(evt[:, p * G:(p + 1) * G], psV[:, 0:G], srg), "vec")
                        rvh[p] = b.inc(ve.tensor_add(rotq[:, h * G:(h + 1) * G],
                                                     evc[:, p * G:(p + 1) * G],
                                                     evt[:, p * G:(p + 1) * G]), "vec")
                        b.wait(ge, "vec", rvh[p])
                        a = rotq[:, h * G:(h + 1) * G].rearrange("p (c t) -> p c t", t=CH)
                        gqb = t_gq[:, h * CH:(h + 1) * CH].rearrange("p (c t) -> p c t", t=CH)
                        a2, b2 = bass.broadcast_tensor_aps(a, gqb)
                        qgv = b.inc(ge.tensor_tensor(
                            qg_all[:, h * G:(h + 1) * G].rearrange("p (c t) -> p c t", t=CH),
                            a2, b2, ALU.mult), "pool")
                        b.wait(te, "pool", qgv)
                        sttl = [0, 0]
                        for tc in range(4):
                            pq = tc % 2
                            if sttl[pq]:
                                b.wait(te, "vec", sttl[pq])
                            cpv = b.inc(te.matmul(bd[pq][:, 0:128], st_Rrelay[:, h * 128:(h + 1) * 128],
                                                  qg_all[:, h * G + tc * 128: h * G + (tc + 1) * 128],
                                                  start=True, stop=True), "pe")
                            b.wait(ve, "pe", cpv)
                            gscale = float(_gammas64()[h] ** (128 * (g * 4 + tc)))
                            ysl = y_grp[:, h * G + tc * 128: h * G + (tc + 1) * 128]
                            sttl[pq] = b.inc(ve.scalar_tensor_tensor(ysl, bd[pq][:, 0:128], gscale, ysl,
                                                                     ALU.mult, ALU.add), "vec")
                    gv_pe[g] = b.val("pe")
                    gv_vec[g] = b.val("vec")
                    if g + 1 < NG:
                        emit_loads_c(g + 1)
                    b.wait(sy, "vec")
                    ystv[g] = b.inc(sy.dma_start(out=_rearr_cols(y_dram, tg, G),
                                                 in_=y_grp.rearrange("p (h t) -> p h t", h=H)), "st", 16)
                    if g + 1 < NG:
                        emit_yload_c(g + 1)
                b.wait(sy, "st")
            nc.all_engine_barrier()
            if "post" in RET_PARTS:
                _ret_post(li)
                nc.all_engine_barrier()
            if "ffn" in RET_PARTS:
                _ret_ffn(li)
                nc.all_engine_barrier()

        def _ret_post(li):
            outer = ExitStack()
            st_A8 = outer.enter_context(nc.sbuf_tensor(f"po{li}_A8", [H, TOK], F32))
            st_B8 = outer.enter_context(nc.sbuf_tensor(f"po{li}_B8", [H, TOK], F32))
            with ExitStack() as s:
                y_g = s.enter_context(nc.sbuf_tensor(f"po{li}_yg", [128, H * G], BF16))
                sqb = s.enter_context(nc.sbuf_tensor(f"po{li}_sq", [128, 2 * G], BF16))
                mu8 = s.enter_context(nc.sbuf_tensor(f"po{li}_mu", [8, TOK], F32))
                var8 = s.enter_context(nc.sbuf_tensor(f"po{li}_va", [8, TOK], F32))
                t8 = s.enter_context(nc.sbuf_tensor(f"po{li}_t8", [8, G], F32))
                qg_g = s.enter_context(nc.sbuf_tensor(f"po{li}_qgg", [128, H * G], BF16))
                psS = s.enter_context(nc.psum_tensor(f"po{li}_pS", [128, 512], F32))
                psQ = s.enter_context(nc.psum_tensor(f"po{li}_pQ", [128, 512], F32))
                psX = [s.enter_context(nc.psum_tensor(f"po{li}_pX{i}", [128, 512], F32)) for i in range(2)]
                gam64 = _gammas64()
                for g in range(NG):
                    tg = g * G
                    if g > 0:
                        b.wait(sy, "pe")
                    v = b.inc(sy.dma_start(out=y_g.rearrange("p (h t) -> p h t", h=H),
                                           in_=_rearr_cols(y_dram, tg, G)), "ld", 16)
                    qv = 0
                    if "corr" in RET_PARTS:
                        qv = b.inc(sy.dma_start(out=qg_g.rearrange("p (h t) -> p h t", h=H),
                                                in_=_rearr_cols(qg_dram, tg, G)), "ld", 16)
                    b.wait(ve, "ld", v)
                    b.wait(te, "ld", v)
                    if g > 0:
                        b.wait(te, "vec")
                    if "corr" in RET_PARTS:
                        b.wait(te, "ld", qv)
                        sttl = [0, 0]
                        for h in range(H):
                            for tc in range(4):
                                gscale = float(gam64[h] ** (128 * (g * 4 + tc)))
                                if gscale == 0.0:
                                    continue
                                pq = tc % 2
                                if sttl[pq]:
                                    b.wait(te, "vec", sttl[pq])
                                cpv = b.inc(te.matmul(psX[pq][:, 0:128], st_Rrelay[:, h * 128:(h + 1) * 128],
                                                      qg_g[:, h * G + tc * 128: h * G + (tc + 1) * 128],
                                                      start=True, stop=True), "pe")
                                b.wait(ve, "pe", cpv)
                                ysl = y_g[:, h * G + tc * 128: h * G + (tc + 1) * 128]
                                sttl[pq] = b.inc(ve.scalar_tensor_tensor(ysl, psX[pq][:, 0:128], gscale, ysl,
                                                                         ALU.mult, ALU.add), "vec")
                    pv = 0
                    for h in range(H):
                        ysl = y_g[:, h * G:(h + 1) * G]
                        sqs = sqb[:, (h % 2) * G:(h % 2) * G + G]
                        if h >= 2 or g > 0:
                            b.wait(ve, "pe")
                        sv = b.inc(ve.tensor_mul(sqs, ysl, ysl), "vec")
                        b.inc(te.matmul(psS[0:8, 0:G], t_ehs_b[:, h * H:(h + 1) * H], ysl,
                                        start=(h == 0), stop=(h == H - 1)), "pe")
                        b.wait(te, "vec", sv)
                        pv = b.inc(te.matmul(psQ[0:8, 0:G], t_ehs_b[:, h * H:(h + 1) * H], sqs,
                                             start=(h == 0), stop=(h == H - 1)), "pe")
                    if "corr" in RET_PARTS:
                        b.wait(sy, "vec")
                        b.inc(sy.dma_start(out=_rearr_cols(y_dram, tg, G),
                                           in_=y_g.rearrange("p (h t) -> p h t", h=H)), "st", 16)
                    b.wait(ve, "pe", pv)
                    b.inc(ve.tensor_scalar_mul(mu8[0:8, tg:tg + G], psS[0:8, 0:G], 1.0 / DH), "vec")
                    b.inc(ve.tensor_mul(t8[0:8, :], mu8[0:8, tg:tg + G], mu8[0:8, tg:tg + G]), "vec")
                    b.inc(ve.tensor_scalar_mul(var8[0:8, tg:tg + G], psQ[0:8, 0:G], 1.0 / DH), "vec")
                    b.inc(ve.tensor_sub(var8[0:8, tg:tg + G], var8[0:8, tg:tg + G], t8[0:8, :]), "vec")
                b.inc(ve.tensor_scalar_add(var8[0:8, :], var8[0:8, :], EPS), "vec")
                rv = b.inc(ve.reciprocal(var8[0:8, :], var8[0:8, :]), "vec")
                b.wait(se, "vec", rv)
                av = b.inc(se.activation(st_A8[0:8, :], var8[0:8, :], AF.Sqrt), "act")
                b.wait(ve, "act", av)
                bv = b.inc(ve.scalar_tensor_tensor(st_B8[0:8, :], mu8[0:8, :], -1.0,
                                                   st_A8[0:8, :], ALU.mult, ALU.mult), "vec")
                b.wait(se, "vec", bv)
                b.wait(te, "vec", bv)
            nc.all_engine_barrier()
            with ExitStack() as s:
                wg_sb = s.enter_context(nc.sbuf_tensor(f"po{li}_wg", [128, NC * D], BF16))
                wo_sb = s.enter_context(nc.sbuf_tensor(f"po{li}_wo", [128, NC * D], BF16))
                zg = s.enter_context(nc.sbuf_tensor(f"po{li}_zg", [128, NC * G], F32))
                xn_g = s.enter_context(nc.sbuf_tensor(f"po{li}_xn", [128, NC * G], BF16))
                y_g = s.enter_context(nc.sbuf_tensor(f"po{li}_yg2", [128, H * G], BF16))
                g_sb = s.enter_context(nc.sbuf_tensor(f"po{li}_gs", [128, NC * G], BF16))
                m_sb = s.enter_context(nc.sbuf_tensor(f"po{li}_ms", [128, NC * G], BF16))
                scr = s.enter_context(nc.sbuf_tensor(f"po{li}_scr", [128, 2 * G], F32))
                psA = s.enter_context(nc.psum_tensor(f"po{li}_pA", [128, 512], F32))
                psB = s.enter_context(nc.psum_tensor(f"po{li}_pB", [128, 512], F32))
                psG = s.enter_context(nc.psum_tensor(f"po{li}_pG", [128, 512], F32))
                psO = s.enter_context(nc.psum_tensor(f"po{li}_pO", [128, 512], F32))
                psbc = s.enter_context(nc.psum_tensor(f"po{li}_pbc", [128, 512], F32))
                psbd = s.enter_context(nc.psum_tensor(f"po{li}_pbd", [128, 512], F32))
                b.inc(ge.dma_start(out=wg_sb.rearrange("p (c d) -> p c d", d=D),
                                   in_=w_g[li].rearrange("(c p) d -> p c d", p=128)), "wld", 16)
                wldv = b.inc(ge.dma_start(out=wo_sb.rearrange("p (c d) -> p c d", d=D),
                                          in_=w_o[li].rearrange("(c p) d -> p c d", p=128)), "wld", 16)
                b.wait(te, "wld", wldv)
                for g in range(NG):
                    tg = g * G
                    if g > 0:
                        b.wait(sy, "pe")
                        b.wait(sy, "vec")
                        b.wait(sy, "st")
                    v = b.inc(sy.dma_start(out=zg.rearrange("p (c t) -> p c t", c=NC),
                                           in_=_rearr_cols(z_buf, tg, G)), "ld", 16)
                    yv = b.inc(sy.dma_start(out=y_g.rearrange("p (h t) -> p h t", h=H),
                                            in_=_rearr_cols(y_dram, tg, G)), "ld", 16)
                    b.wait(ve, "ld", v)
                    _xn_build(zg, xn_g, psA, psB, tg, scr)
                    xnv = b.val("vec")
                    b.wait(te, "vec", xnv)
                    gv = 0
                    for co in range(NC):
                        if gv > 0:
                            b.wait(te, "act", gv)
                        pv = 0
                        for c in range(NC):
                            pv = b.inc(te.matmul(psG[:, 0:G], wg_sb[:, c * D + co * 128: c * D + (co + 1) * 128],
                                                 xn_g[:, c * G:(c + 1) * G],
                                                 start=(c == 0), stop=(c == NC - 1)), "pe")
                        b.wait(se, "pe", pv)
                        b.wait(se, "vec") if co == 0 and g > 0 else None
                        gv = b.inc(se.activation(g_sb[:, co * G:(co + 1) * G], psG[:, 0:G], AF.Silu), "act")
                    b.wait(ve, "ld", yv)
                    for h in range(H):
                        if h > 0 or g > 0:
                            b.wait(te, "vec")
                        bca = b.inc(te.matmul(psbc[:, 0:G], t_ebc_f[:, h * 128:(h + 1) * 128],
                                              st_A8[0:8, tg:tg + G], start=True, stop=True), "pe")
                        bcb = b.inc(te.matmul(psbd[:, 0:G], t_ebc_f[:, h * 128:(h + 1) * 128],
                                              st_B8[0:8, tg:tg + G], start=True, stop=True), "pe")
                        b.wait(ve, "pe", bcb)
                        b.wait(ve, "act", gv)
                        ysl = y_g[:, h * G:(h + 1) * G]
                        b.inc(ve.tensor_mul(scr[:, 0:G], ysl, psbc[:, 0:G]), "vec")
                        b.inc(ve.tensor_add(scr[:, 0:G], scr[:, 0:G], psbd[:, 0:G]), "vec")
                        b.inc(ve.tensor_scalar(scr[:, G:2 * G], scr[:, 0:G],
                                               t_gns[:, li * NC + h: li * NC + h + 1],
                                               t_gnb[:, li * NC + h: li * NC + h + 1],
                                               ALU.mult, ALU.add), "vec")
                        mv = b.inc(ve.tensor_mul(m_sb[:, h * G:(h + 1) * G], scr[:, G:2 * G],
                                                 g_sb[:, h * G:(h + 1) * G]), "vec")
                    b.wait(te, "vec", mv)
                    zv = 0
                    for co in range(NC):
                        if zv > 0:
                            b.wait(te, "vec", zv)
                        pv = 0
                        for c in range(NC):
                            pv = b.inc(te.matmul(psO[:, 0:G], wo_sb[:, c * D + co * 128: c * D + (co + 1) * 128],
                                                 m_sb[:, c * G:(c + 1) * G],
                                                 start=(c == 0), stop=(c == NC - 1)), "pe")
                        b.wait(ve, "pe", pv)
                        zv = b.inc(ve.tensor_add(zg[:, co * G:(co + 1) * G], zg[:, co * G:(co + 1) * G],
                                                 psO[:, 0:G]), "vec")
                    b.wait(sy, "vec", zv)
                    b.inc(sy.dma_start(out=_rearr_cols(z_buf, tg, G),
                                       in_=zg.rearrange("p (c t) -> p c t", c=NC)), "st", 16)
                b.wait(sy, "st")
            outer.close()

        def _ret_ffn(li):
            lnstats(z_buf, 0, TOK, f"rf{li}s")
            nc.all_engine_barrier()
            with ExitStack() as s:
                w1_sb = s.enter_context(nc.sbuf_tensor(f"rf{li}_w1", [128, NC * FF], BF16))
                w2_sb = s.enter_context(nc.sbuf_tensor(f"rf{li}_w2", [128, 16 * D], BF16))
                zg = s.enter_context(nc.sbuf_tensor(f"rf{li}_zg", [128, NC * G], F32))
                xn_g = s.enter_context(nc.sbuf_tensor(f"rf{li}_xn", [128, NC * G], BF16))
                hid = s.enter_context(nc.sbuf_tensor(f"rf{li}_hid", [128, 16 * G], BF16))
                scr = s.enter_context(nc.sbuf_tensor(f"rf{li}_scr", [128, 4 * G], F32))
                psA = s.enter_context(nc.psum_tensor(f"rf{li}_pA", [128, 512], F32))
                psB = s.enter_context(nc.psum_tensor(f"rf{li}_pB", [128, 512], F32))
                psF = [s.enter_context(nc.psum_tensor(f"rf{li}_pF{i}", [128, 512], F32)) for i in range(2)]
                psW = [s.enter_context(nc.psum_tensor(f"rf{li}_pW{i}", [128, 512], F32)) for i in range(2)]
                ringF = Ring(b, f"rf{li}_rF", [p[:, 0:512] for p in psF])
                ringW = Ring(b, f"rf{li}_rW", [p[:, 0:512] for p in psW])
                b.inc(ge.dma_start(out=w1_sb.rearrange("p (c d) -> p c d", d=FF),
                                   in_=w_1[li].rearrange("(c p) d -> p c d", p=128)), "wld", 16)
                wldv = b.inc(ge.dma_start(out=w2_sb.rearrange("p (c d) -> p c d", d=D),
                                          in_=w_2[li].rearrange("(c p) d -> p c d", p=128)), "wld", 16)
                b.wait(te, "wld", wldv)
                c1 = GELU_C
                c1a = GELU_C * GELU_A
                for g in range(NG):
                    tg = g * G
                    if g > 0:
                        b.wait(sy, "pe")
                        b.wait(sy, "vec")
                        b.wait(sy, "st")
                    v = b.inc(sy.dma_start(out=zg.rearrange("p (c t) -> p c t", c=NC),
                                           in_=_rearr_cols(z_buf, tg, G)), "ld", 16)
                    b.wait(ve, "ld", v)
                    _xn_build(zg, xn_g, psA, psB, tg, scr)
                    xnv = b.val("vec")
                    b.wait(te, "vec", xnv)
                    hv = 0
                    for fo in range(16):
                        pf = ringF.acquire(te)
                        pv = 0
                        for c in range(NC):
                            pv = b.inc(te.matmul(pf, w1_sb[:, c * FF + fo * 128: c * FF + (fo + 1) * 128],
                                                 xn_g[:, c * G:(c + 1) * G],
                                                 start=(c == 0), stop=(c == NC - 1)), "pe")
                        b.wait(se, "pe", pv)
                        if hv > 0:
                            b.wait(se, ringF.name, hv)
                        sqv = b.inc(se.activation(scr[:, 0:G], pf, AF.Square), "act")
                        b.wait(ve, "act", sqv)
                        b.inc(ve.tensor_scalar(scr[:, G:2 * G], scr[:, 0:G], c1a, c1, ALU.mult, ALU.add), "vec")
                        uv = b.inc(ve.tensor_mul(scr[:, 2 * G:3 * G], pf, scr[:, G:2 * G]), "vec")
                        b.wait(se, "vec", uv)
                        tv = b.inc(se.activation(scr[:, 3 * G:4 * G], scr[:, 2 * G:3 * G], AF.Tanh), "act")
                        b.wait(ve, "act", tv)
                        b.inc(ve.tensor_scalar(scr[:, G:2 * G], scr[:, 3 * G:4 * G], 0.5, 0.5,
                                               ALU.mult, ALU.add), "vec")
                        hvi = ve.tensor_mul(hid[:, fo * G:(fo + 1) * G], pf, scr[:, G:2 * G])
                        ringF.release(hvi)
                        hv = b.cnt[ringF.name]
                    b.wait(te, ringF.name, hv)
                    for co in range(NC):
                        pw = ringW.acquire(te)
                        pv = 0
                        for fc in range(16):
                            pv = b.inc(te.matmul(pw, w2_sb[:, fc * D + co * 128: fc * D + (co + 1) * 128],
                                                 hid[:, fc * G:(fc + 1) * G],
                                                 start=(fc == 0), stop=(fc == 15)), "pe")
                        b.wait(ve, "pe", pv)
                        zvi = ve.tensor_add(zg[:, co * G:(co + 1) * G], zg[:, co * G:(co + 1) * G], pw)
                        ringW.release(zvi)
                    b.wait(sy, ringW.name)
                    b.inc(sy.dma_start(out=_rearr_cols(z_buf, tg, G),
                                       in_=zg.rearrange("p (c t) -> p c t", c=NC)), "st", 16)
                b.wait(sy, "st")

        def merge_rz():
            with ExitStack() as s:
                rg = s.enter_context(nc.sbuf_tensor("mg_rg", [128, NC * G], F32))
                zg2 = s.enter_context(nc.sbuf_tensor("mg_zg", [128, NC * G], F32))
                for g in range(NG):
                    tg = g * G
                    if g > 0:
                        b.wait(sy, "st")
                        b.wait(sy, "vec")
                    b.inc(sy.dma_start(out=rg.rearrange("p (c t) -> p c t", c=NC),
                                       in_=_rearr_cols(r_res, tg, G)), "ld", 16)
                    v = b.inc(sy.dma_start(out=zg2.rearrange("p (c t) -> p c t", c=NC),
                                           in_=_rearr_cols(z_buf, tg, G)), "ld", 16)
                    b.wait(ve, "ld", v)
                    vv = 0
                    for c in range(NC):
                        vv = b.inc(ve.tensor_add(rg[:, c * G:(c + 1) * G], rg[:, c * G:(c + 1) * G],
                                                 zg2[:, c * G:(c + 1) * G]), "vec")
                    b.wait(sy, "vec", vv)
                    b.inc(sy.dma_start(out=_rearr_cols(r_res, tg, G),
                                       in_=rg.rearrange("p (c t) -> p c t", c=NC)), "st", 16)
                b.wait(sy, "st")
            nc.all_engine_barrier()

        def p4_output():
            with ExitStack() as ph:
                def src(tc, buf):
                    return b.inc(sy.dma_start(
                        out=buf.rearrange("p (c t) -> p c t", c=NC),
                        in_=_rearr_cols(r_res, tc * 128, 128)), "ld", 16)
                def dst(tc, buf):
                    b.inc(sy.dma_start(out=out_t[tc * 128:(tc + 1) * 128, :], in_=buf[:, :]), "st", 16)
                transpose_stream(src, dst, ph, "p4")

        load_tables()
        if "P0" in phases:
            p0_ingest()
        if "FFN1" in phases:
            moe_phase("f1")
        if "RET" in phases:
            znorm_init()
            for li in range(L):
                ret_layer(li)
        if "MERGE" in phases:
            merge_rz()
        if "FFN2" in phases:
            moe_phase("f2")
        if "P4" in phases:
            p4_output()

    return nc, b


def _shard_in_maps(inputs):
    X = np.ascontiguousarray(inputs["L0_attn_X"])
    maps = []
    for core in range(8):
        bat, half = core >> 1, core & 1
        t = host_tables(core)
        m = {
            "x": np.ascontiguousarray(X[bat, half * TOK:(half + 1) * TOK, :]),
            "wq": inputs["L0_attn_Wq"], "wk": inputs["L0_attn_Wk"],
            "wv": inputs["L0_attn_Wv"], "wg": inputs["L0_attn_Wg"],
            "wo": inputs["L0_attn_Wo"], "w1": inputs["L0_attn_W1"],
            "w2": inputs["L0_attn_W2"],
            "ln_s": _feat_tile(inputs["L0_attn_ln_scale"]),
            "ln_b": _feat_tile(inputs["L0_attn_ln_bias"]),
            "gn_s": np.concatenate([_feat_tile(inputs["L0_attn_gn_scale"][i]) for i in range(L)], axis=1),
            "gn_b": np.concatenate([_feat_tile(inputs["L0_attn_gn_bias"][i]) for i in range(L)], axis=1),
            "gate_w": inputs["L0_ffn_gate_w"],
            "we_g": inputs["L0_ffn_w_gate"], "we_u": inputs["L0_ffn_w_up"],
            "we_d": inputs["L0_ffn_w_down"],
            "ws_g": inputs["L0_ffn_ws_gate"], "ws_u": inputs["L0_ffn_ws_up"],
            "ws_d": inputs["L0_ffn_ws_down"],
            "crot": t["crot"], "srot": t["srot"], "ddec": t["ddec"], "gq": t["gq"],
            "kscale": t["kscale"], "id_f": t["id_f"],
            "ones_row_f": t["ones_row_f"], "ones_col_f": t["ones_col_f"],
            "ehs_f": t["ehs_f"], "ebc_f": t["ebc_f"],
            "id_b": t["id_b"], "pswap_b": t["pswap_b"],
            "ones_col_b": t["ones_col_b"], "ehs_b": t["ehs_b"],
            "relay_idx": t["relay_idx"],
        }
        m = {k: (np.ascontiguousarray(v) if isinstance(v, np.ndarray) else np.ascontiguousarray(np.asarray(v))) for k, v in m.items()}
        maps.append(m)
    return maps


def kernel(**inputs):
    inputs = {k: np.asarray(v) for k, v in inputs.items()}
    maps = _shard_in_maps(inputs)
    nc, _ = build_program(ALL_PHASES)
    res = run_bass_kernel_spmd(nc, maps, core_ids=list(range(8)))
    out = np.empty((BAT, S, D), np.float32)
    for core in range(8):
        bat, half = core >> 1, core & 1
        out[bat, half * TOK:(half + 1) * TOK, :] = res.results[core]["out"]
    return out


# version 04
# speedup vs baseline: 1.1502x; 1.0164x over previous
import numpy as np
import ml_dtypes
from contextlib import ExitStack
import concourse.bass as bass
import concourse.mybir as mybir
from concourse.bass_core_rust import IndirectOffsetOnAxis
from concourse.bass_utils import run_bass_kernel_spmd

F32 = mybir.dt.float32
BF16 = mybir.dt.bfloat16
I32 = mybir.dt.int32
AF = mybir.ActivationFunctionType
ALU = mybir.AluOpType
AX = mybir.AxisListType

BAT, S, D, H, DH, FF = 4, 8192, 1024, 8, 128, 2048
NEXP = 8
L = 4
TOK = 4096
NC = 8
G = 512
NG = TOK // G
CH = 128
NCH = TOK // CH
QB = 1024
NQB = TOK // QB
EPS = 1e-5
GELU_C = float(np.sqrt(2.0 / np.pi))
GELU_A = 0.044715

ALL_PHASES = ("P0", "FFN1", "RET", "MERGE", "FFN2", "P4")
DEBUG_WPE = False
RET_PARTS = ("scan", "relay", "corr", "post", "ffn")
DEBUG_SCRATCH = False


def _gammas64():
    return 1.0 - np.exp(np.linspace(np.log(1 / 32), np.log(1 / 512), H))


def host_tables(core_id):
    gam = _gammas64()
    t = {}
    half = core_id & 1
    pos = np.arange(half * TOK, (half + 1) * TOK, dtype=np.float64)
    inv_freq = 1.0 / 10000.0 ** (np.arange(0, DH, 2, dtype=np.float64) / DH)
    ang = pos[:, None] * inv_freq[None, :]
    cos, sin = np.cos(ang), np.sin(ang)
    crot = np.empty((DH, TOK), np.float64)
    srot = np.empty((DH, TOK), np.float64)
    crot[0::2, :] = cos.T
    crot[1::2, :] = cos.T
    srot[0::2, :] = -sin.T
    srot[1::2, :] = sin.T
    t["crot"] = crot.astype(np.float32)
    t["srot"] = srot.astype(np.float32)
    jj, ii = np.meshgrid(np.arange(CH), np.arange(CH), indexing="ij")
    dd = np.zeros((CH, H * CH), np.float64)
    gq = np.zeros((CH, H * CH), np.float64)
    for h in range(H):
        m = (ii - jj).astype(np.float64)
        dd[:, h * CH:(h + 1) * CH] = np.where(m >= 0, gam[h] ** m, 0.0)
        gq[:, h * CH:(h + 1) * CH] = gam[h] ** (ii[0][None, :] + 1.0)
    t["ddec"] = dd.astype(np.float32)
    t["gq"] = gq.astype(np.float32)
    ks = np.zeros((CH, H), np.float64)
    for h in range(H):
        ks[:, h] = gam[h] ** (CH - np.arange(CH, dtype=np.float64))
    t["kscale"] = ks.astype(np.float32)
    eye = np.eye(128, dtype=np.float32)
    t["id_f"] = eye
    t["id_b"] = eye.astype(ml_dtypes.bfloat16)
    pswap = np.zeros((128, 128), np.float32)
    for d in range(128):
        pswap[d ^ 1, d] = 1.0
    t["pswap_b"] = pswap.astype(ml_dtypes.bfloat16)
    ones_col = np.ones((128, 1), np.float32)
    t["ones_col_f"] = ones_col
    t["ones_col_b"] = ones_col.astype(ml_dtypes.bfloat16)
    t["ones_row_f"] = np.ones((1, 128), np.float32)
    ehs = np.zeros((128, H * H), np.float32)
    for h in range(H):
        ehs[:, h * H + h] = 1.0
    t["ehs_b"] = ehs.astype(ml_dtypes.bfloat16)
    t["ehs_f"] = ehs
    ebc = np.zeros((H, H * 128), np.float32)
    for h in range(H):
        ebc[h, h * 128:(h + 1) * 128] = 1.0
    t["ebc_f"] = ebc
    if core_id & 1:
        idx = (np.arange(128, dtype=np.int32) * 8 + (core_id - 1) * 1024).reshape(128, 1)
    else:
        idx = np.full((128, 1), 2 ** 30, dtype=np.int32)
    t["relay_idx"] = idx
    return t


def _feat_tile(vec):
    return np.ascontiguousarray(vec.reshape(NC, 128).T)


class Bld:
    def __init__(self, nc):
        self.nc = nc
        self.sems = {}
        self.cnt = {}

    def sem(self, name):
        if name not in self.sems:
            self.sems[name] = self.nc.alloc_semaphore(name)
            self.cnt[name] = 0
        return self.sems[name]

    def inc(self, instr, name, amt=1):
        instr.then_inc(self.sem(name), amt)
        self.cnt[name] += amt
        return self.cnt[name]

    def val(self, name):
        self.sem(name)
        return self.cnt[name]

    def wait(self, eng, name, v=None):
        if v is None:
            v = self.cnt[name]
        eng.wait_ge(self.sem(name), v)


class Ring:
    def __init__(self, b: Bld, name, tiles):
        self.b = b
        self.name = name
        self.tiles = tiles
        self.n = 0
        b.sem(name)
        self.base = b.cnt[name]

    def acquire(self, eng):
        k = len(self.tiles)
        need = self.n - (k - 1)
        if need > 0:
            self.b.wait(eng, self.name, self.base + need)
        tile = self.tiles[self.n % k]
        self.n += 1
        return tile

    def release(self, instr, n=1):
        self.b.inc(instr, self.name, n)


def _rearr_cols(dram_ap, c0, cols):
    sl = dram_ap[:, c0:c0 + cols]
    return sl.rearrange("(c p) t -> p c t", p=128)


def build_program(phases=ALL_PHASES):
    nc = bass.Bass("TRN2", target_bir_lowering=False, debug=False)
    b = Bld(nc)

    din = {}
    def dram_in(name, shape):
        din[name] = nc.dram_tensor(name, list(shape), F32, kind="ExternalInput").ap()
        return din[name]

    x_in = dram_in("x", (TOK, D))
    w_q = dram_in("wq", (L, D, D))
    w_k = dram_in("wk", (L, D, D))
    w_v = dram_in("wv", (L, D, D))
    w_g = dram_in("wg", (L, D, D))
    w_o = dram_in("wo", (L, D, D))
    w_1 = dram_in("w1", (L, D, FF))
    w_2 = dram_in("w2", (L, FF, D))
    ln_s = dram_in("ln_s", (128, NC))
    ln_b = dram_in("ln_b", (128, NC))
    gn_s = dram_in("gn_s", (128, L * NC))
    gn_b = dram_in("gn_b", (128, L * NC))
    gate_w = dram_in("gate_w", (D, NEXP))
    we_g = dram_in("we_g", (NEXP, D, FF))
    we_u = dram_in("we_u", (NEXP, D, FF))
    we_d = dram_in("we_d", (NEXP, FF, D))
    ws_g = dram_in("ws_g", (D, FF))
    ws_u = dram_in("ws_u", (D, FF))
    ws_d = dram_in("ws_d", (FF, D))
    crot = dram_in("crot", (DH, TOK))
    srot = dram_in("srot", (DH, TOK))
    ddec = dram_in("ddec", (CH, H * CH))
    gq_t = dram_in("gq", (CH, H * CH))
    kscale = dram_in("kscale", (CH, H))
    id_f = dram_in("id_f", (128, 128))
    ones_row_f = dram_in("ones_row_f", (1, 128))
    ehs_f = dram_in("ehs_f", (128, H * H))
    ebc_f = dram_in("ebc_f", (H, H * 128))
    bf16_tabs = {}
    for nm, shape in [("id_b", (128, 128)), ("pswap_b", (128, 128)),
                      ("ones_col_b", (128, 1)), ("ehs_b", (128, H * H))]:
        bf16_tabs[nm] = nc.dram_tensor(nm, list(shape), BF16, kind="ExternalInput").ap()
    ones_col_f = dram_in("ones_col_f", (128, 1))
    relay_idx = nc.dram_tensor("relay_idx", [128, 1], I32, kind="ExternalInput").ap()

    out_t = nc.dram_tensor("out", [TOK, D], F32, kind="ExternalOutput").ap()
    dbg_t = nc.dram_tensor("dbg", [8, TOK], F32, kind="ExternalOutput").ap() if DEBUG_WPE else None
    dbg2_t = nc.dram_tensor("dbg2", [128, 64], F32, kind="ExternalOutput").ap() if DEBUG_WPE else None

    _scr_kind = "ExternalOutput" if DEBUG_SCRATCH else "Internal"
    r_res = nc.dram_tensor("r_res", [D, TOK], F32, kind=_scr_kind).ap()
    z_buf = nc.dram_tensor("z_buf", [D, TOK], F32, kind=_scr_kind).ap()
    y_dram = nc.dram_tensor("y_dram", [D, TOK], BF16, kind=_scr_kind).ap()
    qg_dram = nc.dram_tensor("qg_dram", [D, TOK], BF16).ap()
    relay_in = nc.dram_tensor("relay_in", [H * DH, DH], F32)
    relay_out = nc.dram_tensor("relay_out", [NC * H * DH, DH], F32)

    with ExitStack() as ctx:
        sb = lambda name, shape, dt: ctx.enter_context(nc.sbuf_tensor(name, shape, dt))
        psb = lambda name, shape, dt=F32: ctx.enter_context(nc.psum_tensor(name, shape, dt))

        te, ve, se, ge, sy = nc.tensor, nc.vector, nc.scalar, nc.gpsimd, nc.sync

        t_id_f = sb("t_id_f", [128, 128], F32)
        t_id_b = sb("t_id_b", [128, 128], BF16)
        t_pswap = sb("t_pswap", [128, 128], BF16)
        t_ones_row = sb("t_ones_row", [1, 128], F32)
        t_ones_col_f = sb("t_ones_col_f", [128, 1], F32)
        t_ones_col_b = sb("t_ones_col_b", [128, 1], BF16)
        t_ehs_b = sb("t_ehs_b", [128, H * H], BF16)
        t_ehs_f = sb("t_ehs_f", [128, H * H], F32)
        t_ebc_f = sb("t_ebc_f", [H, H * 128], F32)
        t_ddec = sb("t_ddec", [CH, H * CH], F32)
        t_gq = sb("t_gq", [CH, H * CH], F32)
        t_kscale = sb("t_kscale", [CH, H], F32)
        t_lns = sb("t_lns", [128, NC], F32)
        t_lnb = sb("t_lnb", [128, NC], F32)
        t_gns = sb("t_gns", [128, L * NC], F32)
        t_gnb = sb("t_gnb", [128, L * NC], F32)
        t_ridx = sb("t_ridx", [128, 1], I32)
        st_A = sb("st_A", [1, TOK], F32)
        st_B = sb("st_B", [1, TOK], F32)
        st_R = sb("st_R", [128, H * DH], F32)
        st_Rb = sb("st_Rb", [128, H * DH], BF16)
        st_Rrelay = sb("st_Rrelay", [128, H * DH], BF16)
        rlb_f = sb("rlb_f", [128, H * DH], F32)

        consts = sb("consts", [128, 2], F32)
        nc.const_aps.register(F32, 0.5, consts[:, 0:1])
        nc.const_aps.register(F32, 2.0, consts[:, 1:2])
        t_one2 = sb("t_one2", [128, 2], F32)
        t_neg2 = sb("t_neg2", [128, 2], F32)

        def load_tables():
            ld = [
                (t_id_f, id_f), (t_ones_row, ones_row_f), (t_ones_col_f, ones_col_f),
                (t_ehs_f, ehs_f), (t_ebc_f, ebc_f), (t_ddec, ddec), (t_gq, gq_t),
                (t_kscale, kscale), (t_lns, ln_s), (t_lnb, ln_b), (t_gns, gn_s),
                (t_gnb, gn_b),
                (t_id_b, bf16_tabs["id_b"]), (t_pswap, bf16_tabs["pswap_b"]),
                (t_ones_col_b, bf16_tabs["ones_col_b"]), (t_ehs_b, bf16_tabs["ehs_b"]),
                (t_ridx, relay_idx),
            ]
            for dst, src in ld:
                b.inc(sy.dma_start(out=dst[:], in_=src[:, :]), "ld", 16)
            ge.memset(consts[:, 0:1], 0.5)
            ge.memset(t_one2[:], 1.0)
            ge.memset(t_neg2[:], -1.0)
            b.inc(ge.memset(consts[:, 1:2], 2.0), "pool", 1)
            b.wait(sy, "ld")
            b.wait(ge, "pool")
            nc.all_engine_barrier()

        def transpose_stream(src_load, dst_store, ph, tag):
            xin = ph.enter_context(nc.sbuf_tensor(f"{tag}_xin", [128, 2 * D], F32))
            xout = ph.enter_context(nc.sbuf_tensor(f"{tag}_xout", [128, 2 * D], F32))
            tps = [ph.enter_context(nc.psum_tensor(f"{tag}_tp{i}", [128, 512], F32)) for i in range(4)]
            ring = Ring(b, f"{tag}_r", [tp[:, 0:128] for tp in tps])
            stv = []
            for tc in range(NCH):
                par = tc % 2
                xin_s = xin[:, par * D:(par + 1) * D]
                xout_s = xout[:, par * D:(par + 1) * D]
                if tc >= 2:
                    b.wait(sy, "st", stv[tc - 2])
                v = src_load(tc, xin_s)
                b.wait(te, "ld", v)
                for fc in range(NC):
                    ps = ring.acquire(te)
                    pv = b.inc(te.transpose(ps, xin_s[:, fc * 128:(fc + 1) * 128], t_id_f[:]), "pe")
                    b.wait(ve, "pe", pv)
                    if fc == 0 and tc >= 2:
                        b.wait(ve, "st", stv[tc - 2])
                    ring.release(ve.tensor_copy(xout_s[:, fc * 128:(fc + 1) * 128], ps))
                b.wait(sy, ring.name)
                dst_store(tc, xout_s)
                stv.append(b.cnt["st"])
            b.wait(sy, "st")

        def p0_ingest():
            with ExitStack() as ph:
                def src(tc, buf):
                    return b.inc(sy.dma_start(out=buf[:, :], in_=x_in[tc * 128:(tc + 1) * 128, :]), "ld", 16)
                def dst(tc, buf):
                    b.inc(sy.dma_start(
                        out=_rearr_cols(r_res, tc * 128, 128),
                        in_=buf.rearrange("p (c t) -> p c t", c=NC)), "st", 16)
                transpose_stream(src, dst, ph, "p0")
            nc.all_engine_barrier()

        def lnstats(src_dram, t0, ntok, tag):
            ngr = ntok // G
            with ExitStack() as s:
                rg = s.enter_context(nc.sbuf_tensor(f"{tag}_rg", [128, 2 * NC * G], F32))
                sq = s.enter_context(nc.sbuf_tensor(f"{tag}_sq", [128, 2 * G], F32))
                mu = s.enter_context(nc.sbuf_tensor(f"{tag}_mu", [1, ntok], F32))
                var = s.enter_context(nc.sbuf_tensor(f"{tag}_var", [1, ntok], F32))
                tmp = s.enter_context(nc.sbuf_tensor(f"{tag}_tmp", [1, ntok], F32))
                psS = s.enter_context(nc.psum_tensor(f"{tag}_psS", [128, 512], F32))
                psQ = s.enter_context(nc.psum_tensor(f"{tag}_psQ", [128, 512], F32))
                pev = [0] * ngr
                vev = [0] * ngr
                for g in range(ngr):
                    tg = t0 + g * G
                    par = g % 2
                    rgs = rg[:, par * NC * G:(par + 1) * NC * G]
                    if g >= 2:
                        b.wait(sy, "pe", pev[g - 2])
                    v = b.inc(sy.dma_start(out=rgs.rearrange("p (c t) -> p c t", c=NC),
                                           in_=_rearr_cols(src_dram, tg, G)), "ld", 16)
                    b.wait(ve, "ld", v)
                    b.wait(te, "ld", v)
                    if g >= 1:
                        b.wait(te, "vec", vev[g - 1])
                    pv = 0
                    for c in range(NC):
                        rc = rgs[:, c * G:(c + 1) * G]
                        sqs = sq[:, (c % 2) * G:(c % 2) * G + G]
                        if c >= 2 or g > 0:
                            b.wait(ve, "pe")
                        vv = b.inc(ve.tensor_mul(sqs, rc, rc), "vec")
                        b.inc(te.matmul(psS[0:1, :], t_ones_col_f[:], rc,
                                        start=(c == 0), stop=(c == NC - 1)), "pe")
                        b.wait(te, "vec", vv)
                        pv = b.inc(te.matmul(psQ[0:1, :], t_ones_col_f[:], sqs,
                                             start=(c == 0), stop=(c == NC - 1)), "pe")
                    pev[g] = pv
                    b.wait(ve, "pe", pv)
                    o = tg - t0
                    b.inc(ve.tensor_scalar_mul(mu[0:1, o:o + G], psS[0:1, :], 1.0 / D), "vec")
                    b.inc(ve.tensor_mul(tmp[0:1, o:o + G], mu[0:1, o:o + G], mu[0:1, o:o + G]), "vec")
                    b.inc(ve.tensor_scalar_mul(var[0:1, o:o + G], psQ[0:1, :], 1.0 / D), "vec")
                    vev[g] = b.inc(ve.tensor_sub(var[0:1, o:o + G], var[0:1, o:o + G], tmp[0:1, o:o + G]), "vec")
                b.inc(ve.tensor_scalar_add(var[0:1, 0:ntok], var[0:1, 0:ntok], EPS), "vec")
                vv = b.inc(ve.reciprocal(tmp[0:1, 0:ntok], var[0:1, 0:ntok]), "vec")
                b.wait(se, "vec", vv)
                av = b.inc(se.activation(st_A[0:1, t0:t0 + ntok], tmp[0:1, 0:ntok], AF.Sqrt), "act")
                b.wait(ve, "act", av)
                bv = b.inc(ve.scalar_tensor_tensor(st_B[0:1, t0:t0 + ntok], mu[0:1, 0:ntok], -1.0,
                                                   st_A[0:1, t0:t0 + ntok], ALU.mult, ALU.mult), "vec")
                b.wait(se, "vec", bv)
            return bv

        def moe_phase(tag):
            with ExitStack() as ph:
                xn = ph.enter_context(nc.sbuf_tensor(f"{tag}_xn", [128, NC * QB], BF16))
                yac = ph.enter_context(nc.sbuf_tensor(f"{tag}_yac", [128, NC * QB], F32))
                wpeT = ph.enter_context(nc.sbuf_tensor(f"{tag}_wpeT", [8, QB], F32))
                gwt = ph.enter_context(nc.sbuf_tensor(f"{tag}_gwt", [128, NC * 8], F32))
                wrow = ph.enter_context(nc.sbuf_tensor(f"{tag}_wrow", [128, QB], BF16))
                v = b.inc(sy.dma_start(out=gwt.rearrange("p (c e) -> p c e", c=NC),
                                       in_=gate_w.rearrange("(c p) e -> p c e", p=128)), "ld", 16)
                b.wait(te, "ld", v)
                for q in range(NQB):
                    t0 = q * QB
                    lnstats(r_res, t0, QB, f"{tag}q{q}s")
                    nc.all_engine_barrier()
                    _moe_a2(tag, q, xn, wpeT, gwt)
                    nc.all_engine_barrier()
                    if DEBUG_WPE and tag == "f1":
                        b.inc(sy.dma_start(out=dbg_t[:, t0:t0 + QB], in_=wpeT[:, :]), "st", 16)
                        b.wait(sy, "st")
                    _moe_experts(tag, q, xn, yac, wpeT, wrow)
                    nc.all_engine_barrier()
                    _moe_cadd(tag, q, yac)
                    nc.all_engine_barrier()

        def _moe_a2(tag, q, xn, wpeT, gwt):
            t0 = q * QB
            with ExitStack() as s:
                rg = s.enter_context(nc.sbuf_tensor(f"{tag}a2rg_{q}", [128, NC * G], F32))
                tmpa = s.enter_context(nc.sbuf_tensor(f"{tag}a2ta_{q}", [128, 2 * G], F32))
                tmpf = s.enter_context(nc.sbuf_tensor(f"{tag}a2tf_{q}", [128, 2 * G], F32))
                scsb = s.enter_context(nc.sbuf_tensor(f"{tag}a2sc_{q}", [8, G], F32))
                tk = s.enter_context(nc.sbuf_tensor(f"{tag}a2tk_{q}", [128, 6 * 8], F32))
                sm = s.enter_context(nc.sbuf_tensor(f"{tag}a2sm_{q}", [128, 48], F32))
                psA = s.enter_context(nc.psum_tensor(f"{tag}a2pA_{q}", [128, 512], F32))
                psB = s.enter_context(nc.psum_tensor(f"{tag}a2pB_{q}", [128, 512], F32))
                psS = s.enter_context(nc.psum_tensor(f"{tag}a2pS_{q}", [128, 512], F32))
                psT = s.enter_context(nc.psum_tensor(f"{tag}a2pT_{q}", [128, 512], F32))
                psW = s.enter_context(nc.psum_tensor(f"{tag}a2pW_{q}", [128, 512], F32))
                for g2 in range(2):
                    tg = t0 + g2 * G
                    if g2 > 0:
                        b.wait(sy, "pe")
                        b.wait(sy, "vec")
                    v = b.inc(sy.dma_start(out=rg.rearrange("p (c t) -> p c t", c=NC),
                                           in_=_rearr_cols(r_res, tg, G)), "ld", 16)
                    if g2 > 0:
                        b.wait(te, "vec")
                    bca = b.inc(te.matmul(psA[:, 0:G], t_ones_row[0:1, :], st_A[0:1, tg:tg + G],
                                          start=True, stop=True), "pe")
                    bcb = b.inc(te.matmul(psB[:, 0:G], t_ones_row[0:1, :], st_B[0:1, tg:tg + G],
                                          start=True, stop=True), "pe")
                    b.wait(ve, "ld", v)
                    b.wait(ve, "pe", bcb)
                    pv_gate = 0
                    for c in range(NC):
                        sl = c % 2
                        ta = tmpa[:, sl * G:(sl + 1) * G]
                        tf = tmpf[:, sl * G:(sl + 1) * G]
                        if c >= 2:
                            b.wait(ve, "pe")
                        b.inc(ve.tensor_mul(ta, rg[:, c * G:(c + 1) * G], psA[:, 0:G]), "vec")
                        fv = b.inc(ve.tensor_add(tf, ta, psB[:, 0:G]), "vec")
                        b.inc(ve.tensor_copy(xn[:, c * QB + g2 * G: c * QB + g2 * G + G], tf), "vec")
                        b.wait(te, "vec", fv)
                        pv_gate = b.inc(te.matmul(psS[0:8, 0:G], gwt[:, c * 8:(c + 1) * 8], tf,
                                                  start=(c == 0), stop=(c == NC - 1)), "pe")
                    b.wait(ve, "pe", pv_gate)
                    sv = b.inc(ve.tensor_copy(scsb[:, :], psS[0:8, 0:G]), "vec")
                    for tc in range(4):
                        b.wait(te, "vec")
                        ptv = b.inc(te.transpose(psT[:, 0:8], scsb[:, tc * 128:(tc + 1) * 128],
                                                 t_id_f[0:8, 0:8]), "pe")
                        b.wait(ve, "pe", ptv)
                        def nvw(instr):
                            vv = b.inc(instr, "vec")
                            b.wait(ve, "vec", vv)
                            return vv
                        toksc, esb = tk[:, 0:8], tk[:, 8:16]
                        scn, sc2 = tk[:, 16:24], tk[:, 24:32]
                        mk1, mk2 = tk[:, 32:40], tk[:, 40:48]
                        t4 = sm[:, 0:4]
                        t2 = sm[:, 4:6]
                        mx2 = sm[:, 8:10]
                        nm2 = sm[:, 12:14]
                        zr2 = sm[:, 16:18]
                        m1_2 = sm[:, 20:22]
                        m2_2 = sm[:, 24:26]
                        den2 = sm[:, 28:30]
                        wr2 = sm[:, 32:34]
                        w1_2 = sm[:, 36:38]
                        w2_2 = sm[:, 40:42]

                        def rowmax(src8, out2):
                            nvw(ve.tensor_max(t4, src8[:, 0:4], src8[:, 4:8]))
                            nvw(ve.tensor_max(t2, t4[:, 0:2], t4[:, 2:4]))
                            nvw(ve.tensor_single_scalar(out2, t2, t2[:, 1:2], ALU.max))

                        def rowsum(src8, out2):
                            nvw(ve.tensor_add(t4, src8[:, 0:4], src8[:, 4:8]))
                            nvw(ve.tensor_add(t2, t4[:, 0:2], t4[:, 2:4]))
                            nvw(ve.tensor_single_scalar(out2, t2, t2[:, 1:2], ALU.add))

                        nvw(ve.tensor_copy(toksc, psT[:, 0:8]))
                        rowmax(toksc, mx2)
                        nv = nvw(ve.tensor_single_scalar(nm2, t_neg2[:], mx2[:, 0:1], ALU.mult))
                        b.wait(se, "vec", nv)
                        av = b.inc(se.activation(esb, toksc, AF.Exp, bias=nm2[:, 0:1]), "act")
                        b.wait(ve, "act", av)
                        rowsum(esb, zr2)
                        nvw(ve.reciprocal(zr2, zr2))
                        nvw(ve.tensor_scalar_mul(scn, esb, zr2[:, 0:1]))
                        rowmax(scn, m1_2)
                        nvw(ve.tensor_single_scalar(mk1, scn, m1_2[:, 0:1], ALU.is_ge))
                        nvw(ve.tensor_scalar(sc2, mk1, -1.0, 1.0, ALU.mult, ALU.add))
                        nvw(ve.tensor_mul(sc2, scn, sc2))
                        rowmax(sc2, m2_2)
                        nvw(ve.tensor_single_scalar(mk2, sc2, m2_2[:, 0:1], ALU.is_ge))
                        nvw(ve.tensor_single_scalar(den2, t_one2[:], m1_2[:, 0:1], ALU.mult))
                        nvw(ve.tensor_single_scalar(den2, den2, m2_2[:, 0:1], ALU.add))
                        nvw(ve.tensor_scalar_add(den2, den2, 1e-20))
                        nvw(ve.reciprocal(wr2, den2))
                        nvw(ve.tensor_scalar_mul(wr2, wr2, 0.5))
                        nvw(ve.tensor_single_scalar(w1_2, wr2, m1_2[:, 0:1], ALU.mult))
                        nvw(ve.tensor_single_scalar(w2_2, wr2, m2_2[:, 0:1], ALU.mult))
                        nvw(ve.tensor_scalar_mul(toksc, mk1, w1_2[:, 0:1]))
                        wv = nvw(ve.scalar_tensor_tensor(toksc, mk2, w2_2[:, 0:1], toksc,
                                                         ALU.mult, ALU.add))
                        b.wait(te, "vec", wv)
                        pwv = b.inc(te.transpose(psW[0:8, 0:128], toksc, t_id_f[:, :]), "pe")
                        b.wait(ve, "pe", pwv)
                        b.inc(ve.tensor_copy(wpeT[:, g2 * G + tc * 128: g2 * G + (tc + 1) * 128],
                                             psW[0:8, 0:128]), "vec")
                b.wait(se, "vec")
                b.wait(te, "vec")

        def _w_col_block(base_ap, e, shared_ap, c0, cols):
            if e < NEXP:
                sl = base_ap[e, :, c0:c0 + cols]
            else:
                sl = shared_ap[:, c0:c0 + cols]
            return sl.rearrange("(c p) d -> p c d", p=128)

        def _moe_experts(tag, q, xn, yac, wpeT, wrow):
            with ExitStack() as s:
                wgb = s.enter_context(nc.sbuf_tensor(f"{tag}wg_{q}", [128, 3 * NC * 128], BF16))
                wub = s.enter_context(nc.sbuf_tensor(f"{tag}wu_{q}", [128, 3 * NC * 128], BF16))
                wdb = s.enter_context(nc.sbuf_tensor(f"{tag}wd_{q}", [128, 3 * 16 * 128], BF16))
                hid = s.enter_context(nc.sbuf_tensor(f"{tag}hid_{q}", [128, 16 * QB], BF16))
                sgb = s.enter_context(nc.sbuf_tensor(f"{tag}sg_{q}", [128, 2 * 512], F32))
                h1b = s.enter_context(nc.sbuf_tensor(f"{tag}h1_{q}", [128, 2 * 512], F32))
                psgu = [s.enter_context(nc.psum_tensor(f"{tag}pgu{i}_{q}", [128, 512], F32)) for i in range(4)]
                psd = [s.enter_context(nc.psum_tensor(f"{tag}pd{i}_{q}", [128, 512], F32)) for i in range(2)]
                pswr = s.enter_context(nc.psum_tensor(f"{tag}pwr_{q}", [128, 512], F32))
                ringGU = Ring(b, f"{tag}rgu_{q}", [p[:, 0:512] for p in psgu])
                ringD = Ring(b, f"{tag}rd_{q}", [p[:, 0:512] for p in psd])
                wrl = 0
                slot_gu = [0, 0, 0]
                slot_d = [0, 0, 0]
                for e in range(NEXP + 1):
                    if e == NEXP:
                        wrl = b.inc(ve.memset(wrow[:, :], 0.5), "vec")
                    else:
                        for g2 in range(2):
                            if e > 0 or g2 > 0:
                                b.wait(te, "vec", wrl)
                            pv = b.inc(te.matmul(pswr[:, 0:G], t_ebc_f[:, e * 128:(e + 1) * 128],
                                                 wpeT[0:8, g2 * G:(g2 + 1) * G], start=True, stop=True), "pe")
                            b.wait(ve, "pe", pv)
                            wrl = b.inc(ve.tensor_copy(wrow[:, g2 * G:(g2 + 1) * G], pswr[:, 0:G]), "vec")
                    wuse = {}
                    hid_last = 0
                    for dc in range(16):
                        sl = dc % 3
                        wg_s = wgb[:, sl * NC * 128:(sl + 1) * NC * 128]
                        wu_s = wub[:, sl * NC * 128:(sl + 1) * NC * 128]
                        if slot_gu[sl]:
                            b.wait(ge, "pe", slot_gu[sl])
                        b.inc(ge.dma_start(out=wg_s.rearrange("p (c d) -> p c d", d=128),
                                           in_=_w_col_block(we_g, e, ws_g, dc * 128, 128)), "wld", 16)
                        lv = b.inc(ge.dma_start(out=wu_s.rearrange("p (c d) -> p c d", d=128),
                                                in_=_w_col_block(we_u, e, ws_u, dc * 128, 128)), "wld", 16)
                        for ts in range(2):
                            pg = ringGU.acquire(te)
                            pu = ringGU.acquire(te)
                            b.wait(te, "wld", lv)
                            pgv = puv = 0
                            for c in range(NC):
                                pgv = b.inc(te.matmul(pg, wg_s[:, c * 128:(c + 1) * 128],
                                                      xn[:, c * QB + ts * 512: c * QB + ts * 512 + 512],
                                                      start=(c == 0), stop=(c == NC - 1)), "pe")
                            for c in range(NC):
                                puv = b.inc(te.matmul(pu, wu_s[:, c * 128:(c + 1) * 128],
                                                      xn[:, c * QB + ts * 512: c * QB + ts * 512 + 512],
                                                      start=(c == 0), stop=(c == NC - 1)), "pe")
                            wuse[dc] = puv
                            slot_gu[sl] = puv
                            sgs = sgb[:, ts * 512:(ts + 1) * 512]
                            h1s = h1b[:, ts * 512:(ts + 1) * 512]
                            b.wait(se, "pe", pgv)
                            b.wait(se, "vec", hid_last)
                            av = b.inc(se.activation(sgs, pg, AF.Silu), "act")
                            b.wait(ve, "act", av)
                            b.wait(ve, "pe", puv)
                            h1i = ve.tensor_mul(h1s, sgs, pu)
                            ringGU.release(h1i, 2)
                            hidi = ve.tensor_mul(hid[:, dc * QB + ts * 512: dc * QB + ts * 512 + 512],
                                                 h1s, wrow[:, ts * 512:(ts + 1) * 512])
                            hv = b.inc(hidi, "vec")
                            hid_last = hv
                    b.wait(te, "vec", hid_last)
                    wduse = {}
                    for o2 in range(NC):
                        sl = o2 % 3
                        wd_s = wdb[:, sl * 16 * 128:(sl + 1) * 16 * 128]
                        if slot_d[sl]:
                            b.wait(ge, "pe", slot_d[sl])
                        lv = b.inc(ge.dma_start(out=wd_s.rearrange("p (c d) -> p c d", d=128),
                                                in_=_w_col_block(we_d, e, ws_d, o2 * 128, 128)), "wld", 16)
                        for ts in range(2):
                            pd = ringD.acquire(te)
                            b.wait(te, "wld", lv)
                            pdv = 0
                            for dc in range(16):
                                pdv = b.inc(te.matmul(pd, wd_s[:, dc * 128:(dc + 1) * 128],
                                                      hid[:, dc * QB + ts * 512: dc * QB + ts * 512 + 512],
                                                      start=(dc == 0), stop=(dc == 15)), "pe")
                            wduse[o2] = pdv
                            slot_d[sl] = pdv
                            b.wait(ve, "pe", pdv)
                            ysl = yac[:, o2 * QB + ts * 512: o2 * QB + ts * 512 + 512]
                            if e == 0:
                                yi = ve.tensor_copy(ysl, pd)
                            else:
                                yi = ve.tensor_add(ysl, ysl, pd)
                            ringD.release(yi)

        def _moe_cadd(tag, q, yac):
            t0 = q * QB
            with ExitStack() as s:
                rg = s.enter_context(nc.sbuf_tensor(f"{tag}crg_{q}", [128, NC * G], F32))
                for g2 in range(2):
                    tg = t0 + g2 * G
                    if g2 > 0:
                        b.wait(sy, "st")
                        b.wait(sy, "vec")
                    v = b.inc(sy.dma_start(out=rg.rearrange("p (c t) -> p c t", c=NC),
                                           in_=_rearr_cols(r_res, tg, G)), "ld", 16)
                    b.wait(ve, "ld", v)
                    vv = 0
                    for c in range(NC):
                        vv = b.inc(ve.tensor_add(rg[:, c * G:(c + 1) * G], rg[:, c * G:(c + 1) * G],
                                                 yac[:, c * QB + g2 * G: c * QB + g2 * G + G]), "vec")
                    b.wait(sy, "vec", vv)
                    b.inc(sy.dma_start(out=_rearr_cols(r_res, tg, G),
                                       in_=rg.rearrange("p (c t) -> p c t", c=NC)), "st", 16)
                b.wait(sy, "st")

        def znorm_init():
            lnstats(r_res, 0, TOK, "zi")
            nc.all_engine_barrier()
            with ExitStack() as s:
                zg = s.enter_context(nc.sbuf_tensor("zi_zg", [128, NC * G], F32))
                zo = s.enter_context(nc.sbuf_tensor("zi_zo", [128, NC * G], F32))
                psA = s.enter_context(nc.psum_tensor("zi_pA", [128, 512], F32))
                psB = s.enter_context(nc.psum_tensor("zi_pB", [128, 512], F32))
                stv = 0
                for g in range(NG):
                    tg = g * G
                    if g > 0:
                        b.wait(sy, "st", stv)
                        b.wait(sy, "vec")
                    v = b.inc(sy.dma_start(out=zg.rearrange("p (c t) -> p c t", c=NC),
                                           in_=_rearr_cols(r_res, tg, G)), "ld", 16)
                    if g > 0:
                        b.wait(te, "vec")
                    ba = b.inc(te.matmul(psA[:, 0:G], t_ones_row[0:1, :], st_A[0:1, tg:tg + G],
                                         start=True, stop=True), "pe")
                    bb2 = b.inc(te.matmul(psB[:, 0:G], t_ones_row[0:1, :], st_B[0:1, tg:tg + G],
                                          start=True, stop=True), "pe")
                    b.wait(ve, "ld", v)
                    b.wait(ve, "pe", bb2)
                    vv = 0
                    for c in range(NC):
                        b.inc(ve.tensor_mul(zo[:, c * G:(c + 1) * G], zg[:, c * G:(c + 1) * G], psA[:, 0:G]), "vec")
                        vv = b.inc(ve.tensor_add(zo[:, c * G:(c + 1) * G], zo[:, c * G:(c + 1) * G], psB[:, 0:G]), "vec")
                    b.wait(sy, "vec", vv)
                    stv = b.inc(sy.dma_start(out=_rearr_cols(z_buf, tg, G),
                                             in_=zo.rearrange("p (c t) -> p c t", c=NC)), "st", 16)
                b.wait(sy, "st")
            nc.all_engine_barrier()

        def _xn_build(zg, xn_g, psA, psB, tg, scr, scaled=True):
            b.inc(te.matmul(psA[:, 0:G], t_ones_row[0:1, :], st_A[0:1, tg:tg + G],
                            start=True, stop=True), "pe")
            bb2 = b.inc(te.matmul(psB[:, 0:G], t_ones_row[0:1, :], st_B[0:1, tg:tg + G],
                                  start=True, stop=True), "pe")
            b.wait(ve, "pe", bb2)
            vv = 0
            for c in range(NC):
                t1 = xn_g[:, c * G:(c + 1) * G]
                b.inc(ve.tensor_mul(scr[:, 0:G], zg[:, c * G:(c + 1) * G], psA[:, 0:G]), "vec")
                b.inc(ve.tensor_add(scr[:, 0:G], scr[:, 0:G], psB[:, 0:G]), "vec")
                if scaled:
                    vv = b.inc(ve.tensor_scalar(t1, scr[:, 0:G], t_lns[:, c:c + 1],
                                                t_lnb[:, c:c + 1], ALU.mult, ALU.add), "vec")
                else:
                    vv = b.inc(ve.tensor_copy(t1, scr[:, 0:G]), "vec")
            return vv

        def ret_layer(li):
            lnstats(z_buf, 0, TOK, f"r{li}s")
            nc.all_engine_barrier()
            b.inc(ve.memset(st_R[:], 0.0), "vec")
            b.inc(ve.memset(st_Rb[:], 0.0), "vec")
            gamC = [float(_gammas64()[h] ** CH) for h in range(H)]

            with ExitStack() as s:
                wq_sb = s.enter_context(nc.sbuf_tensor(f"r{li}_wq", [128, NC * D], BF16))
                wk_sb = s.enter_context(nc.sbuf_tensor(f"r{li}_wk", [128, NC * D], BF16))
                wv_sb = s.enter_context(nc.sbuf_tensor(f"r{li}_wv", [128, NC * D], BF16))
                zg2 = s.enter_context(nc.sbuf_tensor(f"r{li}_zg", [128, 2 * NC * G], F32))
                xn2 = s.enter_context(nc.sbuf_tensor(f"r{li}_xn", [128, NC * G], BF16))
                crg2 = s.enter_context(nc.sbuf_tensor(f"r{li}_cr", [128, G], F32))
                srg2 = s.enter_context(nc.sbuf_tensor(f"r{li}_sr", [128, G], F32))
                rotq = s.enter_context(nc.sbuf_tensor(f"r{li}_rq", [128, H * G], BF16))
                rotk = s.enter_context(nc.sbuf_tensor(f"r{li}_rk", [128, H * G], BF16))
                v_tok = s.enter_context(nc.sbuf_tensor(f"r{li}_vt", [128, 2 * D], BF16))
                y_grp = s.enter_context(nc.sbuf_tensor(f"r{li}_yg", [128, H * G], BF16))
                evt = s.enter_context(nc.sbuf_tensor(f"r{li}_ev", [128, 2 * G], F32))
                evc = s.enter_context(nc.sbuf_tensor(f"r{li}_ec", [128, 2 * G], F32))
                attd_all = s.enter_context(nc.sbuf_tensor(f"r{li}_ad", [128, H * CH], BF16))
                qg_all = s.enter_context(nc.sbuf_tensor(f"r{li}_qg", [128, H * G], BF16))
                kts2 = s.enter_context(nc.sbuf_tensor(f"r{li}_kt", [128, 2 * CH], BF16))
                ba = [s.enter_context(nc.psum_tensor(f"r{li}_ba{i}", [128, 512], F32)) for i in range(2)]
                by = [s.enter_context(nc.psum_tensor(f"r{li}_by{i}", [128, 512], F32)) for i in range(2)]
                bd = [s.enter_context(nc.psum_tensor(f"r{li}_bd{i}", [128, 512], F32)) for i in range(2)]
                psV = s.enter_context(nc.psum_tensor(f"r{li}_pV", [128, 512], F32))
                psKT_b = s.enter_context(nc.psum_tensor(f"r{li}_pKT", [128, 1024], BF16))
                for w_dram, wsb in [(w_q, wq_sb), (w_k, wk_sb), (w_v, wv_sb)]:
                    b.inc(ge.dma_start(out=wsb.rearrange("p (c d) -> p c d", d=D),
                                       in_=w_dram[li].rearrange("(c p) d -> p c d", p=128)), "wld", 16)
                wldv = b.val("wld")
                b.wait(te, "wld", wldv)

                ldv = [0] * NG
                gv_pe = [0] * NG
                gv_vec = [0] * NG
                rbv_prev = [0] * H

                def emit_loads(g):
                    sl = g % 2
                    if g >= 2:
                        b.wait(sy, "pe", gv_pe[g - 2])
                        b.wait(sy, "vec", gv_vec[g - 2])
                    tg = g * G
                    ldv[g] = b.inc(sy.dma_start(out=zg2[:, sl * NC * G:(sl + 1) * NC * G].rearrange("p (c t) -> p c t", c=NC),
                                                in_=_rearr_cols(z_buf, tg, G)), "ld", 16)

                def emit_cs_loads(g):
                    tg = g * G
                    if g > 0:
                        b.wait(sy, "vec")
                    b.inc(sy.dma_start(out=crg2[:, 0:G], in_=crot[:, tg:tg + G]), "ld", 16)
                    csld[0] = b.inc(sy.dma_start(out=srg2[:, 0:G], in_=srot[:, tg:tg + G]), "ld", 16)

                csld = [0]
                emit_loads(0)
                stv_y = 0
                ybank = [0, 0]
                rbp = [0] * H
                qgst = [0]
                ba_free = [(None, 0), (None, 0)]
                for g in (range(NG) if "scan" in RET_PARTS else []):
                    sl = g % 2
                    tg = g * G
                    zg = zg2[:, sl * NC * G:(sl + 1) * NC * G]
                    xn_g = xn2[:, 0:NC * G]
                    crg = crg2[:, 0:G]
                    srg = srg2[:, 0:G]
                    emit_cs_loads(g)
                    b.wait(ve, "ld", ldv[g])
                    b.wait(ve, "ld", csld[0])
                    if ybank[0]:
                        b.wait(te, "act", max(ybank))
                    _xn_build(zg, xn_g, by[0], by[1], tg, evt)
                    xnv = b.val("vec")
                    b.wait(te, "vec", xnv)
                    for (wsb, rot, sign) in [(wq_sb, rotq, 1.0), (wk_sb, rotk, -1.0)]:
                        evh = [0, 0]
                        rvh = [0, 0]
                        pqb = [0, 0]
                        swv = 0
                        for h in range(H):
                            p = h % 2
                            if evh[p]:
                                b.wait(te, "act", evh[p])
                            elif ba_free[p][0] is not None:
                                b.wait(te, ba_free[p][0], ba_free[p][1])
                            pv = 0
                            for c in range(NC):
                                pv = b.inc(te.matmul(ba[p][:, 0:G], wsb[:, c * D + h * 128: c * D + (h + 1) * 128],
                                                     xn_g[:, c * G:(c + 1) * G],
                                                     start=(c == 0), stop=(c == NC - 1)), "pe")
                            b.wait(se, "pe", pv)
                            if rvh[p]:
                                b.wait(se, "vec", rvh[p])
                                b.wait(se, "pool", pqb[p])
                            evh[p] = b.inc(se.copy(evt[:, p * G:(p + 1) * G], ba[p][:, 0:G]), "act")
                            ba_free[p] = ("act", evh[p])
                            b.wait(ge, "act", evh[p])
                            pqb[p] = b.inc(ge.tensor_copy(rot[:, h * G:(h + 1) * G], evt[:, p * G:(p + 1) * G]), "pool")
                            b.inc(ge.tensor_mul(evc[:, p * G:(p + 1) * G], evt[:, p * G:(p + 1) * G], crg), "pool")
                            pqb[p] = b.val("pool")
                            b.wait(te, "pool", pqb[p])
                            if swv:
                                b.wait(te, "vec", swv)
                            ppv = b.inc(te.matmul(psV[:, 0:G], t_pswap[:], rot[:, h * G:(h + 1) * G],
                                                  start=True, stop=True), "pe")
                            b.wait(ve, "pe", ppv)
                            b.wait(ve, "pool", pqb[p])
                            swv = b.inc(ve.tensor_mul(evt[:, p * G:(p + 1) * G], psV[:, 0:G], srg), "vec")
                            if sign > 0:
                                rvh[p] = b.inc(ve.tensor_add(rot[:, h * G:(h + 1) * G],
                                                             evc[:, p * G:(p + 1) * G],
                                                             evt[:, p * G:(p + 1) * G]), "vec")
                            else:
                                rvh[p] = b.inc(ve.tensor_sub(rot[:, h * G:(h + 1) * G],
                                                             evc[:, p * G:(p + 1) * G],
                                                             evt[:, p * G:(p + 1) * G]), "vec")
                        b.wait(te, "vec", rvh[0])
                        b.wait(te, "vec", rvh[1])
                    qgv_h = [0] * H
                    b.wait(ge, "vec")
                    b.wait(ge, "st", qgst[0])
                    for h in range(H):
                        a = rotq[:, h * G:(h + 1) * G].rearrange("p (c t) -> p c t", t=CH)
                        gqb = t_gq[:, h * CH:(h + 1) * CH].rearrange("p (c t) -> p c t", t=CH)
                        a2, b2 = bass.broadcast_tensor_aps(a, gqb)
                        qgv_h[h] = b.inc(ge.tensor_tensor(
                            qg_all[:, h * G:(h + 1) * G].rearrange("p (c t) -> p c t", t=CH),
                            a2, b2, ALU.mult), "pool")
                    b.wait(sy, "pool", qgv_h[H - 1])
                    qgst[0] = b.inc(sy.dma_start(out=_rearr_cols(qg_dram, tg, G),
                                                 in_=qg_all.rearrange("p (h t) -> p h t", h=H)), "st", 16)
                    ksv_last = 0
                    for tc in range(4):
                        tsl = tc % 2
                        vts = v_tok[:, tsl * D:(tsl + 1) * D]
                        vev = 0
                        for dv in range(2):
                            if dv == 1:
                                b.wait(te, "act", vev)
                            pvv = 0
                            for c in range(NC):
                                pvv = b.inc(te.matmul(psV[:, 0:G], xn_g[:, c * G + tc * 128: c * G + tc * 128 + 128],
                                                      wv_sb[:, c * D + dv * 512: c * D + (dv + 1) * 512],
                                                      start=(c == 0), stop=(c == NC - 1)), "pe")
                            b.wait(se, "pe", pvv)
                            vev = b.inc(se.copy(vts[:, dv * 512:(dv + 1) * 512], psV[:, 0:G]), "act")
                        b.wait(te, "act", vev)
                        adv = [0] * H
                        for h in range(H):
                            p = h % 2
                            rq_c = rotq[:, h * G + tc * 128: h * G + (tc + 1) * 128]
                            rk_c = rotk[:, h * G + tc * 128: h * G + (tc + 1) * 128]
                            if h >= 2:
                                b.wait(te, "vec", adv[h - 2])
                            elif tc == 0:
                                b.wait(te, ba_free[p][0], ba_free[p][1])
                            atv = b.inc(te.matmul(ba[p][:, 0:128], rk_c, rq_c, start=True, stop=True), "pe")
                            b.wait(ve, "pe", atv)
                            adv[h] = b.inc(ve.tensor_mul(attd_all[:, h * CH:(h + 1) * CH], ba[p][:, 0:128],
                                                         t_ddec[:, h * CH:(h + 1) * CH]), "vec")
                            ba_free[p] = ("vec", adv[h])
                        for h in range(H):
                            p = h % 2
                            vt_h = vts[:, h * 128:(h + 1) * 128]
                            if ybank[p]:
                                b.wait(te, "act", ybank[p])
                            b.wait(te, "vec", adv[h])
                            b.wait(te, "pool", max(qgv_h[h], rbp[h]))
                            b.inc(te.matmul(by[p][:, 0:128], vt_h, attd_all[:, h * CH:(h + 1) * CH],
                                            start=True, stop=False), "pe")
                            yv = b.inc(te.matmul(by[p][:, 0:128], st_Rb[:, h * 128:(h + 1) * 128],
                                                 qg_all[:, h * G + tc * 128: h * G + (tc + 1) * 128],
                                                 start=False, stop=True), "pe")
                            b.wait(se, "pe", yv)
                            ybank[p] = b.inc(se.copy(y_grp[:, h * G + tc * 128: h * G + (tc + 1) * 128],
                                                     by[p][:, 0:128]), "act")
                        ksv = [0, 0]
                        ruv = [0] * H
                        for h in range(H):
                            p = h % 2
                            rk_c = rotk[:, h * G + tc * 128: h * G + (tc + 1) * 128]
                            vt_h = vts[:, h * 128:(h + 1) * 128]
                            if ksv_last:
                                b.wait(te, "vec", ksv_last)
                            ktv = b.inc(te.transpose(psKT_b[:, 0:128], rk_c, t_id_b[:]), "pe")
                            b.wait(ve, "pe", ktv)
                            ksv[p] = b.inc(ve.tensor_scalar_mul(kts2[:, p * CH:(p + 1) * CH],
                                                                psKT_b[:, 0:128], t_kscale[:, h:h + 1]), "vec")
                            ksv_last = ksv[p]
                            if h >= 2:
                                b.wait(te, "vec", ruv[h - 2])
                            b.wait(te, "vec", ksv[p])
                            drv = b.inc(te.matmul(bd[p][:, 0:128], kts2[:, p * CH:(p + 1) * CH], vt_h,
                                                  start=True, stop=True), "pe")
                            b.wait(ve, "pe", drv)
                            b.wait(ve, "pool", rbp[h])
                            ruv[h] = b.inc(ve.scalar_tensor_tensor(st_R[:, h * 128:(h + 1) * 128],
                                                                   st_R[:, h * 128:(h + 1) * 128], gamC[h],
                                                                   bd[p][:, 0:128], ALU.mult, ALU.add), "vec")
                            b.wait(ge, "vec", ruv[h])
                            rbp[h] = b.inc(ge.tensor_copy(st_Rb[:, h * 128:(h + 1) * 128],
                                                          st_R[:, h * 128:(h + 1) * 128]), "pool")
                    gv_pe[g] = b.val("pe")
                    gv_vec[g] = b.val("vec")
                    if g + 1 < NG:
                        emit_loads(g + 1)
                    b.wait(sy, "act")
                    stv_y = b.inc(sy.dma_start(out=_rearr_cols(y_dram, tg, G),
                                               in_=y_grp.rearrange("p (h t) -> p h t", h=H)), "st", 16)
                    b.wait(se, "st", stv_y)
                b.wait(sy, "st")
                nc.all_engine_barrier()

                if "relay" in RET_PARTS:
                    b.inc(ge.dma_start(out=relay_in.ap().rearrange("(q h) v -> q h v", h=H),
                                       in_=st_R.rearrange("p (h v) -> p h v", v=128)), "wld", 16)
                    b.wait(ge, "wld")
                    ccv = b.inc(ge.collective_compute(
                        "AllGather", ALU.bypass, replica_groups=[list(range(8))],
                        ins=[relay_in.ap().opt()], outs=[relay_out.ap().opt()]), "cc", 1)
                    b.inc(ge.memset(rlb_f[:], 0.0), "pool")
                    b.wait(ge, "cc", ccv)
                    gvx = b.inc(ge.indirect_dma_start(
                        out=rlb_f[:, :], out_offset=None,
                        in_=relay_out.ap()[:, :],
                        in_offset=IndirectOffsetOnAxis(t_ridx[:, 0:1], 0),
                        bounds_check=NC * H * DH - 1, oob_is_err=False), "wld", 16)
                    b.wait(ve, "wld", gvx)
                    b.wait(ve, "pool")
                    b.inc(ve.tensor_copy(st_Rrelay[:], rlb_f[:]), "vec")
                    nc.all_engine_barrier()
                else:
                    b.inc(ve.memset(st_Rrelay[:], 0.0), "vec")

                ldv = [0] * NG
                gv_pe = [0] * NG
                gv_vec = [0] * NG
                yldv = [0] * NG
                ystv = [0] * NG

                def emit_loads_c(g):
                    sl = g % 2
                    if g >= 2:
                        b.wait(sy, "pe", gv_pe[g - 2])
                        b.wait(sy, "vec", gv_vec[g - 2])
                    tg = g * G
                    ldv[g] = b.inc(sy.dma_start(out=zg2[:, sl * NC * G:(sl + 1) * NC * G].rearrange("p (c t) -> p c t", c=NC),
                                                in_=_rearr_cols(z_buf, tg, G)), "ld", 16)

                def emit_yload_c(g):
                    yldv[g] = b.inc(sy.dma_start(out=y_grp.rearrange("p (h t) -> p h t", h=H),
                                                 in_=_rearr_cols(y_dram, g * G, G)), "ld", 16)

                emit_loads_c(0)
                emit_yload_c(0)
                for g in (range(NG) if "corrOLD" in RET_PARTS else []):
                    sl = g % 2
                    tg = g * G
                    zg = zg2[:, sl * NC * G:(sl + 1) * NC * G]
                    xn_g = xn2[:, 0:NC * G]
                    crg = crg2[:, 0:G]
                    srg = srg2[:, 0:G]
                    emit_cs_loads(g)
                    b.wait(ve, "ld", ldv[g])
                    b.wait(ve, "ld", csld[0])
                    b.wait(te, "act")
                    b.wait(te, "vec")
                    _xn_build(zg, xn_g, by[0], by[1], tg, evt)
                    xnv = b.val("vec")
                    b.wait(te, "vec", xnv)
                    evh = [0, 0]
                    rvh = [0, 0]
                    pqb = [0, 0]
                    swv = 0
                    b.wait(ve, "ld", yldv[g])
                    for h in range(H):
                        p = h % 2
                        if evh[p]:
                            b.wait(te, "act", evh[p])
                        pv = 0
                        for c in range(NC):
                            pv = b.inc(te.matmul(ba[p][:, 0:G], wq_sb[:, c * D + h * 128: c * D + (h + 1) * 128],
                                                 xn_g[:, c * G:(c + 1) * G],
                                                 start=(c == 0), stop=(c == NC - 1)), "pe")
                        b.wait(se, "pe", pv)
                        if rvh[p]:
                            b.wait(se, "vec", rvh[p])
                            b.wait(se, "pool", pqb[p])
                        evh[p] = b.inc(se.copy(evt[:, p * G:(p + 1) * G], ba[p][:, 0:G]), "act")
                        b.wait(ge, "act", evh[p])
                        pqb[p] = b.inc(ge.tensor_copy(rotq[:, h * G:(h + 1) * G], evt[:, p * G:(p + 1) * G]), "pool")
                        b.inc(ge.tensor_mul(evc[:, p * G:(p + 1) * G], evt[:, p * G:(p + 1) * G], crg), "pool")
                        pqb[p] = b.val("pool")
                        b.wait(te, "pool", pqb[p])
                        if swv:
                            b.wait(te, "vec", swv)
                        ppv = b.inc(te.matmul(psV[:, 0:G], t_pswap[:], rotq[:, h * G:(h + 1) * G],
                                              start=True, stop=True), "pe")
                        b.wait(ve, "pe", ppv)
                        b.wait(ve, "pool", pqb[p])
                        swv = b.inc(ve.tensor_mul(evt[:, p * G:(p + 1) * G], psV[:, 0:G], srg), "vec")
                        rvh[p] = b.inc(ve.tensor_add(rotq[:, h * G:(h + 1) * G],
                                                     evc[:, p * G:(p + 1) * G],
                                                     evt[:, p * G:(p + 1) * G]), "vec")
                        b.wait(ge, "vec", rvh[p])
                        a = rotq[:, h * G:(h + 1) * G].rearrange("p (c t) -> p c t", t=CH)
                        gqb = t_gq[:, h * CH:(h + 1) * CH].rearrange("p (c t) -> p c t", t=CH)
                        a2, b2 = bass.broadcast_tensor_aps(a, gqb)
                        qgv = b.inc(ge.tensor_tensor(
                            qg_all[:, h * G:(h + 1) * G].rearrange("p (c t) -> p c t", t=CH),
                            a2, b2, ALU.mult), "pool")
                        b.wait(te, "pool", qgv)
                        sttl = [0, 0]
                        for tc in range(4):
                            pq = tc % 2
                            if sttl[pq]:
                                b.wait(te, "vec", sttl[pq])
                            cpv = b.inc(te.matmul(bd[pq][:, 0:128], st_Rrelay[:, h * 128:(h + 1) * 128],
                                                  qg_all[:, h * G + tc * 128: h * G + (tc + 1) * 128],
                                                  start=True, stop=True), "pe")
                            b.wait(ve, "pe", cpv)
                            gscale = float(_gammas64()[h] ** (128 * (g * 4 + tc)))
                            ysl = y_grp[:, h * G + tc * 128: h * G + (tc + 1) * 128]
                            sttl[pq] = b.inc(ve.scalar_tensor_tensor(ysl, bd[pq][:, 0:128], gscale, ysl,
                                                                     ALU.mult, ALU.add), "vec")
                    gv_pe[g] = b.val("pe")
                    gv_vec[g] = b.val("vec")
                    if g + 1 < NG:
                        emit_loads_c(g + 1)
                    b.wait(sy, "vec")
                    ystv[g] = b.inc(sy.dma_start(out=_rearr_cols(y_dram, tg, G),
                                                 in_=y_grp.rearrange("p (h t) -> p h t", h=H)), "st", 16)
                    if g + 1 < NG:
                        emit_yload_c(g + 1)
                b.wait(sy, "st")
            nc.all_engine_barrier()
            if "post" in RET_PARTS:
                _ret_post(li)
                nc.all_engine_barrier()
            if "ffn" in RET_PARTS:
                _ret_ffn(li)
                nc.all_engine_barrier()

        def _ret_post(li):
            outer = ExitStack()
            st_A8 = outer.enter_context(nc.sbuf_tensor(f"po{li}_A8", [H, TOK], F32))
            st_B8 = outer.enter_context(nc.sbuf_tensor(f"po{li}_B8", [H, TOK], F32))
            with ExitStack() as s:
                y_g = s.enter_context(nc.sbuf_tensor(f"po{li}_yg", [128, H * G], BF16))
                sqb = s.enter_context(nc.sbuf_tensor(f"po{li}_sq", [128, 2 * G], BF16))
                mu8 = s.enter_context(nc.sbuf_tensor(f"po{li}_mu", [8, TOK], F32))
                var8 = s.enter_context(nc.sbuf_tensor(f"po{li}_va", [8, TOK], F32))
                t8 = s.enter_context(nc.sbuf_tensor(f"po{li}_t8", [8, G], F32))
                qg_g = s.enter_context(nc.sbuf_tensor(f"po{li}_qgg", [128, H * G], BF16))
                psS = s.enter_context(nc.psum_tensor(f"po{li}_pS", [128, 512], F32))
                psQ = s.enter_context(nc.psum_tensor(f"po{li}_pQ", [128, 512], F32))
                psX = [s.enter_context(nc.psum_tensor(f"po{li}_pX{i}", [128, 512], F32)) for i in range(2)]
                gam64 = _gammas64()
                for g in range(NG):
                    tg = g * G
                    if g > 0:
                        b.wait(sy, "pe")
                    v = b.inc(sy.dma_start(out=y_g.rearrange("p (h t) -> p h t", h=H),
                                           in_=_rearr_cols(y_dram, tg, G)), "ld", 16)
                    qv = 0
                    if "corr" in RET_PARTS:
                        qv = b.inc(sy.dma_start(out=qg_g.rearrange("p (h t) -> p h t", h=H),
                                                in_=_rearr_cols(qg_dram, tg, G)), "ld", 16)
                    b.wait(ve, "ld", v)
                    b.wait(te, "ld", v)
                    if g > 0:
                        b.wait(te, "vec")
                    if "corr" in RET_PARTS:
                        b.wait(te, "ld", qv)
                        sttl = [0, 0]
                        for h in range(H):
                            for tc in range(4):
                                gscale = float(gam64[h] ** (128 * (g * 4 + tc)))
                                if gscale == 0.0:
                                    continue
                                pq = tc % 2
                                if sttl[pq]:
                                    b.wait(te, "vec", sttl[pq])
                                cpv = b.inc(te.matmul(psX[pq][:, 0:128], st_Rrelay[:, h * 128:(h + 1) * 128],
                                                      qg_g[:, h * G + tc * 128: h * G + (tc + 1) * 128],
                                                      start=True, stop=True), "pe")
                                b.wait(ve, "pe", cpv)
                                ysl = y_g[:, h * G + tc * 128: h * G + (tc + 1) * 128]
                                sttl[pq] = b.inc(ve.scalar_tensor_tensor(ysl, psX[pq][:, 0:128], gscale, ysl,
                                                                         ALU.mult, ALU.add), "vec")
                    pv = 0
                    for h in range(H):
                        ysl = y_g[:, h * G:(h + 1) * G]
                        sqs = sqb[:, (h % 2) * G:(h % 2) * G + G]
                        if h >= 2 or g > 0:
                            b.wait(ve, "pe")
                        sv = b.inc(ve.tensor_mul(sqs, ysl, ysl), "vec")
                        b.inc(te.matmul(psS[0:8, 0:G], t_ehs_b[:, h * H:(h + 1) * H], ysl,
                                        start=(h == 0), stop=(h == H - 1)), "pe")
                        b.wait(te, "vec", sv)
                        pv = b.inc(te.matmul(psQ[0:8, 0:G], t_ehs_b[:, h * H:(h + 1) * H], sqs,
                                             start=(h == 0), stop=(h == H - 1)), "pe")
                    if "corr" in RET_PARTS:
                        b.wait(sy, "vec")
                        b.inc(sy.dma_start(out=_rearr_cols(y_dram, tg, G),
                                           in_=y_g.rearrange("p (h t) -> p h t", h=H)), "st", 16)
                    b.wait(ve, "pe", pv)
                    b.inc(ve.tensor_scalar_mul(mu8[0:8, tg:tg + G], psS[0:8, 0:G], 1.0 / DH), "vec")
                    b.inc(ve.tensor_mul(t8[0:8, :], mu8[0:8, tg:tg + G], mu8[0:8, tg:tg + G]), "vec")
                    b.inc(ve.tensor_scalar_mul(var8[0:8, tg:tg + G], psQ[0:8, 0:G], 1.0 / DH), "vec")
                    b.inc(ve.tensor_sub(var8[0:8, tg:tg + G], var8[0:8, tg:tg + G], t8[0:8, :]), "vec")
                b.inc(ve.tensor_scalar_add(var8[0:8, :], var8[0:8, :], EPS), "vec")
                rv = b.inc(ve.reciprocal(var8[0:8, :], var8[0:8, :]), "vec")
                b.wait(se, "vec", rv)
                av = b.inc(se.activation(st_A8[0:8, :], var8[0:8, :], AF.Sqrt), "act")
                b.wait(ve, "act", av)
                bv = b.inc(ve.scalar_tensor_tensor(st_B8[0:8, :], mu8[0:8, :], -1.0,
                                                   st_A8[0:8, :], ALU.mult, ALU.mult), "vec")
                b.wait(se, "vec", bv)
                b.wait(te, "vec", bv)
            nc.all_engine_barrier()
            with ExitStack() as s:
                wg_sb = s.enter_context(nc.sbuf_tensor(f"po{li}_wg", [128, NC * D], BF16))
                wo_sb = s.enter_context(nc.sbuf_tensor(f"po{li}_wo", [128, NC * D], BF16))
                zg = s.enter_context(nc.sbuf_tensor(f"po{li}_zg", [128, NC * G], F32))
                xn_g = s.enter_context(nc.sbuf_tensor(f"po{li}_xn", [128, NC * G], BF16))
                y_g = s.enter_context(nc.sbuf_tensor(f"po{li}_yg2", [128, H * G], BF16))
                g_sb = s.enter_context(nc.sbuf_tensor(f"po{li}_gs", [128, NC * G], BF16))
                m_sb = s.enter_context(nc.sbuf_tensor(f"po{li}_ms", [128, NC * G], BF16))
                scr = s.enter_context(nc.sbuf_tensor(f"po{li}_scr", [128, 4 * G], F32))
                psA = s.enter_context(nc.psum_tensor(f"po{li}_pA", [128, 512], F32))
                psB = s.enter_context(nc.psum_tensor(f"po{li}_pB", [128, 512], F32))
                psG = s.enter_context(nc.psum_tensor(f"po{li}_pG", [128, 512], F32))
                psO = s.enter_context(nc.psum_tensor(f"po{li}_pO", [128, 512], F32))
                psbc = [s.enter_context(nc.psum_tensor(f"po{li}_pbc{i}", [128, 512], F32)) for i in range(2)]
                psbd = [s.enter_context(nc.psum_tensor(f"po{li}_pbd{i}", [128, 512], F32)) for i in range(2)]
                b.inc(ge.dma_start(out=wg_sb.rearrange("p (c d) -> p c d", d=D),
                                   in_=w_g[li].rearrange("(c p) d -> p c d", p=128)), "wld", 16)
                wldv = b.inc(ge.dma_start(out=wo_sb.rearrange("p (c d) -> p c d", d=D),
                                          in_=w_o[li].rearrange("(c p) d -> p c d", p=128)), "wld", 16)
                b.wait(te, "wld", wldv)
                bcfree = [0, 0]
                for g in range(NG):
                    tg = g * G
                    if g > 0:
                        b.wait(sy, "pe")
                        b.wait(sy, "vec")
                        b.wait(sy, "st")
                    v = b.inc(sy.dma_start(out=zg.rearrange("p (c t) -> p c t", c=NC),
                                           in_=_rearr_cols(z_buf, tg, G)), "ld", 16)
                    yv = b.inc(sy.dma_start(out=y_g.rearrange("p (h t) -> p h t", h=H),
                                            in_=_rearr_cols(y_dram, tg, G)), "ld", 16)
                    b.wait(ve, "ld", v)
                    _xn_build(zg, xn_g, psA, psB, tg, scr)
                    xnv = b.val("vec")
                    b.wait(te, "vec", xnv)
                    gv = 0
                    for co in range(NC):
                        if gv > 0:
                            b.wait(te, "act", gv)
                        pv = 0
                        for c in range(NC):
                            pv = b.inc(te.matmul(psG[:, 0:G], wg_sb[:, c * D + co * 128: c * D + (co + 1) * 128],
                                                 xn_g[:, c * G:(c + 1) * G],
                                                 start=(c == 0), stop=(c == NC - 1)), "pe")
                        b.wait(se, "pe", pv)
                        b.wait(se, "vec") if co == 0 and g > 0 else None
                        gv = b.inc(se.activation(g_sb[:, co * G:(co + 1) * G], psG[:, 0:G], AF.Silu), "act")
                    b.wait(ve, "ld", yv)
                    for h in range(H):
                        p2 = h % 2
                        if bcfree[p2]:
                            b.wait(te, "vec", bcfree[p2])
                        bca = b.inc(te.matmul(psbc[p2][:, 0:G], t_ebc_f[:, h * 128:(h + 1) * 128],
                                              st_A8[0:8, tg:tg + G], start=True, stop=True), "pe")
                        bcb = b.inc(te.matmul(psbd[p2][:, 0:G], t_ebc_f[:, h * 128:(h + 1) * 128],
                                              st_B8[0:8, tg:tg + G], start=True, stop=True), "pe")
                        b.wait(ve, "pe", bcb)
                        b.wait(ve, "act", gv)
                        ysl = y_g[:, h * G:(h + 1) * G]
                        b.inc(ve.tensor_mul(scr[:, p2 * G:p2 * G + G], ysl, psbc[p2][:, 0:G]), "vec")
                        bcfree[p2] = b.inc(ve.tensor_add(scr[:, p2 * G:p2 * G + G],
                                                         scr[:, p2 * G:p2 * G + G], psbd[p2][:, 0:G]), "vec")
                        b.inc(ve.tensor_scalar(scr[:, (2 + p2) * G:(3 + p2) * G], scr[:, p2 * G:p2 * G + G],
                                               t_gns[:, li * NC + h: li * NC + h + 1],
                                               t_gnb[:, li * NC + h: li * NC + h + 1],
                                               ALU.mult, ALU.add), "vec")
                        mv = b.inc(ve.tensor_mul(m_sb[:, h * G:(h + 1) * G], scr[:, (2 + p2) * G:(3 + p2) * G],
                                                 g_sb[:, h * G:(h + 1) * G]), "vec")
                    b.wait(te, "vec", mv)
                    zv = 0
                    for co in range(NC):
                        if zv > 0:
                            b.wait(te, "vec", zv)
                        pv = 0
                        for c in range(NC):
                            pv = b.inc(te.matmul(psO[:, 0:G], wo_sb[:, c * D + co * 128: c * D + (co + 1) * 128],
                                                 m_sb[:, c * G:(c + 1) * G],
                                                 start=(c == 0), stop=(c == NC - 1)), "pe")
                        b.wait(ve, "pe", pv)
                        zv = b.inc(ve.tensor_add(zg[:, co * G:(co + 1) * G], zg[:, co * G:(co + 1) * G],
                                                 psO[:, 0:G]), "vec")
                    b.wait(sy, "vec", zv)
                    b.inc(sy.dma_start(out=_rearr_cols(z_buf, tg, G),
                                       in_=zg.rearrange("p (c t) -> p c t", c=NC)), "st", 16)
                b.wait(sy, "st")
            outer.close()

        def _ret_ffn(li):
            lnstats(z_buf, 0, TOK, f"rf{li}s")
            nc.all_engine_barrier()
            with ExitStack() as s:
                w1_sb = s.enter_context(nc.sbuf_tensor(f"rf{li}_w1", [128, NC * FF], BF16))
                w2_sb = s.enter_context(nc.sbuf_tensor(f"rf{li}_w2", [128, 16 * D], BF16))
                zg = s.enter_context(nc.sbuf_tensor(f"rf{li}_zg", [128, NC * G], F32))
                xn_g = s.enter_context(nc.sbuf_tensor(f"rf{li}_xn", [128, NC * G], BF16))
                hid = s.enter_context(nc.sbuf_tensor(f"rf{li}_hid", [128, 16 * G], BF16))
                scr = s.enter_context(nc.sbuf_tensor(f"rf{li}_scr", [128, 4 * G], F32))
                psA = s.enter_context(nc.psum_tensor(f"rf{li}_pA", [128, 512], F32))
                psB = s.enter_context(nc.psum_tensor(f"rf{li}_pB", [128, 512], F32))
                psF = [s.enter_context(nc.psum_tensor(f"rf{li}_pF{i}", [128, 512], F32)) for i in range(2)]
                psW = [s.enter_context(nc.psum_tensor(f"rf{li}_pW{i}", [128, 512], F32)) for i in range(2)]
                ringF = Ring(b, f"rf{li}_rF", [p[:, 0:512] for p in psF])
                ringW = Ring(b, f"rf{li}_rW", [p[:, 0:512] for p in psW])
                b.inc(ge.dma_start(out=w1_sb.rearrange("p (c d) -> p c d", d=FF),
                                   in_=w_1[li].rearrange("(c p) d -> p c d", p=128)), "wld", 16)
                wldv = b.inc(ge.dma_start(out=w2_sb.rearrange("p (c d) -> p c d", d=D),
                                          in_=w_2[li].rearrange("(c p) d -> p c d", p=128)), "wld", 16)
                b.wait(te, "wld", wldv)
                c1 = GELU_C
                c1a = GELU_C * GELU_A
                for g in range(NG):
                    tg = g * G
                    if g > 0:
                        b.wait(sy, "pe")
                        b.wait(sy, "vec")
                        b.wait(sy, "st")
                    v = b.inc(sy.dma_start(out=zg.rearrange("p (c t) -> p c t", c=NC),
                                           in_=_rearr_cols(z_buf, tg, G)), "ld", 16)
                    b.wait(ve, "ld", v)
                    _xn_build(zg, xn_g, psA, psB, tg, scr)
                    xnv = b.val("vec")
                    b.wait(te, "vec", xnv)
                    hv = 0
                    for fo in range(16):
                        pf = ringF.acquire(te)
                        pv = 0
                        for c in range(NC):
                            pv = b.inc(te.matmul(pf, w1_sb[:, c * FF + fo * 128: c * FF + (fo + 1) * 128],
                                                 xn_g[:, c * G:(c + 1) * G],
                                                 start=(c == 0), stop=(c == NC - 1)), "pe")
                        b.wait(se, "pe", pv)
                        if hv > 0:
                            b.wait(se, ringF.name, hv)
                        sqv = b.inc(se.activation(scr[:, 0:G], pf, AF.Square), "act")
                        b.wait(ve, "act", sqv)
                        b.inc(ve.tensor_scalar(scr[:, G:2 * G], scr[:, 0:G], c1a, c1, ALU.mult, ALU.add), "vec")
                        uv = b.inc(ve.tensor_mul(scr[:, 2 * G:3 * G], pf, scr[:, G:2 * G]), "vec")
                        b.wait(se, "vec", uv)
                        tv = b.inc(se.activation(scr[:, 3 * G:4 * G], scr[:, 2 * G:3 * G], AF.Tanh), "act")
                        b.wait(ve, "act", tv)
                        b.inc(ve.tensor_scalar(scr[:, G:2 * G], scr[:, 3 * G:4 * G], 0.5, 0.5,
                                               ALU.mult, ALU.add), "vec")
                        hvi = ve.tensor_mul(hid[:, fo * G:(fo + 1) * G], pf, scr[:, G:2 * G])
                        ringF.release(hvi)
                        hv = b.cnt[ringF.name]
                    b.wait(te, ringF.name, hv)
                    for co in range(NC):
                        pw = ringW.acquire(te)
                        pv = 0
                        for fc in range(16):
                            pv = b.inc(te.matmul(pw, w2_sb[:, fc * D + co * 128: fc * D + (co + 1) * 128],
                                                 hid[:, fc * G:(fc + 1) * G],
                                                 start=(fc == 0), stop=(fc == 15)), "pe")
                        b.wait(ve, "pe", pv)
                        zvi = ve.tensor_add(zg[:, co * G:(co + 1) * G], zg[:, co * G:(co + 1) * G], pw)
                        ringW.release(zvi)
                    b.wait(sy, ringW.name)
                    b.inc(sy.dma_start(out=_rearr_cols(z_buf, tg, G),
                                       in_=zg.rearrange("p (c t) -> p c t", c=NC)), "st", 16)
                b.wait(sy, "st")

        def merge_rz():
            with ExitStack() as s:
                rg = s.enter_context(nc.sbuf_tensor("mg_rg", [128, NC * G], F32))
                zg2 = s.enter_context(nc.sbuf_tensor("mg_zg", [128, NC * G], F32))
                for g in range(NG):
                    tg = g * G
                    if g > 0:
                        b.wait(sy, "st")
                        b.wait(sy, "vec")
                    b.inc(sy.dma_start(out=rg.rearrange("p (c t) -> p c t", c=NC),
                                       in_=_rearr_cols(r_res, tg, G)), "ld", 16)
                    v = b.inc(sy.dma_start(out=zg2.rearrange("p (c t) -> p c t", c=NC),
                                           in_=_rearr_cols(z_buf, tg, G)), "ld", 16)
                    b.wait(ve, "ld", v)
                    vv = 0
                    for c in range(NC):
                        vv = b.inc(ve.tensor_add(rg[:, c * G:(c + 1) * G], rg[:, c * G:(c + 1) * G],
                                                 zg2[:, c * G:(c + 1) * G]), "vec")
                    b.wait(sy, "vec", vv)
                    b.inc(sy.dma_start(out=_rearr_cols(r_res, tg, G),
                                       in_=rg.rearrange("p (c t) -> p c t", c=NC)), "st", 16)
                b.wait(sy, "st")
            nc.all_engine_barrier()

        def p4_output():
            with ExitStack() as ph:
                def src(tc, buf):
                    return b.inc(sy.dma_start(
                        out=buf.rearrange("p (c t) -> p c t", c=NC),
                        in_=_rearr_cols(r_res, tc * 128, 128)), "ld", 16)
                def dst(tc, buf):
                    b.inc(sy.dma_start(out=out_t[tc * 128:(tc + 1) * 128, :], in_=buf[:, :]), "st", 16)
                transpose_stream(src, dst, ph, "p4")

        load_tables()
        if "P0" in phases:
            p0_ingest()
        if "FFN1" in phases:
            moe_phase("f1")
        if "RET" in phases:
            znorm_init()
            for li in range(L):
                ret_layer(li)
        if "MERGE" in phases:
            merge_rz()
        if "FFN2" in phases:
            moe_phase("f2")
        if "P4" in phases:
            p4_output()

    return nc, b


def _shard_in_maps(inputs):
    X = np.ascontiguousarray(inputs["L0_attn_X"])
    maps = []
    for core in range(8):
        bat, half = core >> 1, core & 1
        t = host_tables(core)
        m = {
            "x": np.ascontiguousarray(X[bat, half * TOK:(half + 1) * TOK, :]),
            "wq": inputs["L0_attn_Wq"], "wk": inputs["L0_attn_Wk"],
            "wv": inputs["L0_attn_Wv"], "wg": inputs["L0_attn_Wg"],
            "wo": inputs["L0_attn_Wo"], "w1": inputs["L0_attn_W1"],
            "w2": inputs["L0_attn_W2"],
            "ln_s": _feat_tile(inputs["L0_attn_ln_scale"]),
            "ln_b": _feat_tile(inputs["L0_attn_ln_bias"]),
            "gn_s": np.concatenate([_feat_tile(inputs["L0_attn_gn_scale"][i]) for i in range(L)], axis=1),
            "gn_b": np.concatenate([_feat_tile(inputs["L0_attn_gn_bias"][i]) for i in range(L)], axis=1),
            "gate_w": inputs["L0_ffn_gate_w"],
            "we_g": inputs["L0_ffn_w_gate"], "we_u": inputs["L0_ffn_w_up"],
            "we_d": inputs["L0_ffn_w_down"],
            "ws_g": inputs["L0_ffn_ws_gate"], "ws_u": inputs["L0_ffn_ws_up"],
            "ws_d": inputs["L0_ffn_ws_down"],
            "crot": t["crot"], "srot": t["srot"], "ddec": t["ddec"], "gq": t["gq"],
            "kscale": t["kscale"], "id_f": t["id_f"],
            "ones_row_f": t["ones_row_f"], "ones_col_f": t["ones_col_f"],
            "ehs_f": t["ehs_f"], "ebc_f": t["ebc_f"],
            "id_b": t["id_b"], "pswap_b": t["pswap_b"],
            "ones_col_b": t["ones_col_b"], "ehs_b": t["ehs_b"],
            "relay_idx": t["relay_idx"],
        }
        m = {k: (np.ascontiguousarray(v) if isinstance(v, np.ndarray) else np.ascontiguousarray(np.asarray(v))) for k, v in m.items()}
        maps.append(m)
    return maps


def kernel(**inputs):
    inputs = {k: np.asarray(v) for k, v in inputs.items()}
    maps = _shard_in_maps(inputs)
    nc, _ = build_program(ALL_PHASES)
    res = run_bass_kernel_spmd(nc, maps, core_ids=list(range(8)))
    out = np.empty((BAT, S, D), np.float32)
    for core in range(8):
        bat, half = core >> 1, core & 1
        out[bat, half * TOK:(half + 1) * TOK, :] = res.results[core]["out"]
    return out


# version 05 (final)
# speedup vs baseline: 1.1569x; 1.0058x over previous
import numpy as np
import ml_dtypes
from contextlib import ExitStack
import concourse.bass as bass
import concourse.mybir as mybir
from concourse.bass_core_rust import IndirectOffsetOnAxis
from concourse.bass_utils import run_bass_kernel_spmd

F32 = mybir.dt.float32
BF16 = mybir.dt.bfloat16
I32 = mybir.dt.int32
AF = mybir.ActivationFunctionType
ALU = mybir.AluOpType
AX = mybir.AxisListType

BAT, S, D, H, DH, FF = 4, 8192, 1024, 8, 128, 2048
NEXP = 8
L = 4
TOK = 4096
NC = 8
G = 512
NG = TOK // G
CH = 128
NCH = TOK // CH
QB = 1024
NQB = TOK // QB
EPS = 1e-5
GELU_C = float(np.sqrt(2.0 / np.pi))
GELU_A = 0.044715

ALL_PHASES = ("P0", "FFN1", "RET", "MERGE", "FFN2", "P4")
DEBUG_WPE = False
RET_PARTS = ("scan", "relay", "corr", "post", "ffn")
DEBUG_SCRATCH = False


def _gammas64():
    return 1.0 - np.exp(np.linspace(np.log(1 / 32), np.log(1 / 512), H))


def host_tables(core_id):
    gam = _gammas64()
    t = {}
    half = core_id & 1
    pos = np.arange(half * TOK, (half + 1) * TOK, dtype=np.float64)
    inv_freq = 1.0 / 10000.0 ** (np.arange(0, DH, 2, dtype=np.float64) / DH)
    ang = pos[:, None] * inv_freq[None, :]
    cos, sin = np.cos(ang), np.sin(ang)
    crot = np.empty((DH, TOK), np.float64)
    srot = np.empty((DH, TOK), np.float64)
    crot[0::2, :] = cos.T
    crot[1::2, :] = cos.T
    srot[0::2, :] = -sin.T
    srot[1::2, :] = sin.T
    t["crot"] = crot.astype(np.float32)
    t["srot"] = srot.astype(np.float32)
    jj, ii = np.meshgrid(np.arange(CH), np.arange(CH), indexing="ij")
    dd = np.zeros((CH, H * CH), np.float64)
    gq = np.zeros((CH, H * CH), np.float64)
    for h in range(H):
        m = (ii - jj).astype(np.float64)
        dd[:, h * CH:(h + 1) * CH] = np.where(m >= 0, gam[h] ** m, 0.0)
        gq[:, h * CH:(h + 1) * CH] = gam[h] ** (ii[0][None, :] + 1.0)
    t["ddec"] = dd.astype(np.float32)
    t["gq"] = gq.astype(np.float32)
    ks = np.zeros((CH, H), np.float64)
    for h in range(H):
        ks[:, h] = gam[h] ** (CH - np.arange(CH, dtype=np.float64))
    t["kscale"] = ks.astype(np.float32)
    eye = np.eye(128, dtype=np.float32)
    t["id_f"] = eye
    t["id_b"] = eye.astype(ml_dtypes.bfloat16)
    pswap = np.zeros((128, 128), np.float32)
    for d in range(128):
        pswap[d ^ 1, d] = 1.0
    t["pswap_b"] = pswap.astype(ml_dtypes.bfloat16)
    ones_col = np.ones((128, 1), np.float32)
    t["ones_col_f"] = ones_col
    t["ones_col_b"] = ones_col.astype(ml_dtypes.bfloat16)
    t["ones_row_f"] = np.ones((1, 128), np.float32)
    ehs = np.zeros((128, H * H), np.float32)
    for h in range(H):
        ehs[:, h * H + h] = 1.0
    t["ehs_b"] = ehs.astype(ml_dtypes.bfloat16)
    t["ehs_f"] = ehs
    ebc = np.zeros((H, H * 128), np.float32)
    for h in range(H):
        ebc[h, h * 128:(h + 1) * 128] = 1.0
    t["ebc_f"] = ebc
    if core_id & 1:
        idx = (np.arange(128, dtype=np.int32) * 8 + (core_id - 1) * 1024).reshape(128, 1)
    else:
        idx = np.full((128, 1), 2 ** 30, dtype=np.int32)
    t["relay_idx"] = idx
    return t


def _feat_tile(vec):
    return np.ascontiguousarray(vec.reshape(NC, 128).T)


class Bld:
    def __init__(self, nc):
        self.nc = nc
        self.sems = {}
        self.cnt = {}

    def sem(self, name):
        if name not in self.sems:
            self.sems[name] = self.nc.alloc_semaphore(name)
            self.cnt[name] = 0
        return self.sems[name]

    def inc(self, instr, name, amt=1):
        instr.then_inc(self.sem(name), amt)
        self.cnt[name] += amt
        return self.cnt[name]

    def val(self, name):
        self.sem(name)
        return self.cnt[name]

    def wait(self, eng, name, v=None):
        if v is None:
            v = self.cnt[name]
        eng.wait_ge(self.sem(name), v)


class Ring:
    def __init__(self, b: Bld, name, tiles):
        self.b = b
        self.name = name
        self.tiles = tiles
        self.n = 0
        b.sem(name)
        self.base = b.cnt[name]

    def acquire(self, eng):
        k = len(self.tiles)
        need = self.n - (k - 1)
        if need > 0:
            self.b.wait(eng, self.name, self.base + need)
        tile = self.tiles[self.n % k]
        self.n += 1
        return tile

    def release(self, instr, n=1):
        self.b.inc(instr, self.name, n)


def _rearr_cols(dram_ap, c0, cols):
    sl = dram_ap[:, c0:c0 + cols]
    return sl.rearrange("(c p) t -> p c t", p=128)


def build_program(phases=ALL_PHASES):
    nc = bass.Bass("TRN2", target_bir_lowering=False, debug=False)
    b = Bld(nc)

    din = {}
    def dram_in(name, shape):
        din[name] = nc.dram_tensor(name, list(shape), F32, kind="ExternalInput").ap()
        return din[name]

    x_in = dram_in("x", (TOK, D))
    w_q = dram_in("wq", (L, D, D))
    w_k = dram_in("wk", (L, D, D))
    w_v = dram_in("wv", (L, D, D))
    w_g = dram_in("wg", (L, D, D))
    w_o = dram_in("wo", (L, D, D))
    w_1 = dram_in("w1", (L, D, FF))
    w_2 = dram_in("w2", (L, FF, D))
    ln_s = dram_in("ln_s", (128, NC))
    ln_b = dram_in("ln_b", (128, NC))
    gn_s = dram_in("gn_s", (128, L * NC))
    gn_b = dram_in("gn_b", (128, L * NC))
    gate_w = dram_in("gate_w", (D, NEXP))
    we_g = dram_in("we_g", (NEXP, D, FF))
    we_u = dram_in("we_u", (NEXP, D, FF))
    we_d = dram_in("we_d", (NEXP, FF, D))
    ws_g = dram_in("ws_g", (D, FF))
    ws_u = dram_in("ws_u", (D, FF))
    ws_d = dram_in("ws_d", (FF, D))
    crot = dram_in("crot", (DH, TOK))
    srot = dram_in("srot", (DH, TOK))
    ddec = dram_in("ddec", (CH, H * CH))
    gq_t = dram_in("gq", (CH, H * CH))
    kscale = dram_in("kscale", (CH, H))
    id_f = dram_in("id_f", (128, 128))
    ones_row_f = dram_in("ones_row_f", (1, 128))
    ehs_f = dram_in("ehs_f", (128, H * H))
    ebc_f = dram_in("ebc_f", (H, H * 128))
    bf16_tabs = {}
    for nm, shape in [("id_b", (128, 128)), ("pswap_b", (128, 128)),
                      ("ones_col_b", (128, 1)), ("ehs_b", (128, H * H))]:
        bf16_tabs[nm] = nc.dram_tensor(nm, list(shape), BF16, kind="ExternalInput").ap()
    ones_col_f = dram_in("ones_col_f", (128, 1))
    relay_idx = nc.dram_tensor("relay_idx", [128, 1], I32, kind="ExternalInput").ap()

    out_t = nc.dram_tensor("out", [TOK, D], F32, kind="ExternalOutput").ap()
    dbg_t = nc.dram_tensor("dbg", [8, TOK], F32, kind="ExternalOutput").ap() if DEBUG_WPE else None
    dbg2_t = nc.dram_tensor("dbg2", [128, 64], F32, kind="ExternalOutput").ap() if DEBUG_WPE else None

    _scr_kind = "ExternalOutput" if DEBUG_SCRATCH else "Internal"
    r_res = nc.dram_tensor("r_res", [D, TOK], F32, kind=_scr_kind).ap()
    z_buf = nc.dram_tensor("z_buf", [D, TOK], F32, kind=_scr_kind).ap()
    y_dram = nc.dram_tensor("y_dram", [D, TOK], BF16, kind=_scr_kind).ap()
    qg_dram = nc.dram_tensor("qg_dram", [D, TOK], BF16).ap()
    relay_in = nc.dram_tensor("relay_in", [H * DH, DH], F32)
    relay_out = nc.dram_tensor("relay_out", [NC * H * DH, DH], F32)

    with ExitStack() as ctx:
        sb = lambda name, shape, dt: ctx.enter_context(nc.sbuf_tensor(name, shape, dt))
        psb = lambda name, shape, dt=F32: ctx.enter_context(nc.psum_tensor(name, shape, dt))

        te, ve, se, ge, sy = nc.tensor, nc.vector, nc.scalar, nc.gpsimd, nc.sync

        t_id_f = sb("t_id_f", [128, 128], F32)
        t_id_b = sb("t_id_b", [128, 128], BF16)
        t_pswap = sb("t_pswap", [128, 128], BF16)
        t_ones_row = sb("t_ones_row", [1, 128], F32)
        t_ones_col_f = sb("t_ones_col_f", [128, 1], F32)
        t_ones_col_b = sb("t_ones_col_b", [128, 1], BF16)
        t_ehs_b = sb("t_ehs_b", [128, H * H], BF16)
        t_ehs_f = sb("t_ehs_f", [128, H * H], F32)
        t_ebc_f = sb("t_ebc_f", [H, H * 128], F32)
        t_ddec = sb("t_ddec", [CH, H * CH], F32)
        t_gq = sb("t_gq", [CH, H * CH], F32)
        t_kscale = sb("t_kscale", [CH, H], F32)
        t_lns = sb("t_lns", [128, NC], F32)
        t_lnb = sb("t_lnb", [128, NC], F32)
        t_gns = sb("t_gns", [128, L * NC], F32)
        t_gnb = sb("t_gnb", [128, L * NC], F32)
        t_ridx = sb("t_ridx", [128, 1], I32)
        st_A = sb("st_A", [1, TOK], F32)
        st_B = sb("st_B", [1, TOK], F32)
        st_R = sb("st_R", [128, H * DH], F32)
        st_Rb = sb("st_Rb", [128, H * DH], BF16)
        st_Rrelay = sb("st_Rrelay", [128, H * DH], BF16)
        rlb_f = sb("rlb_f", [128, H * DH], F32)

        consts = sb("consts", [128, 2], F32)
        nc.const_aps.register(F32, 0.5, consts[:, 0:1])
        nc.const_aps.register(F32, 2.0, consts[:, 1:2])
        t_one2 = sb("t_one2", [128, 2], F32)
        t_neg2 = sb("t_neg2", [128, 2], F32)

        def load_tables():
            ld = [
                (t_id_f, id_f), (t_ones_row, ones_row_f), (t_ones_col_f, ones_col_f),
                (t_ehs_f, ehs_f), (t_ebc_f, ebc_f), (t_ddec, ddec), (t_gq, gq_t),
                (t_kscale, kscale), (t_lns, ln_s), (t_lnb, ln_b), (t_gns, gn_s),
                (t_gnb, gn_b),
                (t_id_b, bf16_tabs["id_b"]), (t_pswap, bf16_tabs["pswap_b"]),
                (t_ones_col_b, bf16_tabs["ones_col_b"]), (t_ehs_b, bf16_tabs["ehs_b"]),
                (t_ridx, relay_idx),
            ]
            for dst, src in ld:
                b.inc(sy.dma_start(out=dst[:], in_=src[:, :]), "ld", 16)
            ge.memset(consts[:, 0:1], 0.5)
            ge.memset(t_one2[:], 1.0)
            ge.memset(t_neg2[:], -1.0)
            b.inc(ge.memset(consts[:, 1:2], 2.0), "pool", 1)
            b.wait(sy, "ld")
            b.wait(ge, "pool")
            nc.all_engine_barrier()

        def transpose_stream(src_load, dst_store, ph, tag):
            xin = ph.enter_context(nc.sbuf_tensor(f"{tag}_xin", [128, 2 * D], F32))
            xout = ph.enter_context(nc.sbuf_tensor(f"{tag}_xout", [128, 2 * D], F32))
            tps = [ph.enter_context(nc.psum_tensor(f"{tag}_tp{i}", [128, 512], F32)) for i in range(4)]
            ring = Ring(b, f"{tag}_r", [tp[:, 0:128] for tp in tps])
            stv = []
            for tc in range(NCH):
                par = tc % 2
                xin_s = xin[:, par * D:(par + 1) * D]
                xout_s = xout[:, par * D:(par + 1) * D]
                if tc >= 2:
                    b.wait(sy, "st", stv[tc - 2])
                v = src_load(tc, xin_s)
                b.wait(te, "ld", v)
                for fc in range(NC):
                    ps = ring.acquire(te)
                    pv = b.inc(te.transpose(ps, xin_s[:, fc * 128:(fc + 1) * 128], t_id_f[:]), "pe")
                    b.wait(ve, "pe", pv)
                    if fc == 0 and tc >= 2:
                        b.wait(ve, "st", stv[tc - 2])
                    ring.release(ve.tensor_copy(xout_s[:, fc * 128:(fc + 1) * 128], ps))
                b.wait(sy, ring.name)
                dst_store(tc, xout_s)
                stv.append(b.cnt["st"])
            b.wait(sy, "st")

        def p0_ingest():
            with ExitStack() as ph:
                def src(tc, buf):
                    return b.inc(sy.dma_start(out=buf[:, :], in_=x_in[tc * 128:(tc + 1) * 128, :]), "ld", 16)
                def dst(tc, buf):
                    b.inc(sy.dma_start(
                        out=_rearr_cols(r_res, tc * 128, 128),
                        in_=buf.rearrange("p (c t) -> p c t", c=NC)), "st", 16)
                transpose_stream(src, dst, ph, "p0")
            nc.all_engine_barrier()

        def lnstats(src_dram, t0, ntok, tag):
            ngr = ntok // G
            with ExitStack() as s:
                rg = s.enter_context(nc.sbuf_tensor(f"{tag}_rg", [128, 2 * NC * G], F32))
                sq = s.enter_context(nc.sbuf_tensor(f"{tag}_sq", [128, 2 * G], F32))
                mu = s.enter_context(nc.sbuf_tensor(f"{tag}_mu", [1, ntok], F32))
                var = s.enter_context(nc.sbuf_tensor(f"{tag}_var", [1, ntok], F32))
                tmp = s.enter_context(nc.sbuf_tensor(f"{tag}_tmp", [1, ntok], F32))
                psS = s.enter_context(nc.psum_tensor(f"{tag}_psS", [128, 512], F32))
                psQ = s.enter_context(nc.psum_tensor(f"{tag}_psQ", [128, 512], F32))
                pev = [0] * ngr
                vev = [0] * ngr
                for g in range(ngr):
                    tg = t0 + g * G
                    par = g % 2
                    rgs = rg[:, par * NC * G:(par + 1) * NC * G]
                    if g >= 2:
                        b.wait(sy, "pe", pev[g - 2])
                    v = b.inc(sy.dma_start(out=rgs.rearrange("p (c t) -> p c t", c=NC),
                                           in_=_rearr_cols(src_dram, tg, G)), "ld", 16)
                    b.wait(ve, "ld", v)
                    b.wait(te, "ld", v)
                    if g >= 1:
                        b.wait(te, "vec", vev[g - 1])
                    pv = 0
                    for c in range(NC):
                        rc = rgs[:, c * G:(c + 1) * G]
                        sqs = sq[:, (c % 2) * G:(c % 2) * G + G]
                        if c >= 2 or g > 0:
                            b.wait(ve, "pe")
                        vv = b.inc(ve.tensor_mul(sqs, rc, rc), "vec")
                        b.inc(te.matmul(psS[0:1, :], t_ones_col_f[:], rc,
                                        start=(c == 0), stop=(c == NC - 1)), "pe")
                        b.wait(te, "vec", vv)
                        pv = b.inc(te.matmul(psQ[0:1, :], t_ones_col_f[:], sqs,
                                             start=(c == 0), stop=(c == NC - 1)), "pe")
                    pev[g] = pv
                    b.wait(ve, "pe", pv)
                    o = tg - t0
                    b.inc(ve.tensor_scalar_mul(mu[0:1, o:o + G], psS[0:1, :], 1.0 / D), "vec")
                    b.inc(ve.tensor_mul(tmp[0:1, o:o + G], mu[0:1, o:o + G], mu[0:1, o:o + G]), "vec")
                    b.inc(ve.tensor_scalar_mul(var[0:1, o:o + G], psQ[0:1, :], 1.0 / D), "vec")
                    vev[g] = b.inc(ve.tensor_sub(var[0:1, o:o + G], var[0:1, o:o + G], tmp[0:1, o:o + G]), "vec")
                b.inc(ve.tensor_scalar_add(var[0:1, 0:ntok], var[0:1, 0:ntok], EPS), "vec")
                vv = b.inc(ve.reciprocal(tmp[0:1, 0:ntok], var[0:1, 0:ntok]), "vec")
                b.wait(se, "vec", vv)
                av = b.inc(se.activation(st_A[0:1, t0:t0 + ntok], tmp[0:1, 0:ntok], AF.Sqrt), "act")
                b.wait(ve, "act", av)
                bv = b.inc(ve.scalar_tensor_tensor(st_B[0:1, t0:t0 + ntok], mu[0:1, 0:ntok], -1.0,
                                                   st_A[0:1, t0:t0 + ntok], ALU.mult, ALU.mult), "vec")
                b.wait(se, "vec", bv)
            return bv

        def moe_phase(tag):
            with ExitStack() as ph:
                xn = ph.enter_context(nc.sbuf_tensor(f"{tag}_xn", [128, NC * QB], BF16))
                yac = ph.enter_context(nc.sbuf_tensor(f"{tag}_yac", [128, NC * QB], F32))
                wpeT = ph.enter_context(nc.sbuf_tensor(f"{tag}_wpeT", [8, QB], F32))
                gwt = ph.enter_context(nc.sbuf_tensor(f"{tag}_gwt", [128, NC * 8], F32))
                wrow = ph.enter_context(nc.sbuf_tensor(f"{tag}_wrow", [128, QB], BF16))
                v = b.inc(sy.dma_start(out=gwt.rearrange("p (c e) -> p c e", c=NC),
                                       in_=gate_w.rearrange("(c p) e -> p c e", p=128)), "ld", 16)
                b.wait(te, "ld", v)
                for q in range(NQB):
                    t0 = q * QB
                    lnstats(r_res, t0, QB, f"{tag}q{q}s")
                    nc.all_engine_barrier()
                    _moe_a2(tag, q, xn, wpeT, gwt)
                    nc.all_engine_barrier()
                    if DEBUG_WPE and tag == "f1":
                        b.inc(sy.dma_start(out=dbg_t[:, t0:t0 + QB], in_=wpeT[:, :]), "st", 16)
                        b.wait(sy, "st")
                    _moe_experts(tag, q, xn, yac, wpeT, wrow)
                    nc.all_engine_barrier()
                    _moe_cadd(tag, q, yac)
                    nc.all_engine_barrier()

        def _moe_a2(tag, q, xn, wpeT, gwt):
            t0 = q * QB
            with ExitStack() as s:
                rg = s.enter_context(nc.sbuf_tensor(f"{tag}a2rg_{q}", [128, NC * G], F32))
                tmpa = s.enter_context(nc.sbuf_tensor(f"{tag}a2ta_{q}", [128, 2 * G], F32))
                tmpf = s.enter_context(nc.sbuf_tensor(f"{tag}a2tf_{q}", [128, 2 * G], F32))
                scsb = s.enter_context(nc.sbuf_tensor(f"{tag}a2sc_{q}", [8, G], F32))
                tk = s.enter_context(nc.sbuf_tensor(f"{tag}a2tk_{q}", [128, 6 * 8], F32))
                sm = s.enter_context(nc.sbuf_tensor(f"{tag}a2sm_{q}", [128, 48], F32))
                psA = s.enter_context(nc.psum_tensor(f"{tag}a2pA_{q}", [128, 512], F32))
                psB = s.enter_context(nc.psum_tensor(f"{tag}a2pB_{q}", [128, 512], F32))
                psS = s.enter_context(nc.psum_tensor(f"{tag}a2pS_{q}", [128, 512], F32))
                psT = s.enter_context(nc.psum_tensor(f"{tag}a2pT_{q}", [128, 512], F32))
                psW = s.enter_context(nc.psum_tensor(f"{tag}a2pW_{q}", [128, 512], F32))
                for g2 in range(2):
                    tg = t0 + g2 * G
                    if g2 > 0:
                        b.wait(sy, "pe")
                        b.wait(sy, "vec")
                    v = b.inc(sy.dma_start(out=rg.rearrange("p (c t) -> p c t", c=NC),
                                           in_=_rearr_cols(r_res, tg, G)), "ld", 16)
                    if g2 > 0:
                        b.wait(te, "vec")
                    bca = b.inc(te.matmul(psA[:, 0:G], t_ones_row[0:1, :], st_A[0:1, tg:tg + G],
                                          start=True, stop=True), "pe")
                    bcb = b.inc(te.matmul(psB[:, 0:G], t_ones_row[0:1, :], st_B[0:1, tg:tg + G],
                                          start=True, stop=True), "pe")
                    b.wait(ve, "ld", v)
                    b.wait(ve, "pe", bcb)
                    pv_gate = 0
                    for c in range(NC):
                        sl = c % 2
                        ta = tmpa[:, sl * G:(sl + 1) * G]
                        tf = tmpf[:, sl * G:(sl + 1) * G]
                        if c >= 2:
                            b.wait(ve, "pe")
                        b.inc(ve.tensor_mul(ta, rg[:, c * G:(c + 1) * G], psA[:, 0:G]), "vec")
                        fv = b.inc(ve.tensor_add(tf, ta, psB[:, 0:G]), "vec")
                        b.inc(ve.tensor_copy(xn[:, c * QB + g2 * G: c * QB + g2 * G + G], tf), "vec")
                        b.wait(te, "vec", fv)
                        pv_gate = b.inc(te.matmul(psS[0:8, 0:G], gwt[:, c * 8:(c + 1) * 8], tf,
                                                  start=(c == 0), stop=(c == NC - 1)), "pe")
                    b.wait(ve, "pe", pv_gate)
                    sv = b.inc(ve.tensor_copy(scsb[:, :], psS[0:8, 0:G]), "vec")
                    for tc in range(4):
                        b.wait(te, "vec")
                        ptv = b.inc(te.transpose(psT[:, 0:8], scsb[:, tc * 128:(tc + 1) * 128],
                                                 t_id_f[0:8, 0:8]), "pe")
                        b.wait(ve, "pe", ptv)
                        def nvw(instr):
                            vv = b.inc(instr, "vec")
                            b.wait(ve, "vec", vv)
                            return vv
                        toksc, esb = tk[:, 0:8], tk[:, 8:16]
                        scn, sc2 = tk[:, 16:24], tk[:, 24:32]
                        mk1, mk2 = tk[:, 32:40], tk[:, 40:48]
                        t4 = sm[:, 0:4]
                        t2 = sm[:, 4:6]
                        mx2 = sm[:, 8:10]
                        nm2 = sm[:, 12:14]
                        zr2 = sm[:, 16:18]
                        m1_2 = sm[:, 20:22]
                        m2_2 = sm[:, 24:26]
                        den2 = sm[:, 28:30]
                        wr2 = sm[:, 32:34]
                        w1_2 = sm[:, 36:38]
                        w2_2 = sm[:, 40:42]

                        def rowmax(src8, out2):
                            nvw(ve.tensor_max(t4, src8[:, 0:4], src8[:, 4:8]))
                            nvw(ve.tensor_max(t2, t4[:, 0:2], t4[:, 2:4]))
                            nvw(ve.tensor_single_scalar(out2, t2, t2[:, 1:2], ALU.max))

                        def rowsum(src8, out2):
                            nvw(ve.tensor_add(t4, src8[:, 0:4], src8[:, 4:8]))
                            nvw(ve.tensor_add(t2, t4[:, 0:2], t4[:, 2:4]))
                            nvw(ve.tensor_single_scalar(out2, t2, t2[:, 1:2], ALU.add))

                        nvw(ve.tensor_copy(toksc, psT[:, 0:8]))
                        rowmax(toksc, mx2)
                        nv = nvw(ve.tensor_single_scalar(nm2, t_neg2[:], mx2[:, 0:1], ALU.mult))
                        b.wait(se, "vec", nv)
                        av = b.inc(se.activation(esb, toksc, AF.Exp, bias=nm2[:, 0:1]), "act")
                        b.wait(ve, "act", av)
                        rowsum(esb, zr2)
                        nvw(ve.reciprocal(zr2, zr2))
                        nvw(ve.tensor_scalar_mul(scn, esb, zr2[:, 0:1]))
                        rowmax(scn, m1_2)
                        nvw(ve.tensor_single_scalar(mk1, scn, m1_2[:, 0:1], ALU.is_ge))
                        nvw(ve.tensor_scalar(sc2, mk1, -1.0, 1.0, ALU.mult, ALU.add))
                        nvw(ve.tensor_mul(sc2, scn, sc2))
                        rowmax(sc2, m2_2)
                        nvw(ve.tensor_single_scalar(mk2, sc2, m2_2[:, 0:1], ALU.is_ge))
                        nvw(ve.tensor_single_scalar(den2, t_one2[:], m1_2[:, 0:1], ALU.mult))
                        nvw(ve.tensor_single_scalar(den2, den2, m2_2[:, 0:1], ALU.add))
                        nvw(ve.tensor_scalar_add(den2, den2, 1e-20))
                        nvw(ve.reciprocal(wr2, den2))
                        nvw(ve.tensor_scalar_mul(wr2, wr2, 0.5))
                        nvw(ve.tensor_single_scalar(w1_2, wr2, m1_2[:, 0:1], ALU.mult))
                        nvw(ve.tensor_single_scalar(w2_2, wr2, m2_2[:, 0:1], ALU.mult))
                        nvw(ve.tensor_scalar_mul(toksc, mk1, w1_2[:, 0:1]))
                        wv = nvw(ve.scalar_tensor_tensor(toksc, mk2, w2_2[:, 0:1], toksc,
                                                         ALU.mult, ALU.add))
                        b.wait(te, "vec", wv)
                        pwv = b.inc(te.transpose(psW[0:8, 0:128], toksc, t_id_f[:, :]), "pe")
                        b.wait(ve, "pe", pwv)
                        b.inc(ve.tensor_copy(wpeT[:, g2 * G + tc * 128: g2 * G + (tc + 1) * 128],
                                             psW[0:8, 0:128]), "vec")
                b.wait(se, "vec")
                b.wait(te, "vec")

        def _w_col_block(base_ap, e, shared_ap, c0, cols):
            if e < NEXP:
                sl = base_ap[e, :, c0:c0 + cols]
            else:
                sl = shared_ap[:, c0:c0 + cols]
            return sl.rearrange("(c p) d -> p c d", p=128)

        def _moe_experts(tag, q, xn, yac, wpeT, wrow):
            with ExitStack() as s:
                wgb = s.enter_context(nc.sbuf_tensor(f"{tag}wg_{q}", [128, 3 * NC * 128], BF16))
                wub = s.enter_context(nc.sbuf_tensor(f"{tag}wu_{q}", [128, 3 * NC * 128], BF16))
                wdb = s.enter_context(nc.sbuf_tensor(f"{tag}wd_{q}", [128, 3 * 16 * 128], BF16))
                hid = s.enter_context(nc.sbuf_tensor(f"{tag}hid_{q}", [128, 16 * QB], BF16))
                sgb = s.enter_context(nc.sbuf_tensor(f"{tag}sg_{q}", [128, 2 * 512], F32))
                h1b = s.enter_context(nc.sbuf_tensor(f"{tag}h1_{q}", [128, 2 * 512], F32))
                psgu = [s.enter_context(nc.psum_tensor(f"{tag}pgu{i}_{q}", [128, 512], F32)) for i in range(5)]
                psd = [s.enter_context(nc.psum_tensor(f"{tag}pd{i}_{q}", [128, 512], F32)) for i in range(2)]
                pswr = s.enter_context(nc.psum_tensor(f"{tag}pwr_{q}", [128, 512], F32))
                ringGU = Ring(b, f"{tag}rgu_{q}", [p[:, 0:512] for p in psgu])
                ringD = Ring(b, f"{tag}rd_{q}", [p[:, 0:512] for p in psd])
                wrl = 0
                slot_gu = [0, 0, 0]
                slot_d = [0, 0, 0]
                for e in range(NEXP + 1):
                    if e == NEXP:
                        wrl = b.inc(ve.memset(wrow[:, :], 0.5), "vec")
                    else:
                        for g2 in range(2):
                            if e > 0 or g2 > 0:
                                b.wait(te, "vec", wrl)
                            pv = b.inc(te.matmul(pswr[:, 0:G], t_ebc_f[:, e * 128:(e + 1) * 128],
                                                 wpeT[0:8, g2 * G:(g2 + 1) * G], start=True, stop=True), "pe")
                            b.wait(ve, "pe", pv)
                            wrl = b.inc(ve.tensor_copy(wrow[:, g2 * G:(g2 + 1) * G], pswr[:, 0:G]), "vec")
                    wuse = {}
                    hid_last = 0
                    hidv = {}
                    hid_seen = [0]
                    for dc in range(16):
                        sl = dc % 3
                        wg_s = wgb[:, sl * NC * 128:(sl + 1) * NC * 128]
                        wu_s = wub[:, sl * NC * 128:(sl + 1) * NC * 128]
                        if slot_gu[sl]:
                            b.wait(ge, "pe", slot_gu[sl])
                        b.inc(ge.dma_start(out=wg_s.rearrange("p (c d) -> p c d", d=128),
                                           in_=_w_col_block(we_g, e, ws_g, dc * 128, 128)), "wld", 16)
                        lv = b.inc(ge.dma_start(out=wu_s.rearrange("p (c d) -> p c d", d=128),
                                                in_=_w_col_block(we_u, e, ws_u, dc * 128, 128)), "wld", 16)
                        for ts in range(2):
                            pg = ringGU.acquire(te)
                            pu = ringGU.acquire(te)
                            b.wait(te, "wld", lv)
                            pgv = puv = 0
                            for c in range(NC):
                                pgv = b.inc(te.matmul(pg, wg_s[:, c * 128:(c + 1) * 128],
                                                      xn[:, c * QB + ts * 512: c * QB + ts * 512 + 512],
                                                      start=(c == 0), stop=(c == NC - 1)), "pe")
                            for c in range(NC):
                                puv = b.inc(te.matmul(pu, wu_s[:, c * 128:(c + 1) * 128],
                                                      xn[:, c * QB + ts * 512: c * QB + ts * 512 + 512],
                                                      start=(c == 0), stop=(c == NC - 1)), "pe")
                            wuse[dc] = puv
                            slot_gu[sl] = puv
                            sgs = sgb[:, ts * 512:(ts + 1) * 512]
                            h1s = h1b[:, ts * 512:(ts + 1) * 512]
                            b.wait(se, "pe", pgv)
                            b.wait(se, "vec", hid_last)
                            av = b.inc(se.activation(sgs, pg, AF.Silu), "act")
                            b.wait(ve, "act", av)
                            b.wait(ve, "pe", puv)
                            h1i = ve.tensor_mul(h1s, sgs, pu)
                            ringGU.release(h1i, 2)
                            hidi = ve.tensor_mul(hid[:, dc * QB + ts * 512: dc * QB + ts * 512 + 512],
                                                 h1s, wrow[:, ts * 512:(ts + 1) * 512])
                            hv = b.inc(hidi, "vec")
                            hidv[(dc, ts)] = hv
                            hid_last = hv
                    wduse = {}
                    for o2 in range(NC):
                        sl = o2 % 3
                        wd_s = wdb[:, sl * 16 * 128:(sl + 1) * 16 * 128]
                        if slot_d[sl]:
                            b.wait(ge, "pe", slot_d[sl])
                        lv = b.inc(ge.dma_start(out=wd_s.rearrange("p (c d) -> p c d", d=128),
                                                in_=_w_col_block(we_d, e, ws_d, o2 * 128, 128)), "wld", 16)
                        for ts in range(2):
                            pd = ringD.acquire(te)
                            b.wait(te, "wld", lv)
                            pdv = 0
                            for dc in range(16):
                                hv_need = hidv.get((dc, ts), 0)
                                if hv_need and hv_need > hid_seen[0]:
                                    b.wait(te, "vec", hv_need)
                                    hid_seen[0] = hv_need
                                pdv = b.inc(te.matmul(pd, wd_s[:, dc * 128:(dc + 1) * 128],
                                                      hid[:, dc * QB + ts * 512: dc * QB + ts * 512 + 512],
                                                      start=(dc == 0), stop=(dc == 15)), "pe")
                            wduse[o2] = pdv
                            slot_d[sl] = pdv
                            b.wait(ve, "pe", pdv)
                            ysl = yac[:, o2 * QB + ts * 512: o2 * QB + ts * 512 + 512]
                            if e == 0:
                                yi = ve.tensor_copy(ysl, pd)
                            else:
                                yi = ve.tensor_add(ysl, ysl, pd)
                            ringD.release(yi)

        def _moe_cadd(tag, q, yac):
            t0 = q * QB
            with ExitStack() as s:
                rg = s.enter_context(nc.sbuf_tensor(f"{tag}crg_{q}", [128, NC * G], F32))
                for g2 in range(2):
                    tg = t0 + g2 * G
                    if g2 > 0:
                        b.wait(sy, "st")
                        b.wait(sy, "vec")
                    v = b.inc(sy.dma_start(out=rg.rearrange("p (c t) -> p c t", c=NC),
                                           in_=_rearr_cols(r_res, tg, G)), "ld", 16)
                    b.wait(ve, "ld", v)
                    vv = 0
                    for c in range(NC):
                        vv = b.inc(ve.tensor_add(rg[:, c * G:(c + 1) * G], rg[:, c * G:(c + 1) * G],
                                                 yac[:, c * QB + g2 * G: c * QB + g2 * G + G]), "vec")
                    b.wait(sy, "vec", vv)
                    b.inc(sy.dma_start(out=_rearr_cols(r_res, tg, G),
                                       in_=rg.rearrange("p (c t) -> p c t", c=NC)), "st", 16)
                b.wait(sy, "st")

        def znorm_init():
            lnstats(r_res, 0, TOK, "zi")
            nc.all_engine_barrier()
            with ExitStack() as s:
                zg = s.enter_context(nc.sbuf_tensor("zi_zg", [128, NC * G], F32))
                zo = s.enter_context(nc.sbuf_tensor("zi_zo", [128, NC * G], F32))
                psA = s.enter_context(nc.psum_tensor("zi_pA", [128, 512], F32))
                psB = s.enter_context(nc.psum_tensor("zi_pB", [128, 512], F32))
                stv = 0
                for g in range(NG):
                    tg = g * G
                    if g > 0:
                        b.wait(sy, "st", stv)
                        b.wait(sy, "vec")
                    v = b.inc(sy.dma_start(out=zg.rearrange("p (c t) -> p c t", c=NC),
                                           in_=_rearr_cols(r_res, tg, G)), "ld", 16)
                    if g > 0:
                        b.wait(te, "vec")
                    ba = b.inc(te.matmul(psA[:, 0:G], t_ones_row[0:1, :], st_A[0:1, tg:tg + G],
                                         start=True, stop=True), "pe")
                    bb2 = b.inc(te.matmul(psB[:, 0:G], t_ones_row[0:1, :], st_B[0:1, tg:tg + G],
                                          start=True, stop=True), "pe")
                    b.wait(ve, "ld", v)
                    b.wait(ve, "pe", bb2)
                    vv = 0
                    for c in range(NC):
                        b.inc(ve.tensor_mul(zo[:, c * G:(c + 1) * G], zg[:, c * G:(c + 1) * G], psA[:, 0:G]), "vec")
                        vv = b.inc(ve.tensor_add(zo[:, c * G:(c + 1) * G], zo[:, c * G:(c + 1) * G], psB[:, 0:G]), "vec")
                    b.wait(sy, "vec", vv)
                    stv = b.inc(sy.dma_start(out=_rearr_cols(z_buf, tg, G),
                                             in_=zo.rearrange("p (c t) -> p c t", c=NC)), "st", 16)
                b.wait(sy, "st")
            nc.all_engine_barrier()

        def _xn_build(zg, xn_g, psA, psB, tg, scr, scaled=True):
            b.inc(te.matmul(psA[:, 0:G], t_ones_row[0:1, :], st_A[0:1, tg:tg + G],
                            start=True, stop=True), "pe")
            bb2 = b.inc(te.matmul(psB[:, 0:G], t_ones_row[0:1, :], st_B[0:1, tg:tg + G],
                                  start=True, stop=True), "pe")
            b.wait(ve, "pe", bb2)
            vv = 0
            for c in range(NC):
                t1 = xn_g[:, c * G:(c + 1) * G]
                b.inc(ve.tensor_mul(scr[:, 0:G], zg[:, c * G:(c + 1) * G], psA[:, 0:G]), "vec")
                b.inc(ve.tensor_add(scr[:, 0:G], scr[:, 0:G], psB[:, 0:G]), "vec")
                if scaled:
                    vv = b.inc(ve.tensor_scalar(t1, scr[:, 0:G], t_lns[:, c:c + 1],
                                                t_lnb[:, c:c + 1], ALU.mult, ALU.add), "vec")
                else:
                    vv = b.inc(ve.tensor_copy(t1, scr[:, 0:G]), "vec")
            return vv

        def ret_layer(li):
            lnstats(z_buf, 0, TOK, f"r{li}s")
            nc.all_engine_barrier()
            b.inc(ve.memset(st_R[:], 0.0), "vec")
            b.inc(ve.memset(st_Rb[:], 0.0), "vec")
            gamC = [float(_gammas64()[h] ** CH) for h in range(H)]

            with ExitStack() as s:
                wq_sb = s.enter_context(nc.sbuf_tensor(f"r{li}_wq", [128, NC * D], BF16))
                wk_sb = s.enter_context(nc.sbuf_tensor(f"r{li}_wk", [128, NC * D], BF16))
                wv_sb = s.enter_context(nc.sbuf_tensor(f"r{li}_wv", [128, NC * D], BF16))
                zg2 = s.enter_context(nc.sbuf_tensor(f"r{li}_zg", [128, 2 * NC * G], F32))
                xn2 = s.enter_context(nc.sbuf_tensor(f"r{li}_xn", [128, NC * G], BF16))
                crg2 = s.enter_context(nc.sbuf_tensor(f"r{li}_cr", [128, G], F32))
                srg2 = s.enter_context(nc.sbuf_tensor(f"r{li}_sr", [128, G], F32))
                rotq = s.enter_context(nc.sbuf_tensor(f"r{li}_rq", [128, H * G], BF16))
                rotk = s.enter_context(nc.sbuf_tensor(f"r{li}_rk", [128, H * G], BF16))
                v_tok = s.enter_context(nc.sbuf_tensor(f"r{li}_vt", [128, 2 * D], BF16))
                y_grp = s.enter_context(nc.sbuf_tensor(f"r{li}_yg", [128, H * G], BF16))
                evt = s.enter_context(nc.sbuf_tensor(f"r{li}_ev", [128, 2 * G], F32))
                evc = s.enter_context(nc.sbuf_tensor(f"r{li}_ec", [128, 2 * G], F32))
                attd_all = s.enter_context(nc.sbuf_tensor(f"r{li}_ad", [128, H * CH], BF16))
                qg_all = s.enter_context(nc.sbuf_tensor(f"r{li}_qg", [128, H * G], BF16))
                kts2 = s.enter_context(nc.sbuf_tensor(f"r{li}_kt", [128, 2 * CH], BF16))
                ba = [s.enter_context(nc.psum_tensor(f"r{li}_ba{i}", [128, 512], F32)) for i in range(2)]
                by = [s.enter_context(nc.psum_tensor(f"r{li}_by{i}", [128, 512], F32)) for i in range(2)]
                bd = [s.enter_context(nc.psum_tensor(f"r{li}_bd{i}", [128, 512], F32)) for i in range(2)]
                psV = s.enter_context(nc.psum_tensor(f"r{li}_pV", [128, 512], F32))
                psKT_b = s.enter_context(nc.psum_tensor(f"r{li}_pKT", [128, 1024], BF16))
                for w_dram, wsb in [(w_q, wq_sb), (w_k, wk_sb), (w_v, wv_sb)]:
                    b.inc(ge.dma_start(out=wsb.rearrange("p (c d) -> p c d", d=D),
                                       in_=w_dram[li].rearrange("(c p) d -> p c d", p=128)), "wld", 16)
                wldv = b.val("wld")
                b.wait(te, "wld", wldv)

                ldv = [0] * NG
                gv_pe = [0] * NG
                gv_vec = [0] * NG
                rbv_prev = [0] * H

                def emit_loads(g):
                    sl = g % 2
                    if g >= 2:
                        b.wait(sy, "pe", gv_pe[g - 2])
                        b.wait(sy, "vec", gv_vec[g - 2])
                    tg = g * G
                    ldv[g] = b.inc(sy.dma_start(out=zg2[:, sl * NC * G:(sl + 1) * NC * G].rearrange("p (c t) -> p c t", c=NC),
                                                in_=_rearr_cols(z_buf, tg, G)), "ld", 16)

                def emit_cs_loads(g):
                    tg = g * G
                    if g > 0:
                        b.wait(sy, "vec")
                    b.inc(sy.dma_start(out=crg2[:, 0:G], in_=crot[:, tg:tg + G]), "ld", 16)
                    csld[0] = b.inc(sy.dma_start(out=srg2[:, 0:G], in_=srot[:, tg:tg + G]), "ld", 16)

                csld = [0]
                emit_loads(0)
                stv_y = 0
                ybank = [0, 0]
                rbp = [0] * H
                qgst = [0]
                ba_free = [(None, 0), (None, 0)]
                for g in (range(NG) if "scan" in RET_PARTS else []):
                    sl = g % 2
                    tg = g * G
                    zg = zg2[:, sl * NC * G:(sl + 1) * NC * G]
                    xn_g = xn2[:, 0:NC * G]
                    crg = crg2[:, 0:G]
                    srg = srg2[:, 0:G]
                    emit_cs_loads(g)
                    b.wait(ve, "ld", ldv[g])
                    b.wait(ve, "ld", csld[0])
                    if ybank[0]:
                        b.wait(te, "act", max(ybank))
                    _xn_build(zg, xn_g, by[0], by[1], tg, evt)
                    xnv = b.val("vec")
                    b.wait(te, "vec", xnv)
                    for (wsb, rot, sign) in [(wq_sb, rotq, 1.0), (wk_sb, rotk, -1.0)]:
                        evh = [0, 0]
                        rvh = [0, 0]
                        pqb = [0, 0]
                        swv = 0
                        for h in range(H):
                            p = h % 2
                            if evh[p]:
                                b.wait(te, "act", evh[p])
                            elif ba_free[p][0] is not None:
                                b.wait(te, ba_free[p][0], ba_free[p][1])
                            pv = 0
                            for c in range(NC):
                                pv = b.inc(te.matmul(ba[p][:, 0:G], wsb[:, c * D + h * 128: c * D + (h + 1) * 128],
                                                     xn_g[:, c * G:(c + 1) * G],
                                                     start=(c == 0), stop=(c == NC - 1)), "pe")
                            b.wait(se, "pe", pv)
                            if rvh[p]:
                                b.wait(se, "vec", rvh[p])
                                b.wait(se, "pool", pqb[p])
                            evh[p] = b.inc(se.copy(evt[:, p * G:(p + 1) * G], ba[p][:, 0:G]), "act")
                            ba_free[p] = ("act", evh[p])
                            b.wait(ge, "act", evh[p])
                            pqb[p] = b.inc(ge.tensor_copy(rot[:, h * G:(h + 1) * G], evt[:, p * G:(p + 1) * G]), "pool")
                            b.inc(ge.tensor_mul(evc[:, p * G:(p + 1) * G], evt[:, p * G:(p + 1) * G], crg), "pool")
                            pqb[p] = b.val("pool")
                            b.wait(te, "pool", pqb[p])
                            if swv:
                                b.wait(te, "vec", swv)
                            ppv = b.inc(te.matmul(psV[:, 0:G], t_pswap[:], rot[:, h * G:(h + 1) * G],
                                                  start=True, stop=True), "pe")
                            b.wait(ve, "pe", ppv)
                            b.wait(ve, "pool", pqb[p])
                            swv = b.inc(ve.tensor_mul(evt[:, p * G:(p + 1) * G], psV[:, 0:G], srg), "vec")
                            if sign > 0:
                                rvh[p] = b.inc(ve.tensor_add(rot[:, h * G:(h + 1) * G],
                                                             evc[:, p * G:(p + 1) * G],
                                                             evt[:, p * G:(p + 1) * G]), "vec")
                            else:
                                rvh[p] = b.inc(ve.tensor_sub(rot[:, h * G:(h + 1) * G],
                                                             evc[:, p * G:(p + 1) * G],
                                                             evt[:, p * G:(p + 1) * G]), "vec")
                        b.wait(te, "vec", rvh[0])
                        b.wait(te, "vec", rvh[1])
                    qgv_h = [0] * H
                    b.wait(ge, "vec")
                    b.wait(ge, "st", qgst[0])
                    for h in range(H):
                        a = rotq[:, h * G:(h + 1) * G].rearrange("p (c t) -> p c t", t=CH)
                        gqb = t_gq[:, h * CH:(h + 1) * CH].rearrange("p (c t) -> p c t", t=CH)
                        a2, b2 = bass.broadcast_tensor_aps(a, gqb)
                        qgv_h[h] = b.inc(ge.tensor_tensor(
                            qg_all[:, h * G:(h + 1) * G].rearrange("p (c t) -> p c t", t=CH),
                            a2, b2, ALU.mult), "pool")
                    b.wait(sy, "pool", qgv_h[H - 1])
                    qgst[0] = b.inc(sy.dma_start(out=_rearr_cols(qg_dram, tg, G),
                                                 in_=qg_all.rearrange("p (h t) -> p h t", h=H)), "st", 16)
                    ksv_last = 0
                    for tc in range(4):
                        tsl = tc % 2
                        vts = v_tok[:, tsl * D:(tsl + 1) * D]
                        vev = 0
                        for dv in range(2):
                            if dv == 1:
                                b.wait(te, "act", vev)
                            pvv = 0
                            for c in range(NC):
                                pvv = b.inc(te.matmul(psV[:, 0:G], xn_g[:, c * G + tc * 128: c * G + tc * 128 + 128],
                                                      wv_sb[:, c * D + dv * 512: c * D + (dv + 1) * 512],
                                                      start=(c == 0), stop=(c == NC - 1)), "pe")
                            b.wait(se, "pe", pvv)
                            vev = b.inc(se.copy(vts[:, dv * 512:(dv + 1) * 512], psV[:, 0:G]), "act")
                        b.wait(te, "act", vev)
                        adv = [0] * H
                        for h in range(H):
                            p = h % 2
                            rq_c = rotq[:, h * G + tc * 128: h * G + (tc + 1) * 128]
                            rk_c = rotk[:, h * G + tc * 128: h * G + (tc + 1) * 128]
                            if h >= 2:
                                b.wait(te, "vec", adv[h - 2])
                            elif tc == 0:
                                b.wait(te, ba_free[p][0], ba_free[p][1])
                            atv = b.inc(te.matmul(ba[p][:, 0:128], rk_c, rq_c, start=True, stop=True), "pe")
                            b.wait(ve, "pe", atv)
                            adv[h] = b.inc(ve.tensor_mul(attd_all[:, h * CH:(h + 1) * CH], ba[p][:, 0:128],
                                                         t_ddec[:, h * CH:(h + 1) * CH]), "vec")
                            ba_free[p] = ("vec", adv[h])
                        for h in range(H):
                            p = h % 2
                            vt_h = vts[:, h * 128:(h + 1) * 128]
                            if ybank[p]:
                                b.wait(te, "act", ybank[p])
                            b.wait(te, "vec", adv[h])
                            b.wait(te, "pool", max(qgv_h[h], rbp[h]))
                            b.inc(te.matmul(by[p][:, 0:128], vt_h, attd_all[:, h * CH:(h + 1) * CH],
                                            start=True, stop=False), "pe")
                            yv = b.inc(te.matmul(by[p][:, 0:128], st_Rb[:, h * 128:(h + 1) * 128],
                                                 qg_all[:, h * G + tc * 128: h * G + (tc + 1) * 128],
                                                 start=False, stop=True), "pe")
                            b.wait(se, "pe", yv)
                            ybank[p] = b.inc(se.copy(y_grp[:, h * G + tc * 128: h * G + (tc + 1) * 128],
                                                     by[p][:, 0:128]), "act")
                        ksv = [0, 0]
                        ruv = [0] * H
                        for h in range(H):
                            p = h % 2
                            rk_c = rotk[:, h * G + tc * 128: h * G + (tc + 1) * 128]
                            vt_h = vts[:, h * 128:(h + 1) * 128]
                            if ksv_last:
                                b.wait(te, "vec", ksv_last)
                            ktv = b.inc(te.transpose(psKT_b[:, 0:128], rk_c, t_id_b[:]), "pe")
                            b.wait(ve, "pe", ktv)
                            ksv[p] = b.inc(ve.tensor_scalar_mul(kts2[:, p * CH:(p + 1) * CH],
                                                                psKT_b[:, 0:128], t_kscale[:, h:h + 1]), "vec")
                            ksv_last = ksv[p]
                            if h >= 2:
                                b.wait(te, "vec", ruv[h - 2])
                            b.wait(te, "vec", ksv[p])
                            drv = b.inc(te.matmul(bd[p][:, 0:128], kts2[:, p * CH:(p + 1) * CH], vt_h,
                                                  start=True, stop=True), "pe")
                            b.wait(ve, "pe", drv)
                            b.wait(ve, "pool", rbp[h])
                            ruv[h] = b.inc(ve.scalar_tensor_tensor(st_R[:, h * 128:(h + 1) * 128],
                                                                   st_R[:, h * 128:(h + 1) * 128], gamC[h],
                                                                   bd[p][:, 0:128], ALU.mult, ALU.add), "vec")
                            b.wait(ge, "vec", ruv[h])
                            rbp[h] = b.inc(ge.tensor_copy(st_Rb[:, h * 128:(h + 1) * 128],
                                                          st_R[:, h * 128:(h + 1) * 128]), "pool")
                    gv_pe[g] = b.val("pe")
                    gv_vec[g] = b.val("vec")
                    if g + 1 < NG:
                        emit_loads(g + 1)
                    b.wait(sy, "act")
                    stv_y = b.inc(sy.dma_start(out=_rearr_cols(y_dram, tg, G),
                                               in_=y_grp.rearrange("p (h t) -> p h t", h=H)), "st", 16)
                    b.wait(se, "st", stv_y)
                b.wait(sy, "st")
                nc.all_engine_barrier()

                if "relay" in RET_PARTS:
                    b.inc(ge.dma_start(out=relay_in.ap().rearrange("(q h) v -> q h v", h=H),
                                       in_=st_R.rearrange("p (h v) -> p h v", v=128)), "wld", 16)
                    b.wait(ge, "wld")
                    ccv = b.inc(ge.collective_compute(
                        "AllGather", ALU.bypass, replica_groups=[list(range(8))],
                        ins=[relay_in.ap().opt()], outs=[relay_out.ap().opt()]), "cc", 1)
                    b.inc(ge.memset(rlb_f[:], 0.0), "pool")
                    b.wait(ge, "cc", ccv)
                    gvx = b.inc(ge.indirect_dma_start(
                        out=rlb_f[:, :], out_offset=None,
                        in_=relay_out.ap()[:, :],
                        in_offset=IndirectOffsetOnAxis(t_ridx[:, 0:1], 0),
                        bounds_check=NC * H * DH - 1, oob_is_err=False), "wld", 16)
                    b.wait(ve, "wld", gvx)
                    b.wait(ve, "pool")
                    b.inc(ve.tensor_copy(st_Rrelay[:], rlb_f[:]), "vec")
                    nc.all_engine_barrier()
                else:
                    b.inc(ve.memset(st_Rrelay[:], 0.0), "vec")

                ldv = [0] * NG
                gv_pe = [0] * NG
                gv_vec = [0] * NG
                yldv = [0] * NG
                ystv = [0] * NG

                def emit_loads_c(g):
                    sl = g % 2
                    if g >= 2:
                        b.wait(sy, "pe", gv_pe[g - 2])
                        b.wait(sy, "vec", gv_vec[g - 2])
                    tg = g * G
                    ldv[g] = b.inc(sy.dma_start(out=zg2[:, sl * NC * G:(sl + 1) * NC * G].rearrange("p (c t) -> p c t", c=NC),
                                                in_=_rearr_cols(z_buf, tg, G)), "ld", 16)

                def emit_yload_c(g):
                    yldv[g] = b.inc(sy.dma_start(out=y_grp.rearrange("p (h t) -> p h t", h=H),
                                                 in_=_rearr_cols(y_dram, g * G, G)), "ld", 16)

                emit_loads_c(0)
                emit_yload_c(0)
                for g in (range(NG) if "corrOLD" in RET_PARTS else []):
                    sl = g % 2
                    tg = g * G
                    zg = zg2[:, sl * NC * G:(sl + 1) * NC * G]
                    xn_g = xn2[:, 0:NC * G]
                    crg = crg2[:, 0:G]
                    srg = srg2[:, 0:G]
                    emit_cs_loads(g)
                    b.wait(ve, "ld", ldv[g])
                    b.wait(ve, "ld", csld[0])
                    b.wait(te, "act")
                    b.wait(te, "vec")
                    _xn_build(zg, xn_g, by[0], by[1], tg, evt)
                    xnv = b.val("vec")
                    b.wait(te, "vec", xnv)
                    evh = [0, 0]
                    rvh = [0, 0]
                    pqb = [0, 0]
                    swv = 0
                    b.wait(ve, "ld", yldv[g])
                    for h in range(H):
                        p = h % 2
                        if evh[p]:
                            b.wait(te, "act", evh[p])
                        pv = 0
                        for c in range(NC):
                            pv = b.inc(te.matmul(ba[p][:, 0:G], wq_sb[:, c * D + h * 128: c * D + (h + 1) * 128],
                                                 xn_g[:, c * G:(c + 1) * G],
                                                 start=(c == 0), stop=(c == NC - 1)), "pe")
                        b.wait(se, "pe", pv)
                        if rvh[p]:
                            b.wait(se, "vec", rvh[p])
                            b.wait(se, "pool", pqb[p])
                        evh[p] = b.inc(se.copy(evt[:, p * G:(p + 1) * G], ba[p][:, 0:G]), "act")
                        b.wait(ge, "act", evh[p])
                        pqb[p] = b.inc(ge.tensor_copy(rotq[:, h * G:(h + 1) * G], evt[:, p * G:(p + 1) * G]), "pool")
                        b.inc(ge.tensor_mul(evc[:, p * G:(p + 1) * G], evt[:, p * G:(p + 1) * G], crg), "pool")
                        pqb[p] = b.val("pool")
                        b.wait(te, "pool", pqb[p])
                        if swv:
                            b.wait(te, "vec", swv)
                        ppv = b.inc(te.matmul(psV[:, 0:G], t_pswap[:], rotq[:, h * G:(h + 1) * G],
                                              start=True, stop=True), "pe")
                        b.wait(ve, "pe", ppv)
                        b.wait(ve, "pool", pqb[p])
                        swv = b.inc(ve.tensor_mul(evt[:, p * G:(p + 1) * G], psV[:, 0:G], srg), "vec")
                        rvh[p] = b.inc(ve.tensor_add(rotq[:, h * G:(h + 1) * G],
                                                     evc[:, p * G:(p + 1) * G],
                                                     evt[:, p * G:(p + 1) * G]), "vec")
                        b.wait(ge, "vec", rvh[p])
                        a = rotq[:, h * G:(h + 1) * G].rearrange("p (c t) -> p c t", t=CH)
                        gqb = t_gq[:, h * CH:(h + 1) * CH].rearrange("p (c t) -> p c t", t=CH)
                        a2, b2 = bass.broadcast_tensor_aps(a, gqb)
                        qgv = b.inc(ge.tensor_tensor(
                            qg_all[:, h * G:(h + 1) * G].rearrange("p (c t) -> p c t", t=CH),
                            a2, b2, ALU.mult), "pool")
                        b.wait(te, "pool", qgv)
                        sttl = [0, 0]
                        for tc in range(4):
                            pq = tc % 2
                            if sttl[pq]:
                                b.wait(te, "vec", sttl[pq])
                            cpv = b.inc(te.matmul(bd[pq][:, 0:128], st_Rrelay[:, h * 128:(h + 1) * 128],
                                                  qg_all[:, h * G + tc * 128: h * G + (tc + 1) * 128],
                                                  start=True, stop=True), "pe")
                            b.wait(ve, "pe", cpv)
                            gscale = float(_gammas64()[h] ** (128 * (g * 4 + tc)))
                            ysl = y_grp[:, h * G + tc * 128: h * G + (tc + 1) * 128]
                            sttl[pq] = b.inc(ve.scalar_tensor_tensor(ysl, bd[pq][:, 0:128], gscale, ysl,
                                                                     ALU.mult, ALU.add), "vec")
                    gv_pe[g] = b.val("pe")
                    gv_vec[g] = b.val("vec")
                    if g + 1 < NG:
                        emit_loads_c(g + 1)
                    b.wait(sy, "vec")
                    ystv[g] = b.inc(sy.dma_start(out=_rearr_cols(y_dram, tg, G),
                                                 in_=y_grp.rearrange("p (h t) -> p h t", h=H)), "st", 16)
                    if g + 1 < NG:
                        emit_yload_c(g + 1)
                b.wait(sy, "st")
            nc.all_engine_barrier()
            if "post" in RET_PARTS:
                _ret_post(li)
                nc.all_engine_barrier()
            if "ffn" in RET_PARTS:
                _ret_ffn(li)
                nc.all_engine_barrier()

        def _ret_post(li):
            outer = ExitStack()
            st_A8 = outer.enter_context(nc.sbuf_tensor(f"po{li}_A8", [H, TOK], F32))
            st_B8 = outer.enter_context(nc.sbuf_tensor(f"po{li}_B8", [H, TOK], F32))
            with ExitStack() as s:
                y_g = s.enter_context(nc.sbuf_tensor(f"po{li}_yg", [128, H * G], BF16))
                sqb = s.enter_context(nc.sbuf_tensor(f"po{li}_sq", [128, 2 * G], BF16))
                mu8 = s.enter_context(nc.sbuf_tensor(f"po{li}_mu", [8, TOK], F32))
                var8 = s.enter_context(nc.sbuf_tensor(f"po{li}_va", [8, TOK], F32))
                t8 = s.enter_context(nc.sbuf_tensor(f"po{li}_t8", [8, G], F32))
                qg_g = s.enter_context(nc.sbuf_tensor(f"po{li}_qgg", [128, H * G], BF16))
                psS = s.enter_context(nc.psum_tensor(f"po{li}_pS", [128, 512], F32))
                psQ = s.enter_context(nc.psum_tensor(f"po{li}_pQ", [128, 512], F32))
                psX = [s.enter_context(nc.psum_tensor(f"po{li}_pX{i}", [128, 512], F32)) for i in range(2)]
                gam64 = _gammas64()
                for g in range(NG):
                    tg = g * G
                    if g > 0:
                        b.wait(sy, "pe")
                    v = b.inc(sy.dma_start(out=y_g.rearrange("p (h t) -> p h t", h=H),
                                           in_=_rearr_cols(y_dram, tg, G)), "ld", 16)
                    qv = 0
                    if "corr" in RET_PARTS:
                        qv = b.inc(sy.dma_start(out=qg_g.rearrange("p (h t) -> p h t", h=H),
                                                in_=_rearr_cols(qg_dram, tg, G)), "ld", 16)
                    b.wait(ve, "ld", v)
                    b.wait(te, "ld", v)
                    if g > 0:
                        b.wait(te, "vec")
                    if "corr" in RET_PARTS:
                        b.wait(te, "ld", qv)
                        sttl = [0, 0]
                        for h in range(H):
                            for tc in range(4):
                                gscale = float(gam64[h] ** (128 * (g * 4 + tc)))
                                if gscale == 0.0:
                                    continue
                                pq = tc % 2
                                if sttl[pq]:
                                    b.wait(te, "vec", sttl[pq])
                                cpv = b.inc(te.matmul(psX[pq][:, 0:128], st_Rrelay[:, h * 128:(h + 1) * 128],
                                                      qg_g[:, h * G + tc * 128: h * G + (tc + 1) * 128],
                                                      start=True, stop=True), "pe")
                                b.wait(ve, "pe", cpv)
                                ysl = y_g[:, h * G + tc * 128: h * G + (tc + 1) * 128]
                                sttl[pq] = b.inc(ve.scalar_tensor_tensor(ysl, psX[pq][:, 0:128], gscale, ysl,
                                                                         ALU.mult, ALU.add), "vec")
                    pv = 0
                    for h in range(H):
                        ysl = y_g[:, h * G:(h + 1) * G]
                        sqs = sqb[:, (h % 2) * G:(h % 2) * G + G]
                        if h >= 2 or g > 0:
                            b.wait(ve, "pe")
                        sv = b.inc(ve.tensor_mul(sqs, ysl, ysl), "vec")
                        b.inc(te.matmul(psS[0:8, 0:G], t_ehs_b[:, h * H:(h + 1) * H], ysl,
                                        start=(h == 0), stop=(h == H - 1)), "pe")
                        b.wait(te, "vec", sv)
                        pv = b.inc(te.matmul(psQ[0:8, 0:G], t_ehs_b[:, h * H:(h + 1) * H], sqs,
                                             start=(h == 0), stop=(h == H - 1)), "pe")
                    if "corr" in RET_PARTS:
                        b.wait(sy, "vec")
                        b.inc(sy.dma_start(out=_rearr_cols(y_dram, tg, G),
                                           in_=y_g.rearrange("p (h t) -> p h t", h=H)), "st", 16)
                    b.wait(ve, "pe", pv)
                    b.inc(ve.tensor_scalar_mul(mu8[0:8, tg:tg + G], psS[0:8, 0:G], 1.0 / DH), "vec")
                    b.inc(ve.tensor_mul(t8[0:8, :], mu8[0:8, tg:tg + G], mu8[0:8, tg:tg + G]), "vec")
                    b.inc(ve.tensor_scalar_mul(var8[0:8, tg:tg + G], psQ[0:8, 0:G], 1.0 / DH), "vec")
                    b.inc(ve.tensor_sub(var8[0:8, tg:tg + G], var8[0:8, tg:tg + G], t8[0:8, :]), "vec")
                b.inc(ve.tensor_scalar_add(var8[0:8, :], var8[0:8, :], EPS), "vec")
                rv = b.inc(ve.reciprocal(var8[0:8, :], var8[0:8, :]), "vec")
                b.wait(se, "vec", rv)
                av = b.inc(se.activation(st_A8[0:8, :], var8[0:8, :], AF.Sqrt), "act")
                b.wait(ve, "act", av)
                bv = b.inc(ve.scalar_tensor_tensor(st_B8[0:8, :], mu8[0:8, :], -1.0,
                                                   st_A8[0:8, :], ALU.mult, ALU.mult), "vec")
                b.wait(se, "vec", bv)
                b.wait(te, "vec", bv)
            nc.all_engine_barrier()
            with ExitStack() as s:
                wg_sb = s.enter_context(nc.sbuf_tensor(f"po{li}_wg", [128, NC * D], BF16))
                wo_sb = s.enter_context(nc.sbuf_tensor(f"po{li}_wo", [128, NC * D], BF16))
                zg = s.enter_context(nc.sbuf_tensor(f"po{li}_zg", [128, NC * G], F32))
                xn_g = s.enter_context(nc.sbuf_tensor(f"po{li}_xn", [128, NC * G], BF16))
                y_g = s.enter_context(nc.sbuf_tensor(f"po{li}_yg2", [128, H * G], BF16))
                g_sb = s.enter_context(nc.sbuf_tensor(f"po{li}_gs", [128, NC * G], BF16))
                m_sb = s.enter_context(nc.sbuf_tensor(f"po{li}_ms", [128, NC * G], BF16))
                scr = s.enter_context(nc.sbuf_tensor(f"po{li}_scr", [128, 4 * G], F32))
                psA = s.enter_context(nc.psum_tensor(f"po{li}_pA", [128, 512], F32))
                psB = s.enter_context(nc.psum_tensor(f"po{li}_pB", [128, 512], F32))
                psG = s.enter_context(nc.psum_tensor(f"po{li}_pG", [128, 512], F32))
                psO = s.enter_context(nc.psum_tensor(f"po{li}_pO", [128, 512], F32))
                psbc = [s.enter_context(nc.psum_tensor(f"po{li}_pbc{i}", [128, 512], F32)) for i in range(2)]
                psbd = [s.enter_context(nc.psum_tensor(f"po{li}_pbd{i}", [128, 512], F32)) for i in range(2)]
                b.inc(ge.dma_start(out=wg_sb.rearrange("p (c d) -> p c d", d=D),
                                   in_=w_g[li].rearrange("(c p) d -> p c d", p=128)), "wld", 16)
                wldv = b.inc(ge.dma_start(out=wo_sb.rearrange("p (c d) -> p c d", d=D),
                                          in_=w_o[li].rearrange("(c p) d -> p c d", p=128)), "wld", 16)
                b.wait(te, "wld", wldv)
                bcfree = [0, 0]
                for g in range(NG):
                    tg = g * G
                    if g > 0:
                        b.wait(sy, "pe")
                        b.wait(sy, "vec")
                        b.wait(sy, "st")
                    v = b.inc(sy.dma_start(out=zg.rearrange("p (c t) -> p c t", c=NC),
                                           in_=_rearr_cols(z_buf, tg, G)), "ld", 16)
                    yv = b.inc(sy.dma_start(out=y_g.rearrange("p (h t) -> p h t", h=H),
                                            in_=_rearr_cols(y_dram, tg, G)), "ld", 16)
                    b.wait(ve, "ld", v)
                    _xn_build(zg, xn_g, psA, psB, tg, scr)
                    xnv = b.val("vec")
                    b.wait(te, "vec", xnv)
                    gv = 0
                    for co in range(NC):
                        if gv > 0:
                            b.wait(te, "act", gv)
                        pv = 0
                        for c in range(NC):
                            pv = b.inc(te.matmul(psG[:, 0:G], wg_sb[:, c * D + co * 128: c * D + (co + 1) * 128],
                                                 xn_g[:, c * G:(c + 1) * G],
                                                 start=(c == 0), stop=(c == NC - 1)), "pe")
                        b.wait(se, "pe", pv)
                        b.wait(se, "vec") if co == 0 and g > 0 else None
                        gv = b.inc(se.activation(g_sb[:, co * G:(co + 1) * G], psG[:, 0:G], AF.Silu), "act")
                    b.wait(ve, "ld", yv)
                    for h in range(H):
                        p2 = h % 2
                        if bcfree[p2]:
                            b.wait(te, "vec", bcfree[p2])
                        bca = b.inc(te.matmul(psbc[p2][:, 0:G], t_ebc_f[:, h * 128:(h + 1) * 128],
                                              st_A8[0:8, tg:tg + G], start=True, stop=True), "pe")
                        bcb = b.inc(te.matmul(psbd[p2][:, 0:G], t_ebc_f[:, h * 128:(h + 1) * 128],
                                              st_B8[0:8, tg:tg + G], start=True, stop=True), "pe")
                        b.wait(ve, "pe", bcb)
                        b.wait(ve, "act", gv)
                        ysl = y_g[:, h * G:(h + 1) * G]
                        b.inc(ve.tensor_mul(scr[:, p2 * G:p2 * G + G], ysl, psbc[p2][:, 0:G]), "vec")
                        bcfree[p2] = b.inc(ve.tensor_add(scr[:, p2 * G:p2 * G + G],
                                                         scr[:, p2 * G:p2 * G + G], psbd[p2][:, 0:G]), "vec")
                        b.inc(ve.tensor_scalar(scr[:, (2 + p2) * G:(3 + p2) * G], scr[:, p2 * G:p2 * G + G],
                                               t_gns[:, li * NC + h: li * NC + h + 1],
                                               t_gnb[:, li * NC + h: li * NC + h + 1],
                                               ALU.mult, ALU.add), "vec")
                        mv = b.inc(ve.tensor_mul(m_sb[:, h * G:(h + 1) * G], scr[:, (2 + p2) * G:(3 + p2) * G],
                                                 g_sb[:, h * G:(h + 1) * G]), "vec")
                    b.wait(te, "vec", mv)
                    zv = 0
                    for co in range(NC):
                        if zv > 0:
                            b.wait(te, "vec", zv)
                        pv = 0
                        for c in range(NC):
                            pv = b.inc(te.matmul(psO[:, 0:G], wo_sb[:, c * D + co * 128: c * D + (co + 1) * 128],
                                                 m_sb[:, c * G:(c + 1) * G],
                                                 start=(c == 0), stop=(c == NC - 1)), "pe")
                        b.wait(ve, "pe", pv)
                        zv = b.inc(ve.tensor_add(zg[:, co * G:(co + 1) * G], zg[:, co * G:(co + 1) * G],
                                                 psO[:, 0:G]), "vec")
                    b.wait(sy, "vec", zv)
                    b.inc(sy.dma_start(out=_rearr_cols(z_buf, tg, G),
                                       in_=zg.rearrange("p (c t) -> p c t", c=NC)), "st", 16)
                b.wait(sy, "st")
            outer.close()

        def _ret_ffn(li):
            lnstats(z_buf, 0, TOK, f"rf{li}s")
            nc.all_engine_barrier()
            with ExitStack() as s:
                w1_sb = s.enter_context(nc.sbuf_tensor(f"rf{li}_w1", [128, NC * FF], BF16))
                w2_sb = s.enter_context(nc.sbuf_tensor(f"rf{li}_w2", [128, 16 * D], BF16))
                zg = s.enter_context(nc.sbuf_tensor(f"rf{li}_zg", [128, NC * G], F32))
                xn_g = s.enter_context(nc.sbuf_tensor(f"rf{li}_xn", [128, NC * G], BF16))
                hid = s.enter_context(nc.sbuf_tensor(f"rf{li}_hid", [128, 16 * G], BF16))
                scr = s.enter_context(nc.sbuf_tensor(f"rf{li}_scr", [128, 4 * G], F32))
                psA = s.enter_context(nc.psum_tensor(f"rf{li}_pA", [128, 512], F32))
                psB = s.enter_context(nc.psum_tensor(f"rf{li}_pB", [128, 512], F32))
                psF = [s.enter_context(nc.psum_tensor(f"rf{li}_pF{i}", [128, 512], F32)) for i in range(3)]
                psW = [s.enter_context(nc.psum_tensor(f"rf{li}_pW{i}", [128, 512], F32)) for i in range(2)]
                ringF = Ring(b, f"rf{li}_rF", [p[:, 0:512] for p in psF])
                ringW = Ring(b, f"rf{li}_rW", [p[:, 0:512] for p in psW])
                b.inc(ge.dma_start(out=w1_sb.rearrange("p (c d) -> p c d", d=FF),
                                   in_=w_1[li].rearrange("(c p) d -> p c d", p=128)), "wld", 16)
                wldv = b.inc(ge.dma_start(out=w2_sb.rearrange("p (c d) -> p c d", d=D),
                                          in_=w_2[li].rearrange("(c p) d -> p c d", p=128)), "wld", 16)
                b.wait(te, "wld", wldv)
                c1 = GELU_C
                c1a = GELU_C * GELU_A
                for g in range(NG):
                    tg = g * G
                    if g > 0:
                        b.wait(sy, "pe")
                        b.wait(sy, "vec")
                        b.wait(sy, "st")
                    v = b.inc(sy.dma_start(out=zg.rearrange("p (c t) -> p c t", c=NC),
                                           in_=_rearr_cols(z_buf, tg, G)), "ld", 16)
                    b.wait(ve, "ld", v)
                    _xn_build(zg, xn_g, psA, psB, tg, scr)
                    xnv = b.val("vec")
                    b.wait(te, "vec", xnv)
                    hv = 0
                    for fo in range(16):
                        pf = ringF.acquire(te)
                        pv = 0
                        for c in range(NC):
                            pv = b.inc(te.matmul(pf, w1_sb[:, c * FF + fo * 128: c * FF + (fo + 1) * 128],
                                                 xn_g[:, c * G:(c + 1) * G],
                                                 start=(c == 0), stop=(c == NC - 1)), "pe")
                        b.wait(se, "pe", pv)
                        if hv > 0:
                            b.wait(se, ringF.name, hv)
                        sqv = b.inc(se.activation(scr[:, 0:G], pf, AF.Square), "act")
                        b.wait(ve, "act", sqv)
                        b.inc(ve.tensor_scalar(scr[:, G:2 * G], scr[:, 0:G], c1a, c1, ALU.mult, ALU.add), "vec")
                        uv = b.inc(ve.tensor_mul(scr[:, 2 * G:3 * G], pf, scr[:, G:2 * G]), "vec")
                        b.wait(se, "vec", uv)
                        tv = b.inc(se.activation(scr[:, 3 * G:4 * G], scr[:, 2 * G:3 * G], AF.Tanh), "act")
                        b.wait(ve, "act", tv)
                        b.inc(ve.tensor_scalar(scr[:, G:2 * G], scr[:, 3 * G:4 * G], 0.5, 0.5,
                                               ALU.mult, ALU.add), "vec")
                        hvi = ve.tensor_mul(hid[:, fo * G:(fo + 1) * G], pf, scr[:, G:2 * G])
                        ringF.release(hvi)
                        hv = b.cnt[ringF.name]
                    b.wait(te, ringF.name, hv)
                    for co in range(NC):
                        pw = ringW.acquire(te)
                        pv = 0
                        for fc in range(16):
                            pv = b.inc(te.matmul(pw, w2_sb[:, fc * D + co * 128: fc * D + (co + 1) * 128],
                                                 hid[:, fc * G:(fc + 1) * G],
                                                 start=(fc == 0), stop=(fc == 15)), "pe")
                        b.wait(ve, "pe", pv)
                        zvi = ve.tensor_add(zg[:, co * G:(co + 1) * G], zg[:, co * G:(co + 1) * G], pw)
                        ringW.release(zvi)
                    b.wait(sy, ringW.name)
                    b.inc(sy.dma_start(out=_rearr_cols(z_buf, tg, G),
                                       in_=zg.rearrange("p (c t) -> p c t", c=NC)), "st", 16)
                b.wait(sy, "st")

        def merge_rz():
            with ExitStack() as s:
                rg = s.enter_context(nc.sbuf_tensor("mg_rg", [128, NC * G], F32))
                zg2 = s.enter_context(nc.sbuf_tensor("mg_zg", [128, NC * G], F32))
                for g in range(NG):
                    tg = g * G
                    if g > 0:
                        b.wait(sy, "st")
                        b.wait(sy, "vec")
                    b.inc(sy.dma_start(out=rg.rearrange("p (c t) -> p c t", c=NC),
                                       in_=_rearr_cols(r_res, tg, G)), "ld", 16)
                    v = b.inc(sy.dma_start(out=zg2.rearrange("p (c t) -> p c t", c=NC),
                                           in_=_rearr_cols(z_buf, tg, G)), "ld", 16)
                    b.wait(ve, "ld", v)
                    vv = 0
                    for c in range(NC):
                        vv = b.inc(ve.tensor_add(rg[:, c * G:(c + 1) * G], rg[:, c * G:(c + 1) * G],
                                                 zg2[:, c * G:(c + 1) * G]), "vec")
                    b.wait(sy, "vec", vv)
                    b.inc(sy.dma_start(out=_rearr_cols(r_res, tg, G),
                                       in_=rg.rearrange("p (c t) -> p c t", c=NC)), "st", 16)
                b.wait(sy, "st")
            nc.all_engine_barrier()

        def p4_output():
            with ExitStack() as ph:
                def src(tc, buf):
                    return b.inc(sy.dma_start(
                        out=buf.rearrange("p (c t) -> p c t", c=NC),
                        in_=_rearr_cols(r_res, tc * 128, 128)), "ld", 16)
                def dst(tc, buf):
                    b.inc(sy.dma_start(out=out_t[tc * 128:(tc + 1) * 128, :], in_=buf[:, :]), "st", 16)
                transpose_stream(src, dst, ph, "p4")

        load_tables()
        if "P0" in phases:
            p0_ingest()
        if "FFN1" in phases:
            moe_phase("f1")
        if "RET" in phases:
            znorm_init()
            for li in range(L):
                ret_layer(li)
        if "MERGE" in phases:
            merge_rz()
        if "FFN2" in phases:
            moe_phase("f2")
        if "P4" in phases:
            p4_output()

    return nc, b


def _shard_in_maps(inputs):
    X = np.ascontiguousarray(inputs["L0_attn_X"])
    maps = []
    for core in range(8):
        bat, half = core >> 1, core & 1
        t = host_tables(core)
        m = {
            "x": np.ascontiguousarray(X[bat, half * TOK:(half + 1) * TOK, :]),
            "wq": inputs["L0_attn_Wq"], "wk": inputs["L0_attn_Wk"],
            "wv": inputs["L0_attn_Wv"], "wg": inputs["L0_attn_Wg"],
            "wo": inputs["L0_attn_Wo"], "w1": inputs["L0_attn_W1"],
            "w2": inputs["L0_attn_W2"],
            "ln_s": _feat_tile(inputs["L0_attn_ln_scale"]),
            "ln_b": _feat_tile(inputs["L0_attn_ln_bias"]),
            "gn_s": np.concatenate([_feat_tile(inputs["L0_attn_gn_scale"][i]) for i in range(L)], axis=1),
            "gn_b": np.concatenate([_feat_tile(inputs["L0_attn_gn_bias"][i]) for i in range(L)], axis=1),
            "gate_w": inputs["L0_ffn_gate_w"],
            "we_g": inputs["L0_ffn_w_gate"], "we_u": inputs["L0_ffn_w_up"],
            "we_d": inputs["L0_ffn_w_down"],
            "ws_g": inputs["L0_ffn_ws_gate"], "ws_u": inputs["L0_ffn_ws_up"],
            "ws_d": inputs["L0_ffn_ws_down"],
            "crot": t["crot"], "srot": t["srot"], "ddec": t["ddec"], "gq": t["gq"],
            "kscale": t["kscale"], "id_f": t["id_f"],
            "ones_row_f": t["ones_row_f"], "ones_col_f": t["ones_col_f"],
            "ehs_f": t["ehs_f"], "ebc_f": t["ebc_f"],
            "id_b": t["id_b"], "pswap_b": t["pswap_b"],
            "ones_col_b": t["ones_col_b"], "ehs_b": t["ehs_b"],
            "relay_idx": t["relay_idx"],
        }
        m = {k: (np.ascontiguousarray(v) if isinstance(v, np.ndarray) else np.ascontiguousarray(np.asarray(v))) for k, v in m.items()}
        maps.append(m)
    return maps


def kernel(**inputs):
    inputs = {k: np.asarray(v) for k, v in inputs.items()}
    maps = _shard_in_maps(inputs)
    nc, _ = build_program(ALL_PHASES)
    res = run_bass_kernel_spmd(nc, maps, core_ids=list(range(8)))
    out = np.empty((BAT, S, D), np.float32)
    for core in range(8):
        bat, half = core >> 1, core & 1
        out[bat, half * TOK:(half + 1) * TOK, :] = res.results[core]["out"]
    return out
